# version 11
# speedup vs baseline: 1.6717x; 1.0775x over previous
import numpy as np
import concourse.bacc as bacc
import concourse.bass as bass
import concourse.mybir as mybir
from concourse.bass_utils import run_bass_kernel_spmd

F32 = mybir.dt.float32
BF16 = mybir.dt.bfloat16
U32 = mybir.dt.uint32
U16 = mybir.dt.uint16
I32 = mybir.dt.int32
AF = mybir.ActivationFunctionType
ALU = mybir.AluOpType

T = 2048
D = 1024
NC8 = 8
QB = 512
NQB = T // QB
NKC = T // 128
NSEQ = 2
NHB = 24
PKM_H = 8
PKM_S = 512
EPS = 1e-5
NTB = T // 128


class Sem:
    def __init__(self, name):
        self.name = name
        self.val = 0
        self.handle = None

    def inc(self, n=1):
        self.val += n
        return self.val


class Plan:
    AUTO_SELF = ("ve", "se", "gp")

    def __init__(self):
        self.ops = []
        self.sems = {}
        self.auto = {e: self.sem(e) for e in self.AUTO_SELF}

    def sem(self, name):
        s = self.sems.get(name)
        if s is None:
            s = Sem(name)
            self.sems[name] = s
        return s

    def emit(self, engine, fn):
        self.ops.append((engine, fn))

    def op(self, engine, build, sem=None, n=1):
        auto = self.auto.get(engine)
        prev = auto.val if auto is not None else 0
        if auto is not None:
            auto.inc(1)
            if sem is auto:
                sem = None
        if sem is not None:
            sem.inc(n)

        def fn(eng, build=build, sem=sem, n=n, auto=auto, prev=prev):
            if auto is not None and prev > 0:
                eng.wait_ge(auto.handle, prev)
            ins = build(eng)
            assert ins is not None, "builder must return its instruction"
            if auto is not None:
                ins.then_inc(auto.handle, 1)
            if sem is not None:
                ins.then_inc(sem.handle, n)
        self.emit(engine, fn)
        return sem.val if sem is not None else (auto.val if auto else None)

    def wait(self, engine, sem, val=None):
        if val is None:
            val = sem.val
        def fn(eng, sem=sem, val=val):
            eng.wait_ge(sem.handle, val)
        self.emit(engine, fn)

    def dma(self, engine, out, in_, sem, slow_ok=False):
        tgt = sem.inc(16)
        def fn(eng, out=out, in_=in_, sem=sem, slow_ok=slow_ok):
            eng.dma_start(out=out, in_=in_,
                          allow_slow_non_contiguous=slow_ok).then_inc(sem.handle, 16)
        self.emit(engine, fn)
        return tgt

    def indirect(self, out, in_, idx_ap, sem):
        tgt = sem.inc(16)
        def fn(eng, out=out, in_=in_, idx_ap=idx_ap, sem=sem):
            eng.indirect_dma_start(
                out=out, out_offset=None, in_=in_,
                in_offset=bass.IndirectOffsetOnAxis(ap=idx_ap, axis=0),
            ).then_inc(sem.handle, 16)
        self.emit("gp", fn)
        return tgt

    def run(self, nc):
        with nc.Block() as block:
            def make(engine_attr, key):
                def body(eng):
                    for e, fn in self.ops:
                        if e == key:
                            fn(eng)
                engine_attr(body)
            make(block.tensor, "te")
            make(block.vector, "ve")
            make(block.scalar, "se")
            make(block.gpsimd, "gp")
            make(block.sync, "sy")


def build_program(debug_names=()):
    nc = bacc.Bacc("TRN2", target_bir_lowering=False)
    p = Plan()
    from contextlib import ExitStack
    ctx = ExitStack()

    dram = lambda name, shape, dt=F32, kind="ExternalInput": nc.dram_tensor(
        name, shape, dt, kind=kind).ap()

    x_in = dram("x_in", [NSEQ, NC8, 128, T])
    out_hbm = dram("out", [NSEQ, NC8, 128, T], kind="ExternalOutput")
    x1scr = dram("x1scr", [NSEQ, NC8, 128, T], kind="Internal")
    yscr = dram("yscr", [16, 64, T], BF16, kind="Internal")

    wq0 = dram("wq0", [8, 128, NC8 * 128])
    wk0 = dram("wk0", [8, 128, NC8 * 128])
    wv0 = dram("wv0", [8, 128, NC8 * 128])
    wo0 = dram("wo0", [NC8, 128, NC8 * 128])
    bq0 = dram("bq0", [128, 8])
    bk0 = dram("bk0", [128, 8])
    bv0 = dram("bv0", [128, NC8 * 128])
    bo0 = dram("bo0", [128, NC8])
    cos0 = dram("cos0", [128, T], BF16)
    sin0 = dram("sin0", [128, T], BF16)
    wg0 = dram("wg0", [NHB, 128, NC8 * 128], I32)
    wu0 = dram("wu0", [NHB, 128, NC8 * 128], I32)
    wd0 = dram("wd0", [NC8, 128, NHB * 128], I32)
    wgs = dram("wgs", [NHB, 128, NC8 * 128], BF16, kind="Internal")
    wus = dram("wus", [NHB, 128, NC8 * 128], BF16, kind="Internal")
    wds = dram("wds", [NC8, 128, NHB * 128], BF16, kind="Internal")
    wq1 = dram("wq1", [8, 128, NC8 * 128])
    wk1 = dram("wk1", [8, 128, NC8 * 128])
    wv1 = dram("wv1", [8, 128, NC8 * 128])
    wo1 = dram("wo1", [NC8, 128, NC8 * 128])
    cos1 = dram("cos1", [128, T], BF16)
    sin1 = dram("sin1", [128, T], BF16)
    wi1 = dram("wi1", [8, 128, NC8 * 128])
    keysa = dram("keysa", [PKM_H, 128, PKM_S])
    keysb = dram("keysb", [PKM_H, 128, PKM_S])
    ush = dram("ush", [128, 1])
    expv = dram("expv", [PKM_S * PKM_S, 128])
    wout1 = dram("wout1", [NC8, 128, NC8 * 128])
    vecs_in = dram("vecs_in", [128, 64])
    ones_c = dram("ones_c", [128, 32])
    iota16_in = dram("iota16", [128, 16])

    dbg_specs = {
        "xn": [128, NC8 * T], "g": [128, NC8 * T], "y": [128, NC8 * T],
        "x1": [NC8, 128, T], "s1": [128, NC8 * T],
        "eids": [128, 8 * 16], "w16": [128, 8 * 16], "mgt": [128, 8 * 128],
        "hpkm": [128, NC8 * T],
    }
    dbg = {k: dram("dbg_" + k, dbg_specs[k], kind="ExternalOutput")
           for k in debug_names}

    sLG = p.sem("ldg")
    sLS = p.sem("lds")
    sST1 = p.sem("st1")
    sST2 = p.sem("st2")
    sWB0 = p.sem("wb0")
    sWB1 = p.sem("wb1")
    sWBD = p.sem("wbd")
    sWBD2 = p.sem("wbd2")
    sKA = p.sem("ka")
    sKB = p.sem("kb")
    sKA2 = p.sem("ka2")
    sKB2 = p.sem("kb2")
    sXS = p.sem("xs")
    sDBG = p.sem("dbg")
    sTE = p.sem("te_")
    sVE = p.auto["ve"]
    sSE = p.auto["se"]
    sGP = p.auto["gp"]
    sSTS = p.sem("sts")
    sIG = p.sem("ig")
    sIG2 = p.sem("ig2")

    sb = lambda name, shape, dt=BF16: ctx.enter_context(nc.sbuf_tensor(name, shape, dt))
    psum = lambda name, shape: ctx.enter_context(nc.psum_tensor(name, shape, F32))

    XN = sb("XN", [128, NC8 * T])
    GB = sb("GB", [128, NC8 * T])
    BIG = sb("BIG", [128, NC8 * T])
    F32A = sb("F32A", [128, T], F32)
    F32B = sb("F32B", [128, T], F32)
    F32C = sb("F32C", [128, T], F32)
    STG = sb("STG", [128, T], F32)
    STG2 = sb("STG2", [128, T], F32)
    QR = sb("QR", [128, T])
    KR = sb("KR", [128, T])
    QP = sb("QP", [128, T])
    KP = sb("KP", [128, T])
    VT = sb("VT", [128, 2 * NKC * 65])
    ATT = sb("ATT", [128, 2 * QB])
    ATT2 = sb("ATT2", [128, 2 * QB])
    SG = sb("SG", [128, QB])
    WB = sb("WB", [128, 4 * NC8 * 128])
    WBD = sb("WBD", [128, 2 * NHB * 128])
    COS = sb("COS", [128, T])
    SIN = sb("SIN", [128, T])
    SMALL = sb("SMALL", [128, 64], F32)
    BVB = sb("BVB", [128, NC8 * 128], F32)
    ONES1 = sb("ONES1", [128, 1], F32)
    ONESB = sb("ONESB", [128, 1])
    IOTA16 = sb("IOTA16", [128, 16], F32)
    KA = sb("KA", [128, 2 * PKM_S])
    KB_ = sb("KB_", [128, 2 * PKM_S])
    USH = sb("USH", [128, 1])
    ACTT = sb("ACTT", [128, NTB * 8], F32)
    SCF = sb("SCF", [128, 2 * PKM_S], F32)
    TAS = sb("TAS", [128, 8 * 16], F32)
    TBS = sb("TBS", [128, 8 * 16], F32)
    TAI = sb("TAI", [128, 8 * 16], U16)
    TBI = sb("TBI", [128, 8 * 16], U16)
    POS = sb("POS", [128, 8 * 16], U16)
    IJ2 = sb("IJ2", [128, 8 * 16], U16)
    EIDS = sb("EIDS", [128, 8 * 16], U32)
    W16 = sb("W16", [128, 8 * 16], F32)
    WSUM = sb("WSUM", [128, 8], F32)
    ident_in = dram("ident_in", [128, 128])

    PBIG = psum("PBIG", [128, 2048])
    PSC = psum("PSC", [128, 1024])
    PV = psum("PV", [128, 512])
    PST = psum("PST", [128, 512])

    CAND = STG[:, :]
    SCZ = GB[:, 0:1024].bitcast(F32)
    OH = GB[:, 0:4096].bitcast(F32)
    GT = GB[:, 4096:6144]
    GT2 = GB[:, 8192:10240]
    MGT = GB[:, 6144:8192].bitcast(F32)
    MGT2 = GB[:, 10240:12288].bitcast(F32)
    F32V = GB[:, 12288:16384].bitcast(F32)
    TAIF = F32V[:, 0:128]
    TBIF = F32V[:, 128:256]
    TS16 = F32V[:, 256:384]
    IJF = F32V[:, 384:512]
    SLOTA = F32V[:, 512:640]
    EIDF = F32V[:, 640:768]
    CZ = F32V[:, 768:1024]
    EIDSb = GB[:, 14336:14592].bitcast(U32)
    W16b = F32V[:, 1152:1280]

    swapmask = []
    for i in range(0, 32, 2):
        swapmask += [i + 1, i]

    def mm(out_ap, lhsT_ap, rhs_ap, start, stop, sem=None):
        def b(eng, o=out_ap, l=lhsT_ap, r=rhs_ap, st=start, sp=stop):
            return eng.matmul(o, l, r, start=st, stop=sp)
        return p.op("te", b, sem)

    def cp(engine, out_ap, in_ap):
        def b(eng, o=out_ap, i=in_ap):
            return eng.tensor_copy(o, i)
        return p.op(engine, b)

    def act(out_ap, in_ap, func, bias=0.0, scale=1.0):
        def b(eng, o=out_ap, i=in_ap, f=func, bi=bias, sc=scale):
            return eng.activation(out=o, in_=i, func=f, bias=bi, scale=sc)
        return p.op("se", b)

    def tt(out_ap, a_ap, b_ap, alu, engine="ve"):
        def b(eng, o=out_ap, a=a_ap, bb=b_ap, alu=alu):
            return eng.tensor_tensor(o, a, bb, alu)
        return p.op(engine, b)

    def ts(out_ap, in_ap, s1, s2=None, op0=ALU.mult, op1=ALU.bypass, engine="ve"):
        def b(eng, o=out_ap, i=in_ap, s1=s1, s2=s2, op0=op0, op1=op1):
            return eng.tensor_scalar(out=o, in0=i, scalar1=s1, scalar2=s2,
                                     op0=op0, op1=op1)
        return p.op(engine, b)

    def stt(out_ap, in0_ap, scalar, in1_ap, op0, op1, engine="ve"):
        def b(eng, o=out_ap, i0=in0_ap, s=scalar, i1=in1_ap, op0=op0, op1=op1):
            return eng.scalar_tensor_tensor(out=o, in0=i0, scalar=s, in1=i1,
                                            op0=op0, op1=op1)
        return p.op(engine, b)

    def shuffle(out_ap, in_ap):
        def b(eng, o=out_ap, i=in_ap):
            return eng.stream_shuffle(out=o, in_=i, mask=swapmask)
        return p.op("ve", b)

    def vmax8(out_ap, in_ap):
        def b(eng, o=out_ap, i=in_ap):
            return eng.max(out=o, in_=i)
        return p.op("ve", b)

    def vmaxidx(out_ap, max_ap, in_ap):
        def b(eng, o=out_ap, m=max_ap, i=in_ap):
            return eng.max_index(out=o, in_max=m, in_values=i)
        return p.op("ve", b)

    def vmatchrep(out_ap, rep_ap, in_ap, immv):
        def b(eng, o=out_ap, r=rep_ap, i=in_ap, v=immv):
            return eng.match_replace(out=o, in_to_replace=r, in_values=i, imm_value=v)
        return p.op("ve", b)

    def reduce_sum_x(out_ap, in_ap):
        def b(eng, o=out_ap, i=in_ap):
            return eng.reduce_sum(o, i, mybir.AxisListType.X)
        return p.op("ve", b)

    def recip(out_ap, in_ap):
        def b(eng, o=out_ap, i=in_ap):
            return eng.reciprocal_approx_fast(o, i)
        return p.op("ve", b)

    def pbcast(out_ap, row_ap):
        def b(eng, o=out_ap, r=row_ap):
            return eng.partition_broadcast(o, r)
        return p.op("gp", b)

    def transpose_te(out_psum, in_ap, sem=None):
        def b(eng, o=out_psum, i=in_ap):
            return eng.transpose(o, i, BVB[:, 0:128])
        return p.op("te", b, sem)

    def dbg_dump(name, src_ap, f32stage=None):
        if name not in dbg:
            return
        p.wait("gp", sVE, sVE.val)
        p.wait("gp", sSE, sSE.val)
        p.wait("gp", sDBG, sDBG.val)
        p.dma("gp", dbg[name], src_ap, sDBG)
        p.wait("ve", sDBG, sDBG.val)

    p.dma("sy", SMALL[:], vecs_in, sLS)
    p.dma("sy", BVB[:], bv0, sLS)
    p.dma("sy", ONES1[:], ones_c[:, 0:1], sLS, slow_ok=True)
    p.dma("sy", IOTA16[:], iota16_in, sLS)
    p.dma("gp", VT[:].rearrange("p (a c) -> p a c", c=65)[:, :, 64:65],
          ones_c[:, 0:32].unsqueeze(2), sLG, slow_ok=True)
    p.dma("gp", USH[:], ush, sLG, slow_ok=True)
    p.dma("gp", ONESB[:], ones_c[:, 0:1], sLG, slow_ok=True)

    BQ0 = SMALL[:, 0:8]
    BK0 = SMALL[:, 8:16]
    BO0 = SMALL[:, 16:24]
    G1V = SMALL[:, 24:32]
    B1V = SMALL[:, 32:40]
    LNW1 = SMALL[:, 40:48]
    LNB1 = SMALL[:, 48:56]

    p.dma("gp", wgs, wg0, sLG)
    p.dma("gp", wus, wu0, sLG)
    p.dma("gp", wds, wd0, sLG)
    keysa_bf = dram("keysa_bf", [PKM_H, 128, PKM_S], BF16, kind="Internal")
    keysb_bf = dram("keysb_bf", [PKM_H, 128, PKM_S], BF16, kind="Internal")
    p.dma("gp", keysa_bf, keysa, sLG)
    keys_cast_mark = p.dma("gp", keysb_bf, keysb, sLG)


    def emit_ln_stats_finish():
        MS = STG2[0:1, :]
        p.wait("se", sTE, sTE.val)
        act(MS, PBIG[0:1, :], AF.Identity, scale=1.0 / D)
        act(F32C[0:1, :], PBIG[32:33, :], AF.Identity, scale=1.0 / D)
        p.wait("ve", sSE, sSE.val)
        tt(F32B[0:1, :], MS, MS, ALU.mult)
        tt(F32B[0:1, :], F32C[0:1, :], F32B[0:1, :], ALU.subtract)
        ts(F32B[0:1, :], F32B[0:1, :], float(EPS), None, ALU.add)
        recip(F32C[0:1, :], F32B[0:1, :])
        p.wait("se", sVE, sVE.val)
        act(F32C[0:1, :], F32C[0:1, :], AF.Sqrt)
        p.wait("ve", sSE, sSE.val)
        stt(F32B[0:1, :], MS, -1.0, F32C[0:1, :], ALU.mult, ALU.mult)
        p.wait("gp", sVE, sVE.val)
        p.wait("gp", sSE, sSE.val)
        p.wait("gp", sXS, sXS.val)
        pbcast(F32A[:], F32C[0:1, :])
        pbcast(F32B[:], F32B[0:1, :])

    def emit_stats_mms(src_f32_ap, c, qb=None, mark=False, ones=None):
        ones = ONES1 if ones is None else ones
        r = None
        if qb is None:
            for q in range(NQB):
                r = mm(PBIG[0:1, q * QB:(q + 1) * QB], ones[:],
                       src_f32_ap[:, q * QB:(q + 1) * QB],
                       start=(c == 0), stop=(c == NC8 - 1),
                       sem=sTE if (mark and q == NQB - 1) else None)
        else:
            r = mm(PBIG[0:1, qb * QB:(qb + 1) * QB], ones[:], src_f32_ap,
                   start=(c == 0), stop=(c == NC8 - 1),
                   sem=sTE if mark else None)
        return r

    def emit_stats_sq_mms(sq_f32_ap, c, qb=None, mark=False, ones=None):
        ones = ONES1 if ones is None else ones
        r = None
        if qb is None:
            for q in range(NQB):
                r = mm(PBIG[32:33, q * QB:(q + 1) * QB], ones[:],
                       sq_f32_ap[:, q * QB:(q + 1) * QB],
                       start=(c == 0), stop=(c == NC8 - 1),
                       sem=sTE if (mark and q == NQB - 1) else None)
        else:
            r = mm(PBIG[32:33, qb * QB:(qb + 1) * QB], ones[:], sq_f32_ap,
                   start=(c == 0), stop=(c == NC8 - 1),
                   sem=sTE if mark else None)
        return r

    def emit_x_stats(seq, src_dram):
        marks = {}
        p.wait("sy", sSE, sSE.val)
        p.wait("sy", sVE, sVE.val)
        p.wait("sy", sTE, sTE.val)
        p.wait("se", sSTS, sSTS.val)
        p.wait("ve", sSTS, sSTS.val)
        for c in range(NC8):
            S = STG if c % 2 == 0 else STG2
            sS = sST1 if c % 2 == 0 else sST2
            if c >= 2:
                p.wait("sy", sTE, marks[c - 2])
            t = p.dma("sy", S[:], src_dram[seq, c], sS)
            p.wait("se", sS, t)
            if c >= 1:
                p.wait("se", sTE, marks[c - 1])
            act(F32C[:], S[:], AF.Square)
            p.wait("te", sSE, sSE.val)
            p.wait("te", sS, t)
            emit_stats_mms(S, c, mark=False)
            marks[c] = emit_stats_sq_mms(F32C, c, mark=True)

    def emit_xn_apply(seq, src_dram):
        p.wait("ve", sGP, sGP.val)
        p.wait("sy", sGP, sGP.val)
        p.wait("sy", sSE, sSE.val)
        p.wait("sy", sVE, sVE.val)
        p.wait("sy", sTE, sTE.val)
        for c in range(NC8):
            S = STG if c % 2 == 0 else STG2
            sS = sST1 if c % 2 == 0 else sST2
            if c >= 2:
                p.wait("sy", sVE, sVE.val)
            t = p.dma("sy", S[:], src_dram[seq, c], sS)
            p.wait("ve", sS, t)
            tt(F32C[:], S[:], F32A[:], ALU.mult)
            tt(XN[:, c * T:(c + 1) * T], F32C[:], F32B[:], ALU.add)

    def emit_ffn0(seq):
        SCALE = 1.0 / 127.0
        p.wait("gp", sLG, sLG.val)
        wsem = {0: sWB0, 1: sWB1}
        for qb in range(NQB):
            xn_sl = lambda c: XN[:, c * T + qb * QB: c * T + qb * QB + QB]
            hb_te_done = {}
            load_marks = {}

            def issue_pair(hb):
                pair = hb % 2
                if hb >= 2:
                    p.wait("gp", sTE, hb_te_done[hb - 2])
                else:
                    p.wait("gp", sTE, sTE.val)
                t1 = p.dma("gp", WB[:, pair * 2048:pair * 2048 + 1024], wgs[hb],
                           wsem[pair])
                t2 = p.dma("gp", WB[:, pair * 2048 + 1024:pair * 2048 + 2048],
                           wus[hb], wsem[pair])
                load_marks[hb] = t2

            issue_pair(0)
            if NHB > 1:
                issue_pair(1)
            for hb in range(NHB):
                pair = hb % 2
                wg_sl = WB[:, pair * 2048:pair * 2048 + 1024]
                wu_sl = WB[:, pair * 2048 + 1024:pair * 2048 + 2048]
                psl = PSC[:, (hb % 2) * QB:(hb % 2) * QB + QB]
                p.wait("te", wsem[pair], load_marks[hb])
                p.wait("te", sSE, sSE.val)
                p.wait("te", sVE, sVE.val)
                for c in range(NC8):
                    mm(psl, wg_sl[:, c * 128:(c + 1) * 128], xn_sl(c),
                       start=(c == 0), stop=(c == NC8 - 1),
                       sem=sTE if c == NC8 - 1 else None)
                p.wait("se", sTE, sTE.val)
                act(SG[:], psl, AF.Silu, scale=SCALE)
                p.wait("te", sSE, sSE.val)
                for c in range(NC8):
                    mm(psl, wu_sl[:, c * 128:(c + 1) * 128], xn_sl(c),
                       start=(c == 0), stop=(c == NC8 - 1),
                       sem=sTE if c == NC8 - 1 else None)
                hb_te_done[hb] = sTE.val
                if hb + 2 < NHB:
                    issue_pair(hb + 2)
                p.wait("ve", sTE, sTE.val)
                p.wait("ve", sSE, sSE.val)
                def hb_mult(eng, hb=hb, psl=psl):
                    return eng.scalar_tensor_tensor(
                        out=BIG[:, hb * QB:(hb + 1) * QB], in0=psl,
                        scalar=SCALE, in1=SG[:], op0=ALU.mult, op1=ALU.mult)
                p.op("ve", hb_mult)
            dn_te_done = {}
            dn_marks = {}
            def issue_down(ob):
                if ob >= 2:
                    p.wait("gp", sTE, dn_te_done[ob - 2])
                else:
                    p.wait("gp", sTE, hb_te_done[NHB - 1] if ob == 0 else dn_marks.get("first_gate", sTE.val))
                t = p.dma("gp", WBD[:, (ob % 2) * NHB * 128:(ob % 2 + 1) * NHB * 128],
                          wds[ob], sWBD if ob % 2 == 0 else sWBD2)
                dn_marks[ob] = t
            issue_down(0)
            issue_down(1)
            for ob in range(NC8):
                wd_sl = WBD[:, (ob % 2) * NHB * 128:(ob % 2 + 1) * NHB * 128]
                p.wait("te", sWBD if ob % 2 == 0 else sWBD2, dn_marks[ob])
                p.wait("te", sVE, sVE.val)
                p.wait("te", sSE, sSE.val)
                for hb in range(NHB):
                    mm(PV[:], wd_sl[:, hb * 128:(hb + 1) * 128],
                       BIG[:, hb * QB:(hb + 1) * QB],
                       start=(hb == 0), stop=(hb == NHB - 1),
                       sem=sTE if hb == NHB - 1 else None)
                dn_te_done[ob] = sTE.val
                if ob + 2 < NC8:
                    issue_down(ob + 2)
                p.wait("se", sTE, sTE.val)
                act(GB[:, ob * T + qb * QB: ob * T + qb * QB + QB],
                    PV[:], AF.Silu, scale=SCALE)
        attn_state["wb_free"][0] = attn_state["wb_free"][1] = sTE.val

    attn_state = {"te_after_vproj": 0, "wb_free": {0: 0, 1: 0}, "marks": {},
                  "bank_free": {0: 0, 1: 0}, "proj_evac": 0, "rope_done": 0,
                  "evv": 0, "den_free": 0, "norm_done": 0}

    def attn_issue_loads(layer, hp):
        if layer == 0:
            wq, wk, wv = wq0, wk0, wv0
        else:
            wq, wk, wv = wq1, wk1, wv1
        par = hp % 2
        p.wait("gp", sTE, attn_state["wb_free"][par])
        base = par * 2048
        tq = p.dma("gp", WB[:, base:base + 1024], wq[hp], sWB0 if par == 0 else sWB1)
        tk = p.dma("gp", WB[:, base + 1024:base + 2048], wk[hp],
                   sWB0 if par == 0 else sWB1)
        tv = p.dma("gp", WBD[:, par * NHB * 128:par * NHB * 128 + 1024], wv[hp],
                   sWBD if par == 0 else sWBD2)
        attn_state["marks"][hp] = (tk, tv)

    def emit_attn_hp(seq, layer, hp):
        if layer == 0:
            wq, wk, wv = wq0, wk0, wv0
            plain = False
        else:
            wq, wk, wv = wq1, wk1, wv1
            plain = True
        scale = 0.125

        if layer == 0 and hp == 0:
            p.wait("gp", sVE, sVE.val)
            p.wait("gp", sTE, sTE.val)
            tbv = p.dma("gp", BVB[:], bv0, sLG)
            p.wait("ve", sLG, tbv)
        if hp == 0:
            attn_issue_loads(layer, 0)
        par = hp % 2
        tk, tv = attn_state["marks"][hp]
        p.wait("te", sWB0 if par == 0 else sWB1, tk)
        p.wait("te", sWBD if par == 0 else sWBD2, tv)
        base = par * 2048

        for which in ("q", "k"):
            W = WB[:, base:base + 1024] if which == "q" else WB[:, base + 1024:base + 2048]
            OUTR = QR if which == "q" else KR
            OUTP = QP if which == "q" else KP
            p.wait("te", sVE, attn_state["rope_done"])
            for bk in (0, 1):
                if attn_state["bank_free"][bk] > 0:
                    p.wait("te", sVE, attn_state["bank_free"][bk])
            for half in range(2):
                p.wait("te", sSE, attn_state["proj_evac"])
                if attn_state["norm_done"] > 0:
                    p.wait("se", sVE, attn_state["norm_done"])
                for qs_ in range(2):
                    col = half * 1024 + qs_ * QB
                    for c in range(NC8):
                        mm(PBIG[:, qs_ * QB:(qs_ + 1) * QB],
                           W[:, c * 128:(c + 1) * 128],
                           XN[:, c * T + col: c * T + col + QB],
                           start=(c == 0), stop=(c == NC8 - 1),
                           sem=sTE if c == NC8 - 1 else None)
                p.wait("se", sTE, sTE.val)
                if layer == 0:
                    B = BQ0[:, hp:hp + 1] if which == "q" else BK0[:, hp:hp + 1]
                    m = act(F32C[:, half * 1024:(half + 1) * 1024], PBIG[:, 0:1024],
                            AF.Identity, bias=B)
                else:
                    m = act(F32C[:, half * 1024:(half + 1) * 1024], PBIG[:, 0:1024],
                            AF.Identity)
                attn_state["proj_evac"] = m
            p.wait("ve", sSE, attn_state["proj_evac"])
            if plain:
                cp("ve", OUTP[:], F32C[:])
            shuffle(F32B[:], F32C[:])
            tt(F32C[:], F32C[:], COS[:], ALU.mult)
            tt(F32B[:], F32B[:], SIN[:], ALU.mult)
            attn_state["rope_done"] = tt(OUTR[:], F32C[:], F32B[:], ALU.add)

        vbase = par * NHB * 128
        for kc in range(NKC):
            if attn_state["evv"] > 0:
                p.wait("te", sVE, attn_state["evv"])
            for c in range(NC8):
                mm(PV[:, 0:128],
                   XN[:, c * T + kc * 128: c * T + kc * 128 + 128],
                   WBD[:, vbase + c * 128:vbase + (c + 1) * 128],
                   start=(c == 0), stop=(c == NC8 - 1),
                   sem=sTE if c == NC8 - 1 else None)
            p.wait("ve", sTE, sTE.val)
            dst = VT[:].rearrange("p (h k c) -> p h k c", h=2, k=NKC)[:, :, kc, 0:64]
            if layer == 0:
                attn_state["evv"] = tt(
                    dst, PV[:, 0:128].rearrange("p (h c) -> p h c", h=2),
                    BVB[:, hp * 128:(hp + 1) * 128].rearrange("p (h c) -> p h c", h=2),
                    ALU.add)
            else:
                attn_state["evv"] = cp(
                    "ve", dst, PV[:, 0:128].rearrange("p (h c) -> p h c", h=2))

        p.wait("te", sVE, attn_state["evv"])
        attn_state["te_after_vproj"] = sTE.val
        attn_state["wb_free"][par] = sTE.val
        if hp + 1 < 8:
            attn_issue_loads(layer, hp + 1)

        for h01 in range(2):
            h = hp * 2 + h01
            rows = slice(h01 * 64, h01 * 64 + 64)
            for qb in range(NQB):
                bank = qb % 2
                avsl = PBIG[:, bank * QB:(bank + 1) * QB]
                expR = {}
                expP = {}
                NPAIR = NKC // 2
                if attn_state["bank_free"][bank] > 0:
                    p.wait("te", sVE, attn_state["bank_free"][bank])
                if plain and attn_state["den_free"] > 0:
                    p.wait("te", sVE, attn_state["den_free"])
                def av_pair(pr, qb=qb, h01=h01, avsl=avsl, expR=expR, expP=expP):
                    p.wait("te", sSE, (expP if plain else expR)[pr])
                    att_sl = ATT if (plain or pr % 2 == 0) else ATT2
                    for j in range(2):
                        kc = 2 * pr + j
                        vt_base = (h01 * NKC + kc) * 65
                        if not plain:
                            mm(avsl[0:65, :],
                               VT[:, vt_base:vt_base + 65],
                               att_sl[:, j * QB:(j + 1) * QB],
                               start=(kc == 0), stop=(kc == NKC - 1),
                               sem=sTE if kc == NKC - 1 else None)
                        else:
                            mm(avsl[0:64, :],
                               VT[:, vt_base:vt_base + 64],
                               ATT[:, j * QB:(j + 1) * QB],
                               start=(kc == 0), stop=(kc == NKC - 1))
                            mm(PBIG[0:1, 1024:1024 + QB],
                               VT[:, vt_base + 64:vt_base + 65],
                               ATT2[:, j * QB:(j + 1) * QB],
                               start=(kc == 0), stop=(kc == NKC - 1),
                               sem=sTE if kc == NKC - 1 else None)
                if not plain:
                    PB23 = PBIG[:, 1024:2048]
                    for pr in range(NPAIR):
                        k0 = 2 * pr
                        psl = PSC[:, :] if pr % 2 == 0 else PB23
                        if pr >= 2:
                            p.wait("te", sSE, expR[pr - 2])
                        mm(psl[:, 0:QB], KR[rows, k0 * 128:k0 * 128 + 128],
                           QR[rows, qb * QB:qb * QB + QB], True, True)
                        mm(psl[:, QB:2 * QB],
                           KR[rows, k0 * 128 + 128:k0 * 128 + 256],
                           QR[rows, qb * QB:qb * QB + QB], True, True, sem=sTE)
                        p.wait("se", sTE, sTE.val)
                        def expf(eng, psl=psl, dst=(ATT if pr % 2 == 0 else ATT2)):
                            return eng.activation(out=dst[:, :], in_=psl[:, :],
                                                  func=AF.Exp, scale=scale)
                        expR[pr] = p.op("se", expf)
                        if pr >= 1:
                            av_pair(pr - 1)
                    av_pair(NPAIR - 1)
                    p.wait("ve", sTE, sTE.val)
                    evm = cp("ve", F32B[0:64, qb * QB:(qb + 1) * QB], avsl[0:64, :])
                    evm = cp("ve", STG2[0:1, qb * QB:(qb + 1) * QB], avsl[64:65, :])
                    attn_state["bank_free"][bank] = evm
                    pkm_feed(4)
                    continue
                for pr in range(NPAIR):
                    if pr >= 1:
                        av_pair(pr - 1)
                    k0 = 2 * pr
                    mm(PSC[:, 0:QB], KR[rows, k0 * 128:k0 * 128 + 128],
                       QR[rows, qb * QB:qb * QB + QB], True, True)
                    mm(PSC[:, QB:2 * QB],
                       KR[rows, k0 * 128 + 128:k0 * 128 + 256],
                       QR[rows, qb * QB:qb * QB + QB], True, True, sem=sTE)
                    p.wait("se", sTE, sTE.val)
                    def expf(eng):
                        return eng.activation(out=ATT[:], in_=PSC[:],
                                              func=AF.Exp, scale=scale)
                    expR[pr] = p.op("se", expf)
                    if plain:
                        p.wait("te", sSE, expR[pr])
                        mm(PSC[:, 0:QB], KP[rows, k0 * 128:k0 * 128 + 128],
                           QP[rows, qb * QB:qb * QB + QB], True, True)
                        mm(PSC[:, QB:2 * QB],
                           KP[rows, k0 * 128 + 128:k0 * 128 + 256],
                           QP[rows, qb * QB:qb * QB + QB], True, True, sem=sTE)
                        p.wait("se", sTE, sTE.val)
                        def expf2(eng):
                            return eng.activation(out=ATT2[:], in_=PSC[:],
                                                  func=AF.Exp, scale=scale)
                        expP[pr] = p.op("se", expf2)
                av_pair(NPAIR - 1)
                p.wait("ve", sTE, sTE.val)
                evm = cp("ve", F32B[0:64, qb * QB:(qb + 1) * QB], avsl[0:64, :])
                if plain:
                    attn_state["den_free"] = cp(
                        "ve", STG2[0:1, qb * QB:(qb + 1) * QB], PBIG[0:1, 1024:1024 + QB])
                else:
                    evm = cp("ve", STG2[0:1, qb * QB:(qb + 1) * QB], avsl[64:65, :])
                attn_state["bank_free"][bank] = evm if not plain else max(
                    evm, attn_state["den_free"]) if False else evm
                pkm_feed(4)
            rm = recip(STG2[0:1, :], STG2[0:1, :])
            p.wait("gp", sVE, rm)
            pb = pbcast(F32C[:], STG2[0:1, :])
            p.wait("ve", sGP, pb)
            nm = tt(F32B[0:64, :], F32B[0:64, :], F32C[0:64, :], ALU.mult)
            attn_state["norm_done"] = nm
            if layer == 0:
                ychunk = h // 2
                yrows = slice((h % 2) * 64, (h % 2) * 64 + 64)
                cp("ve", BIG[yrows, ychunk * T:(ychunk + 1) * T], F32B[0:64, :])
            else:
                p.wait("gp", sVE, nm)
                p.dma("gp", yscr[h], F32B[0:64, :], sXS)
                p.wait("ve", sXS, sXS.val)

    def emit_wo_update0(seq):
        for ob in range(NC8):
            p.wait("gp", sTE, sTE.val)
            tw = p.dma("gp", WB[:, 0:1024], wo0[ob], sWB0)
            p.wait("te", sWB0, tw)
            for qb in range(NQB):
                if not (ob == 0 and qb == 0):
                    p.wait("sy", sVE, sVE.val)
                else:
                    p.wait("sy", sVE, sVE.val)
                    p.wait("sy", sSE, sSE.val)
                    p.wait("sy", sTE, sTE.val)
                t0 = p.dma("sy", STG[:, 0:QB],
                           x_in[seq, ob][:, qb * QB:(qb + 1) * QB], sST1)
                p.wait("te", sSE, sSE.val)
                p.wait("te", sVE, sVE.val)
                for c in range(NC8):
                    mm(PV[:], WB[:, c * 128:(c + 1) * 128],
                       BIG[:, c * T + qb * QB: c * T + qb * QB + QB],
                       start=(c == 0), stop=(c == NC8 - 1),
                       sem=sTE if c == NC8 - 1 else None)
                p.wait("se", sTE, sTE.val)
                act(F32C[:, 0:QB], PV[:], AF.Identity, bias=BO0[:, ob:ob + 1])
                p.wait("ve", sSE, sSE.val)
                p.wait("ve", sST1, t0)
                p.wait("ve", sTE, sTE.val)
                p.wait("ve", sXS, sXS.val)
                tt(F32A[:, 0:QB], F32C[:, 0:QB],
                   GB[:, ob * T + qb * QB: ob * T + qb * QB + QB], ALU.mult)
                tt(F32A[:, 0:QB], F32A[:, 0:QB], STG[:, 0:QB], ALU.add)
                p.wait("gp", sVE, sVE.val)
                p.dma("gp", x1scr[seq, ob][:, qb * QB:(qb + 1) * QB],
                      F32A[:, 0:QB], sXS)
                p.wait("se", sVE, sVE.val)
                act(F32B[:, 0:QB], F32A[:, 0:QB], AF.Square)
                p.wait("te", sSE, sSE.val)
                emit_stats_mms(F32A[:, 0:QB], ob, qb)
                emit_stats_sq_mms(F32B[:, 0:QB], ob, qb, mark=True)
        attn_state["wb_free"][0] = attn_state["wb_free"][1] = sTE.val

    def emit_pkm_hproj(seq, dbg_on=False):
        p.wait("gp", sVE, sVE.val)
        p.wait("gp", sTE, sTE.val)
        tid = p.dma("gp", BVB[:, 0:128], ident_in, sLG)
        p.wait("te", sLG, tid)
        for hd in range(PKM_H):
            par = hd % 2
            p.wait("gp", sTE, sTE.val)
            tw = p.dma("gp", WB[:, par * 1024:(par + 1) * 1024], wi1[hd],
                       sWB0 if par == 0 else sWB1)
            p.wait("te", sWB0 if par == 0 else sWB1, tw)
            for qb in range(NQB):
                p.wait("te", sSE, sSE.val)
                for c in range(NC8):
                    mm(PV[:], WB[:, par * 1024 + c * 128:par * 1024 + (c + 1) * 128],
                       XN[:, c * T + qb * QB: c * T + qb * QB + QB],
                       start=(c == 0), stop=(c == NC8 - 1),
                       sem=sTE if c == NC8 - 1 else None)
                p.wait("se", sTE, sTE.val)
                act(BIG[:, hd * T + qb * QB: hd * T + qb * QB + QB], PV[:],
                    AF.Identity)
        attn_state["wb_free"][0] = attn_state["wb_free"][1] = sTE.val
        if dbg_on:
            dbg_dump("hpkm", BIG[:])

    def emit_pkm_ugate(seq):
        p.wait("te", sSE, sSE.val)
        for tb in range(NTB):
            for hd in range(PKM_H):
                mm(PST[:, hd:hd + 1],
                   BIG[:, hd * T + tb * 128: hd * T + tb * 128 + 128],
                   USH[:], start=True, stop=True,
                   sem=sTE if hd == PKM_H - 1 else None)
            p.wait("se", sTE, sTE.val)
            act(ACTT[:, tb * 8:(tb + 1) * 8], PST[:, 0:8], AF.Sigmoid)
            p.wait("te", sSE, sSE.val)

    pkm_state = {"keys_free": {0: 0, 1: 0}, "slot": 0, "evac": {0: 0, 1: 0},
                 "topk": {0: 0, 1: 0}, "gt_comb": {0: 0, 1: 0}, "comb": {},
                 "ig_mark": {}, "ig_done_tb": {0: None, 1: None},
                 "tp_mark": {0: 0, 1: 0}, "pending": [], "combq": []}

    def emit_pkm_tb(seq, tb, dbg_on=False):
        for hd in range(PKM_H):
            ksl = hd % 2
            if pkm_state["keys_free"][ksl] > 0:
                p.wait("sy", sTE, pkm_state["keys_free"][ksl])
            ka = KA[:, ksl * PKM_S:(ksl + 1) * PKM_S]
            kb = KB_[:, ksl * PKM_S:(ksl + 1) * PKM_S]
            semA = sKA if ksl == 0 else sKA2
            semB = sKB if ksl == 0 else sKB2
            tk1 = p.dma("sy", ka, keysa_bf[hd], semA)
            tk2 = p.dma("sy", kb, keysb_bf[hd], semB)
            p.wait("te", semA, tk1)
            p.wait("te", semB, tk2)
            hsl = BIG[:, hd * T + tb * 128: hd * T + tb * 128 + 128]
            for ab in range(2):
                K = ka if ab == 0 else kb
                sl = pkm_state["slot"]
                pkm_state["slot"] ^= 1
                scf_sl = SCF[:, sl * PKM_S:(sl + 1) * PKM_S]
                if pkm_state["evac"][sl ^ 1] > 0:
                    p.wait("te", sSE, pkm_state["evac"][sl ^ 1])
                if pkm_state["topk"][sl] > 0:
                    p.wait("se", sVE, pkm_state["topk"][sl])
                sc_mark = mm(PBIG[:, 1536:2048], hsl, K, start=True, stop=True,
                             sem=sTE)
                if ab == 1:
                    pkm_state["keys_free"][ksl] = sc_mark
                p.wait("se", sTE, sc_mark)
                pkm_state["evac"][sl] = act(scf_sl, PBIG[:, 1536:2048], AF.Identity)
                p.wait("ve", sSE, pkm_state["evac"][sl])
                TS_ = TAS if ab == 0 else TBS
                TI_ = TAI if ab == 0 else TBI
                vmax8(TS_[:, hd * 16:hd * 16 + 8], scf_sl)
                vmaxidx(TI_[:, hd * 16:hd * 16 + 8],
                        TS_[:, hd * 16:hd * 16 + 8], scf_sl)
                vmatchrep(SCZ[:], TS_[:, hd * 16:hd * 16 + 8], scf_sl, -3e38)
                vmax8(TS_[:, hd * 16 + 8:hd * 16 + 16], SCZ[:])
                pkm_state["topk"][sl] = vmaxidx(
                    TI_[:, hd * 16 + 8:hd * 16 + 16],
                    TS_[:, hd * 16 + 8:hd * 16 + 16], SCZ[:])
        emit_pkm_stage_c(seq, tb, dbg_on)

    def emit_pkm_wout(seq):
        p.wait("te", sVE, sVE.val)
        wout_marks = {}
        wout_done = {}
        def issue_wout(ob):
            if ob >= 2:
                p.wait("gp", sTE, wout_done[ob - 2])
            else:
                p.wait("gp", sTE, sTE.val)
            t = p.dma("gp", WB[:, (ob % 2) * 1024:(ob % 2) * 1024 + 1024],
                      wout1[ob], sWB0 if ob % 2 == 0 else sWB1)
            wout_marks[ob] = t
        issue_wout(0)
        issue_wout(1)
        for ob in range(NC8):
            p.wait("te", sWB0 if ob % 2 == 0 else sWB1, wout_marks[ob])
            for qb in range(NQB):
                p.wait("te", sSE, sSE.val)
                for c in range(NC8):
                    mm(PV[:], WB[:, (ob % 2) * 1024 + c * 128:(ob % 2) * 1024 + (c + 1) * 128],
                       BIG[:, c * T + qb * QB: c * T + qb * QB + QB],
                       start=(c == 0), stop=(c == NC8 - 1),
                       sem=sTE if c == NC8 - 1 else None)
                p.wait("se", sTE, sTE.val)
                act(GB[:, ob * T + qb * QB: ob * T + qb * QB + QB], PV[:],
                    AF.Identity)
            wout_done[ob] = sTE.val
            if ob + 2 < NC8:
                issue_wout(ob + 2)
        attn_state["wb_free"][0] = attn_state["wb_free"][1] = sTE.val

    def emit_pkm_stage_c(seq, tb, dbg_on=False):
        eids_t = EIDS if tb % 2 == 0 else EIDSb
        w16_t = W16 if tb % 2 == 0 else W16b
        prev_ig = pkm_state["ig_done_tb"][tb % 2]
        if prev_ig is not None:
            p.wait("ve", sIG, prev_ig[0])
            p.wait("ve", sIG2, prev_ig[1])
        cp("ve", TAIF[:], TAI[:])
        cp("ve", TBIF[:], TBI[:])
        tt(CAND[:].rearrange("p (h i j) -> p h i j", h=8, i=16),
           TAS[:].rearrange("p (h i) -> p h i", h=8).unsqueeze(3).to_broadcast(
               [128, 8, 16, 16]),
           TBS[:].rearrange("p (h j) -> p h j", h=8).unsqueeze(2).to_broadcast(
               [128, 8, 16, 16]),
           ALU.add)
        for hd in range(PKM_H):
            csl = CAND[:, hd * 256:(hd + 1) * 256]
            vmax8(TS16[:, hd * 16:hd * 16 + 8], csl)
            vmaxidx(POS[:, hd * 16:hd * 16 + 8], TS16[:, hd * 16:hd * 16 + 8], csl)
            vmatchrep(CZ[:], TS16[:, hd * 16:hd * 16 + 8], csl, -3e38)
            vmax8(TS16[:, hd * 16 + 8:hd * 16 + 16], CZ[:])
            vmaxidx(POS[:, hd * 16 + 8:hd * 16 + 16],
                    TS16[:, hd * 16 + 8:hd * 16 + 16], CZ[:])
        ts(IJ2[:], POS[:], 4, None, ALU.logical_shift_right)
        cp("ve", IJF[:], IJ2[:])
        tt(OH[:].rearrange("p (h m i) -> p h m i", h=8, m=16),
           IJF[:].rearrange("p (h m) -> p h m", h=8).unsqueeze(3).to_broadcast(
               [128, 8, 16, 16]),
           IOTA16[:].unsqueeze(1).unsqueeze(1).to_broadcast([128, 8, 16, 16]),
           ALU.is_equal)
        tt(OH[:].rearrange("p (h m i) -> p h m i", h=8, m=16),
           OH[:].rearrange("p (h m i) -> p h m i", h=8, m=16),
           TAIF[:].rearrange("p (h i) -> p h i", h=8).unsqueeze(2).to_broadcast(
               [128, 8, 16, 16]),
           ALU.mult)
        reduce_sum_x(SLOTA[:].rearrange("p (h m) -> p h m", h=8),
                     OH[:].rearrange("p (h m i) -> p h m i", h=8, m=16))
        ts(EIDF[:], SLOTA[:], 512.0, None, ALU.mult)
        ts(IJ2[:], POS[:], 15, None, ALU.bitwise_and)
        cp("ve", IJF[:], IJ2[:])
        tt(OH[:].rearrange("p (h m i) -> p h m i", h=8, m=16),
           IJF[:].rearrange("p (h m) -> p h m", h=8).unsqueeze(3).to_broadcast(
               [128, 8, 16, 16]),
           IOTA16[:].unsqueeze(1).unsqueeze(1).to_broadcast([128, 8, 16, 16]),
           ALU.is_equal)
        tt(OH[:].rearrange("p (h m i) -> p h m i", h=8, m=16),
           OH[:].rearrange("p (h m i) -> p h m i", h=8, m=16),
           TBIF[:].rearrange("p (h i) -> p h i", h=8).unsqueeze(2).to_broadcast(
               [128, 8, 16, 16]),
           ALU.mult)
        reduce_sum_x(SLOTA[:].rearrange("p (h m) -> p h m", h=8),
                     OH[:].rearrange("p (h m i) -> p h m i", h=8, m=16))
        tt(EIDF[:], EIDF[:], SLOTA[:], ALU.add)
        cp("ve", eids_t[:, :], EIDF[:])
        negm = ts(WSUM[:], TS16[:].rearrange("p (h m) -> p h m", h=8)[:, :, 0:1]
                  .rearrange("p h m -> p (h m)"), -1.0, None, ALU.mult)
        p.wait("se", sVE, negm)
        em = 0
        for hd in range(PKM_H):
            em = act(w16_t[:, hd * 16:(hd + 1) * 16], TS16[:, hd * 16:(hd + 1) * 16],
                     AF.Exp, bias=WSUM[:, hd:hd + 1])
        p.wait("ve", sSE, em)
        reduce_sum_x(WSUM[:].unsqueeze(2), w16_t[:, :].rearrange("p (h m) -> p h m", h=8))
        recip(WSUM[:], WSUM[:])
        tt(w16_t[:, :].rearrange("p (h m) -> p h m", h=8),
           w16_t[:, :].rearrange("p (h m) -> p h m", h=8),
           WSUM[:].unsqueeze(2).to_broadcast([128, 8, 16]), ALU.mult)
        wready = tt(w16_t[:, :].rearrange("p (h m) -> p h m", h=8),
                    w16_t[:, :].rearrange("p (h m) -> p h m", h=8),
                    ACTT[:, tb * 8:(tb + 1) * 8].unsqueeze(2).to_broadcast([128, 8, 16]),
                    ALU.mult)
        if dbg_on and tb == 0:
            dbg_dump("eids", eids_t[:, :])
            dbg_dump("w16", w16_t[:, :])
        for hd in range(PKM_H):
            def iss(hd=hd, tb=tb, wr=wready, eids_t=eids_t):
                gts = GT if hd % 2 == 0 else GT2
                igsem = sIG if hd % 2 == 0 else sIG2
                if hd == 0:
                    p.wait("gp", sVE, wr)
                if pkm_state["gt_comb"][hd % 2] > 0:
                    p.wait("gp", sVE, pkm_state["gt_comb"][hd % 2])
                for m in range(16):
                    p.indirect(gts[:, m * 128:(m + 1) * 128], expv,
                               eids_t[:, hd * 16 + m:hd * 16 + m + 1], igsem)
                pkm_state["ig_mark"][(tb % 2, hd)] = igsem.val
                if hd == PKM_H - 1:
                    pkm_state["ig_done_tb"][tb % 2] = (sIG.val, sIG2.val)

            def comb(hd=hd, tb=tb, w16_t=w16_t):
                gts = GT if hd % 2 == 0 else GT2
                igsem = sIG if hd % 2 == 0 else sIG2
                mgt = MGT if tb % 2 == 0 else MGT2
                if hd == 0 and pkm_state["tp_mark"][tb % 2] > 0:
                    p.wait("ve", sTE, pkm_state["tp_mark"][tb % 2])
                p.wait("ve", igsem, pkm_state["ig_mark"][(tb % 2, hd)])
                ts(mgt[:, hd * 128:(hd + 1) * 128], gts[:, 0:128],
                   w16_t[:, hd * 16:hd * 16 + 1], None, ALU.mult)
                cm = 0
                for m in range(1, 16):
                    cm = stt(mgt[:, hd * 128:(hd + 1) * 128],
                             gts[:, m * 128:(m + 1) * 128],
                             w16_t[:, hd * 16 + m:hd * 16 + m + 1],
                             mgt[:, hd * 128:(hd + 1) * 128], ALU.mult, ALU.add)
                pkm_state["gt_comb"][hd % 2] = cm
                pkm_state["comb"][tb] = cm

            pend = pkm_state["pending"]
            pend.append(iss)
            pkm_state["combq"].append(comb)
            if len(pkm_state["combq"]) > 1:
                pend.append(pkm_state["combq"].pop(0))
        if tb % 2 == 1:
            pend = pkm_state["pending"]
            while pkm_state["combq"]:
                pend.append(pkm_state["combq"].pop(0))

    def pkm_feed(n=1):
        for _ in range(n):
            if pkm_state["pending"]:
                pkm_state["pending"].pop(0)()

    def pkm_flush():
        while pkm_state["pending"]:
            pkm_state["pending"].pop(0)()

    def emit_pkm_transposes(seq, tb):
        mgt = MGT if tb % 2 == 0 else MGT2
        p.wait("te", sVE, pkm_state["comb"][tb])
        tp_evac = pkm_state.setdefault("tp_evac", {0: 0, 1: 0})
        tpm = 0
        for hd in range(PKM_H):
            psl = PST[:, 0:128] if hd % 2 == 0 else PV[:, 0:128]
            if tp_evac[hd % 2] > 0:
                p.wait("te", sVE, tp_evac[hd % 2])
            tpm = transpose_te(psl, mgt[:, hd * 128:(hd + 1) * 128], sem=sTE)
            p.wait("ve", sTE, tpm)
            tp_evac[hd % 2] = cp(
                "ve", BIG[:, hd * T + tb * 128: hd * T + tb * 128 + 128], psl)
        pkm_state["tp_mark"][tb % 2] = tpm
        attn_state["evv"] = tp_evac[1]

    def emit_pkm_lnsilu(seq):
        marks = {}
        p.wait("te", sSE, sSE.val)
        for c in range(NC8):
            p.wait("se", sVE, sVE.val)
            if c >= 1:
                p.wait("se", sTE, marks[c - 1])
            act(F32C[:], GB[:, c * T:(c + 1) * T], AF.Square)
            emit_stats_mms(GB[:, c * T:(c + 1) * T], c, mark=False, ones=ONESB)
            p.wait("te", sSE, sSE.val)
            marks[c] = emit_stats_sq_mms(F32C, c, mark=True)
        emit_ln_stats_finish()
        p.wait("ve", sGP, sGP.val)
        for c in range(NC8):
            tt(F32C[:], GB[:, c * T:(c + 1) * T], F32A[:], ALU.mult)
            tt(F32C[:], F32C[:], F32B[:], ALU.add)
            ts(F32C[:], F32C[:], LNW1[:, c:c + 1], LNB1[:, c:c + 1],
               ALU.mult, ALU.add)
            p.wait("se", sVE, sVE.val)
            act(GB[:, c * T:(c + 1) * T], F32C[:], AF.Silu)
            p.wait("ve", sSE, sSE.val)

    def emit_wo1_ln(seq):
        YB = BIG[:, 0:4096]
        for qb in range(NQB):
            p.wait("sy", sTE, sTE.val)
            ymark = 0
            for c in range(NC8):
                p.dma("sy", YB[0:64, c * QB:(c + 1) * QB],
                      yscr[2 * c, :, qb * QB:(qb + 1) * QB], sLS)
                ymark = p.dma("sy", YB[64:128, c * QB:(c + 1) * QB],
                              yscr[2 * c + 1, :, qb * QB:(qb + 1) * QB], sLS)
            wmarks = {}
            for ob in range(NC8):
                par = ob % 2
                if ob == 0 and qb == 0:
                    p.wait("gp", sTE, sTE.val)
                    wmarks[0] = p.dma("gp", WB[:, 0:1024], wo1[0], sWB0)
                    p.wait("gp", sTE, sTE.val)
                    wmarks[1] = p.dma("gp", WB[:, 1024:2048], wo1[1], sWB1)
                elif ob == 0:
                    p.wait("gp", sTE, sTE.val)
                    wmarks[0] = p.dma("gp", WB[:, 0:1024], wo1[0], sWB0)
                    wmarks[1] = p.dma("gp", WB[:, 1024:2048], wo1[1], sWB1)
                p.wait("te", sWB0 if par == 0 else sWB1, wmarks[ob])
                p.wait("te", sLS, ymark)
                p.wait("te", sSE, sSE.val)
                for c in range(NC8):
                    mm(PV[:], WB[:, par * 1024 + c * 128:par * 1024 + (c + 1) * 128],
                       YB[:, c * QB:(c + 1) * QB],
                       start=(c == 0), stop=(c == NC8 - 1),
                       sem=sTE if c == NC8 - 1 else None)
                if ob + 2 < NC8:
                    p.wait("gp", sTE, sTE.val)
                    wmarks[ob + 2] = p.dma(
                        "gp", WB[:, par * 1024:par * 1024 + 1024], wo1[ob + 2],
                        sWB0 if par == 0 else sWB1)
                p.wait("se", sTE, sTE.val)
                p.wait("se", sVE, sVE.val)
                act(F32A[:, 0:QB], PV[:], AF.Identity)
                p.wait("ve", sSE, sSE.val)
                p.wait("ve", sTE, sTE.val)
                tt(F32A[:, 0:QB], F32A[:, 0:QB],
                   XN[:, ob * T + qb * QB: ob * T + qb * QB + QB], ALU.add)
                p.wait("se", sVE, sVE.val)
                act(F32B[:, 0:QB], F32A[:, 0:QB], AF.Square)
                p.wait("te", sVE, sVE.val)
                p.wait("te", sSE, sSE.val)
                emit_stats_mms(F32A[:, 0:QB], ob, qb)
                emit_stats_sq_mms(F32B[:, 0:QB], ob, qb, mark=True)
                cp("ve", XN[:, ob * T + qb * QB: ob * T + qb * QB + QB],
                   F32A[:, 0:QB])
        attn_state["wb_free"][0] = attn_state["wb_free"][1] = sTE.val
        emit_ln_stats_finish()
        p.wait("ve", sGP, sGP.val)
        for c in range(NC8):
            tt(F32C[:], XN[:, c * T:(c + 1) * T], F32A[:], ALU.mult)
            tt(F32C[:], F32C[:], F32B[:], ALU.add)
            ts(BIG[:, c * T:(c + 1) * T], F32C[:], G1V[:, c:c + 1], B1V[:, c:c + 1],
               ALU.mult, ALU.add)

    def emit_final_update(seq):
        p.wait("sy", sXS, sXS.val)
        p.wait("sy", sSE, sSE.val)
        p.wait("sy", sVE, sVE.val)
        p.wait("sy", sTE, sTE.val)
        for c in range(NC8):
            if c > 0:
                p.wait("sy", sVE, sVE.val)
            t0 = p.dma("sy", STG[:], x1scr[seq, c], sST1)
            p.wait("ve", sST1, t0)
            tt(F32C[:], BIG[:, c * T:(c + 1) * T], GB[:, c * T:(c + 1) * T],
               ALU.mult)
            tt(F32C[:], F32C[:], STG[:], ALU.add)
            p.wait("sy", sVE, sVE.val)
            p.dma("sy", out_hbm[seq, c], F32C[:], sSTS)
            p.wait("ve", sSTS, sSTS.val)

    for seq in range(NSEQ):
        if seq == 0:
            p.wait("te", sLS, sLS.val)
            p.wait("ve", sLS, sLS.val)
            p.wait("se", sLS, sLS.val)
        emit_x_stats(seq, x_in)
        tc = p.dma("sy", COS[:], cos0, sLS)
        tsn = p.dma("sy", SIN[:], sin0, sLS)
        emit_ln_stats_finish()
        emit_xn_apply(seq, x_in)
        if "xn" in dbg and seq == 0:
            dbg_dump("xn", XN[:])
        emit_ffn0(seq)
        if "g" in dbg and seq == 0:
            dbg_dump("g", GB[:])
        p.wait("ve", sLS, tsn)
        for hp in range(8):
            emit_attn_hp(seq, 0, hp)
        if "y" in dbg and seq == 0:
            dbg_dump("y", BIG[:])
        emit_wo_update0(seq)
        emit_ln_stats_finish()
        p.wait("sy", sXS, sXS.val)
        emit_xn_apply(seq, x1scr)
        tc = p.dma("sy", COS[:], cos1, sLS)
        tsn = p.dma("sy", SIN[:], sin1, sLS)
        p.wait("ve", sLS, tsn)
        emit_pkm_hproj(seq, dbg_on=("hpkm" in dbg and seq == 0))
        emit_pkm_ugate(seq)
        p.wait("sy", sLG, keys_cast_mark)
        for hp in range(8):
            if hp >= 2:
                emit_pkm_transposes(seq, 2 * (hp - 2))
                emit_pkm_transposes(seq, 2 * (hp - 2) + 1)
            emit_attn_hp(seq, 1, hp)
            emit_pkm_tb(seq, 2 * hp, dbg_on=(seq == 0))
            emit_pkm_tb(seq, 2 * hp + 1, dbg_on=(seq == 0))
        emit_pkm_transposes(seq, 12)
        emit_pkm_transposes(seq, 13)
        pkm_flush()
        emit_pkm_transposes(seq, 14)
        emit_pkm_transposes(seq, 15)
        emit_pkm_wout(seq)
        emit_pkm_lnsilu(seq)
        emit_wo1_ln(seq)
        if "s1" in dbg and seq == 0:
            dbg_dump("s1", BIG[:])
        emit_final_update(seq)

    p.wait("gp", sSTS, sSTS.val)
    p.wait("gp", sIG, sIG.val)
    p.wait("gp", sIG2, sIG2.val)

    for s in p.sems.values():
        s.handle = ctx.enter_context(nc.semaphore(s.name))
    p.run(nc)
    ctx.close()
    return nc, dbg



def tile_proj(W, nob=None):
    outdim, indim = W.shape
    nob = outdim // 128 if nob is None else nob
    nic = indim // 128
    wt = W.T.reshape(nic, 128, outdim)
    out = np.empty((nob, 128, nic * 128), W.dtype)
    for ob in range(nob):
        sl = wt[:, :, ob * 128:(ob + 1) * 128]
        out[ob] = sl.transpose(1, 0, 2).reshape(128, nic * 128)
    return np.ascontiguousarray(out)


def rope_tables_l0():
    HD = 64
    inv = 1.0 / 10000.0 ** (np.arange(0, HD, 2, dtype=np.float32) / HD)
    t = np.arange(T, dtype=np.float32)
    emb = np.concatenate([np.outer(t, inv)] * 2, axis=-1)
    cos = np.cos(emb); sin = np.sin(emb)
    cos_t = np.zeros((128, T), np.float32)
    sin_t = np.zeros((128, T), np.float32)
    for r in range(128):
        d = r % 64
        cos_t[r] = cos[:, d]
        sin_t[r] = sin[:, d] * (-1.0 if d % 2 == 0 else 1.0)
    return cos_t, sin_t


def rope_tables_l1():
    HD = 64
    i = np.arange(1, HD // 2 + 1, dtype=np.float32)
    thetas = np.repeat(1000.0 ** (-2.0 * i / HD), 2)
    pos = np.arange(1, T + 1, dtype=np.float32)
    args = pos[:, None] * thetas[None, :]
    sin = np.sin(args); cos = np.cos(args)
    cos_t = np.zeros((128, T), np.float32)
    sin_t = np.zeros((128, T), np.float32)
    for r in range(128):
        d = r % 64
        cos_t[r] = cos[:, d]
        sin_t[r] = sin[:, d] * (-1.0 if d % 2 == 0 else 1.0)
    return cos_t, sin_t


def pack_inputs(core, inp):
    d = {}
    x = inp["L0_attn_x"]
    xs = x[2 * core:2 * core + 2]
    d["x_in"] = np.ascontiguousarray(xs.transpose(0, 2, 1)).reshape(NSEQ, NC8, 128, T)

    d["wq0"] = tile_proj(inp["L0_attn_Wq"])
    d["wk0"] = tile_proj(inp["L0_attn_Wk"])
    d["wv0"] = tile_proj(inp["L0_attn_Wv"])
    d["wo0"] = tile_proj(inp["L0_attn_Wo"])
    d["bq0"] = np.ascontiguousarray(inp["L0_attn_bq"].reshape(8, 128).T)
    d["bk0"] = np.ascontiguousarray(inp["L0_attn_bk"].reshape(8, 128).T)
    d["bv0"] = np.ascontiguousarray(
        np.broadcast_to(inp["L0_attn_bv"][None, :], (128, D)))
    vecs = np.zeros((128, 64), np.float32)
    vecs[:, 0:8] = inp["L0_attn_bq"].reshape(8, 128).T
    vecs[:, 8:16] = inp["L0_attn_bk"].reshape(8, 128).T
    vecs[:, 16:24] = inp["L0_attn_bo"].reshape(8, 128).T
    vecs[:, 24:32] = inp["L1_attn_ln_gamma"].reshape(8, 128).T
    vecs[:, 32:40] = inp["L1_attn_ln_beta"].reshape(8, 128).T
    vecs[:, 40:48] = inp["L1_ffn_ln_w"].reshape(8, 128).T
    vecs[:, 48:56] = inp["L1_ffn_ln_b"].reshape(8, 128).T
    d["vecs_in"] = vecs
    d["bo0"] = vecs[:, 16:24].copy()
    import ml_dtypes
    c0, s0 = rope_tables_l0()
    d["cos0"], d["sin0"] = c0.astype(ml_dtypes.bfloat16), s0.astype(ml_dtypes.bfloat16)
    c1, s1 = rope_tables_l1()
    d["cos1"], d["sin1"] = c1.astype(ml_dtypes.bfloat16), s1.astype(ml_dtypes.bfloat16)

    d["wg0"] = tile_proj(inp["L0_ffn_w_gate"])
    d["wu0"] = tile_proj(inp["L0_ffn_w_up"])
    d["wd0"] = tile_proj(inp["L0_ffn_w_down"])

    w_in = inp["L1_attn_w_in"]
    d["wq1"] = tile_proj(w_in[0:D])
    d["wk1"] = tile_proj(w_in[D:2 * D])
    d["wv1"] = tile_proj(w_in[2 * D:3 * D])
    d["wo1"] = tile_proj(inp["L1_attn_w_out"])

    d["wi1"] = tile_proj(inp["L1_ffn_W_in"])
    d["keysa"] = np.ascontiguousarray(inp["L1_ffn_keys_a"].transpose(0, 2, 1))
    d["keysb"] = np.ascontiguousarray(inp["L1_ffn_keys_b"].transpose(0, 2, 1))
    d["ush"] = np.ascontiguousarray(inp["L1_ffn_u_shared"].reshape(128, 1))
    d["expv"] = inp["L1_ffn_expert_v"]
    d["wout1"] = tile_proj(inp["L1_ffn_W_out"])
    d["ones_c"] = np.ones((128, 32), np.float32)
    d["iota16"] = np.broadcast_to(np.arange(16, dtype=np.float32)[None, :],
                                  (128, 16)).copy()
    d["ident_in"] = np.eye(128, dtype=np.float32)
    return d


_CACHED = {}


def kernel(**inputs):
    if "nc" not in _CACHED:
        nc, _ = build_program()
        nc.finalize()
        _CACHED["nc"] = nc
    nc = _CACHED["nc"]
    in_maps = [pack_inputs(c, inputs) for c in range(8)]
    res = run_bass_kernel_spmd(nc, in_maps, core_ids=list(range(8)))
    out = np.empty((16, T, D), np.float32)
    for c in range(8):
        o = res.results[c]["out"]
        o = o.reshape(NSEQ, D, T).transpose(0, 2, 1)
        out[2 * c:2 * c + 2] = o
    return out


# version 12
# speedup vs baseline: 1.8079x; 1.0815x over previous
import numpy as np
import concourse.bacc as bacc
import concourse.bass as bass
import concourse.mybir as mybir
from concourse.bass_utils import run_bass_kernel_spmd

F32 = mybir.dt.float32
BF16 = mybir.dt.bfloat16
U32 = mybir.dt.uint32
U16 = mybir.dt.uint16
I32 = mybir.dt.int32
AF = mybir.ActivationFunctionType
ALU = mybir.AluOpType

T = 2048
D = 1024
NC8 = 8
QB = 512
NQB = T // QB
NKC = T // 128
NSEQ = 2
NHB = 24
PKM_H = 8
PKM_S = 512
EPS = 1e-5
NTB = T // 128


class Sem:
    def __init__(self, name):
        self.name = name
        self.val = 0
        self.handle = None

    def inc(self, n=1):
        self.val += n
        return self.val


class Plan:
    AUTO_SELF = ("ve", "se", "gp")

    def __init__(self):
        self.ops = []
        self.sems = {}
        self.auto = {e: self.sem(e) for e in self.AUTO_SELF}

    def sem(self, name):
        s = self.sems.get(name)
        if s is None:
            s = Sem(name)
            self.sems[name] = s
        return s

    def emit(self, engine, fn):
        self.ops.append((engine, fn))

    def op(self, engine, build, sem=None, n=1):
        auto = self.auto.get(engine)
        prev = auto.val if auto is not None else 0
        if auto is not None:
            auto.inc(1)
            if sem is auto:
                sem = None
        if sem is not None:
            sem.inc(n)

        def fn(eng, build=build, sem=sem, n=n, auto=auto, prev=prev):
            if auto is not None and prev > 0:
                eng.wait_ge(auto.handle, prev)
            ins = build(eng)
            assert ins is not None, "builder must return its instruction"
            if auto is not None:
                ins.then_inc(auto.handle, 1)
            if sem is not None:
                ins.then_inc(sem.handle, n)
        self.emit(engine, fn)
        return sem.val if sem is not None else (auto.val if auto else None)

    def wait(self, engine, sem, val=None):
        if val is None:
            val = sem.val
        def fn(eng, sem=sem, val=val):
            eng.wait_ge(sem.handle, val)
        self.emit(engine, fn)

    def dma(self, engine, out, in_, sem, slow_ok=False):
        tgt = sem.inc(16)
        def fn(eng, out=out, in_=in_, sem=sem, slow_ok=slow_ok):
            eng.dma_start(out=out, in_=in_,
                          allow_slow_non_contiguous=slow_ok).then_inc(sem.handle, 16)
        self.emit(engine, fn)
        return tgt

    def indirect(self, out, in_, idx_ap, sem):
        tgt = sem.inc(16)
        def fn(eng, out=out, in_=in_, idx_ap=idx_ap, sem=sem):
            eng.indirect_dma_start(
                out=out, out_offset=None, in_=in_,
                in_offset=bass.IndirectOffsetOnAxis(ap=idx_ap, axis=0),
            ).then_inc(sem.handle, 16)
        self.emit("gp", fn)
        return tgt

    def run(self, nc):
        with nc.Block() as block:
            def make(engine_attr, key):
                def body(eng):
                    for e, fn in self.ops:
                        if e == key:
                            fn(eng)
                engine_attr(body)
            make(block.tensor, "te")
            make(block.vector, "ve")
            make(block.scalar, "se")
            make(block.gpsimd, "gp")
            make(block.sync, "sy")


def build_program(debug_names=()):
    nc = bacc.Bacc("TRN2", target_bir_lowering=False)
    p = Plan()
    from contextlib import ExitStack
    ctx = ExitStack()

    dram = lambda name, shape, dt=F32, kind="ExternalInput": nc.dram_tensor(
        name, shape, dt, kind=kind).ap()

    x_in = dram("x_in", [NSEQ, NC8, 128, T])
    out_hbm = dram("out", [NSEQ, NC8, 128, T], kind="ExternalOutput")
    x1scr = dram("x1scr", [NSEQ, NC8, 128, T], kind="Internal")
    yscr = dram("yscr", [16, 64, T], BF16, kind="Internal")

    wq0 = dram("wq0", [8, 128, NC8 * 128])
    wk0 = dram("wk0", [8, 128, NC8 * 128])
    wv0 = dram("wv0", [8, 128, NC8 * 128])
    wo0 = dram("wo0", [NC8, 128, NC8 * 128])
    bq0 = dram("bq0", [128, 8])
    bk0 = dram("bk0", [128, 8])
    bv0 = dram("bv0", [128, NC8 * 128])
    bo0 = dram("bo0", [128, NC8])
    cos0 = dram("cos0", [128, T], BF16)
    sin0 = dram("sin0", [128, T], BF16)
    wg0 = dram("wg0", [NHB, 128, NC8 * 128], I32)
    wu0 = dram("wu0", [NHB, 128, NC8 * 128], I32)
    wd0 = dram("wd0", [NC8, 128, NHB * 128], I32)
    wgs = dram("wgs", [NHB, 128, NC8 * 128], BF16, kind="Internal")
    wus = dram("wus", [NHB, 128, NC8 * 128], BF16, kind="Internal")
    wds = dram("wds", [NC8, 128, NHB * 128], BF16, kind="Internal")
    wq1 = dram("wq1", [8, 128, NC8 * 128])
    wk1 = dram("wk1", [8, 128, NC8 * 128])
    wv1 = dram("wv1", [8, 128, NC8 * 128])
    wo1 = dram("wo1", [NC8, 128, NC8 * 128])
    cos1 = dram("cos1", [128, T], BF16)
    sin1 = dram("sin1", [128, T], BF16)
    wi1 = dram("wi1", [8, 128, NC8 * 128])
    keysa = dram("keysa", [PKM_H, 128, PKM_S])
    keysb = dram("keysb", [PKM_H, 128, PKM_S])
    ush = dram("ush", [128, 1])
    expv = dram("expv", [PKM_S * PKM_S, 128])
    wout1 = dram("wout1", [NC8, 128, NC8 * 128])
    vecs_in = dram("vecs_in", [128, 64])
    ones_c = dram("ones_c", [128, 32])
    iota16_in = dram("iota16", [128, 16])

    dbg_specs = {
        "xn": [128, NC8 * T], "g": [128, NC8 * T], "y": [128, NC8 * T],
        "x1": [NC8, 128, T], "s1": [128, NC8 * T],
        "eids": [128, 8 * 16], "w16": [128, 8 * 16], "mgt": [128, 8 * 128],
        "hpkm": [128, NC8 * T],
    }
    dbg = {k: dram("dbg_" + k, dbg_specs[k], kind="ExternalOutput")
           for k in debug_names}

    sLG = p.sem("ldg")
    sLS = p.sem("lds")
    sST1 = p.sem("st1")
    sST2 = p.sem("st2")
    sWB0 = p.sem("wb0")
    sWB1 = p.sem("wb1")
    sWBD = p.sem("wbd")
    sWBD2 = p.sem("wbd2")
    sKA = p.sem("ka")
    sKB = p.sem("kb")
    sKA2 = p.sem("ka2")
    sKB2 = p.sem("kb2")
    sXS = p.sem("xs")
    sDBG = p.sem("dbg")
    sTE = p.sem("te_")
    sVE = p.auto["ve"]
    sSE = p.auto["se"]
    sGP = p.auto["gp"]
    sSTS = p.sem("sts")
    sIG = p.sem("ig")
    sIG2 = p.sem("ig2")

    sb = lambda name, shape, dt=BF16: ctx.enter_context(nc.sbuf_tensor(name, shape, dt))
    psum = lambda name, shape: ctx.enter_context(nc.psum_tensor(name, shape, F32))

    XN = sb("XN", [128, NC8 * T])
    GB = sb("GB", [128, NC8 * T])
    BIG = sb("BIG", [128, NC8 * T])
    F32A = sb("F32A", [128, T], F32)
    F32B = sb("F32B", [128, T], F32)
    F32C = sb("F32C", [128, T], F32)
    STG = sb("STG", [128, T], F32)
    STG2 = sb("STG2", [128, T], F32)
    QR = sb("QR", [128, T])
    KR = sb("KR", [128, T])
    QP = sb("QP", [128, T])
    KP = sb("KP", [128, T])
    VT = sb("VT", [128, 2 * NKC * 65])
    ATT = sb("ATT", [128, 2 * QB])
    ATT2 = sb("ATT2", [128, 2 * QB])
    SG = sb("SG", [128, QB])
    WB = sb("WB", [128, 4 * NC8 * 128])
    WBD = sb("WBD", [128, 2 * NHB * 128])
    COS = sb("COS", [128, T])
    SIN = sb("SIN", [128, T])
    SMALL = sb("SMALL", [128, 64], F32)
    BVB = sb("BVB", [128, NC8 * 128], F32)
    ONES1 = sb("ONES1", [128, 1], F32)
    ONESB = sb("ONESB", [128, 1])
    IOTA16 = sb("IOTA16", [128, 16], F32)
    KA = sb("KA", [128, 2 * PKM_S])
    KB_ = sb("KB_", [128, 2 * PKM_S])
    USH = sb("USH", [128, 1])
    ACTT = sb("ACTT", [128, NTB * 8], F32)
    SCF = sb("SCF", [128, 2 * PKM_S], F32)
    TAS = sb("TAS", [128, 8 * 16], F32)
    TBS = sb("TBS", [128, 8 * 16], F32)
    TAI = sb("TAI", [128, 8 * 16], U16)
    TBI = sb("TBI", [128, 8 * 16], U16)
    POS = sb("POS", [128, 8 * 16], U16)
    IJ2 = sb("IJ2", [128, 8 * 16], U16)
    EIDS = sb("EIDS", [128, 8 * 16], U32)
    W16 = sb("W16", [128, 8 * 16], F32)
    WSUM = sb("WSUM", [128, 8], F32)
    ident_in = dram("ident_in", [128, 128])

    PBIG = psum("PBIG", [128, 2048])
    PSC = psum("PSC", [128, 1024])
    PV = psum("PV", [128, 512])
    PST = psum("PST", [128, 512])

    CAND = STG[:, :]
    SCZ = GB[:, 0:1024].bitcast(F32)
    OH = GB[:, 0:4096].bitcast(F32)
    ATT3 = GB[:, 0:1024]
    ATT4 = GB[:, 1024:2048]
    GT = GB[:, 4096:6144]
    GT2 = GB[:, 8192:10240]
    MGT = GB[:, 6144:8192].bitcast(F32)
    MGT2 = GB[:, 10240:12288].bitcast(F32)
    F32V = GB[:, 12288:16384].bitcast(F32)
    TAIF = F32V[:, 0:128]
    TBIF = F32V[:, 128:256]
    TS16 = F32V[:, 256:384]
    IJF = F32V[:, 384:512]
    SLOTA = F32V[:, 512:640]
    EIDF = F32V[:, 640:768]
    CZ = F32V[:, 768:1024]
    EIDSb = GB[:, 14336:14592].bitcast(U32)
    W16b = F32V[:, 1152:1280]

    swapmask = []
    for i in range(0, 32, 2):
        swapmask += [i + 1, i]

    def mm(out_ap, lhsT_ap, rhs_ap, start, stop, sem=None):
        def b(eng, o=out_ap, l=lhsT_ap, r=rhs_ap, st=start, sp=stop):
            return eng.matmul(o, l, r, start=st, stop=sp)
        return p.op("te", b, sem)

    def cp(engine, out_ap, in_ap):
        def b(eng, o=out_ap, i=in_ap):
            return eng.tensor_copy(o, i)
        return p.op(engine, b)

    def act(out_ap, in_ap, func, bias=0.0, scale=1.0):
        def b(eng, o=out_ap, i=in_ap, f=func, bi=bias, sc=scale):
            return eng.activation(out=o, in_=i, func=f, bias=bi, scale=sc)
        return p.op("se", b)

    def tt(out_ap, a_ap, b_ap, alu, engine="ve"):
        def b(eng, o=out_ap, a=a_ap, bb=b_ap, alu=alu):
            return eng.tensor_tensor(o, a, bb, alu)
        return p.op(engine, b)

    def ts(out_ap, in_ap, s1, s2=None, op0=ALU.mult, op1=ALU.bypass, engine="ve"):
        def b(eng, o=out_ap, i=in_ap, s1=s1, s2=s2, op0=op0, op1=op1):
            return eng.tensor_scalar(out=o, in0=i, scalar1=s1, scalar2=s2,
                                     op0=op0, op1=op1)
        return p.op(engine, b)

    def stt(out_ap, in0_ap, scalar, in1_ap, op0, op1, engine="ve"):
        def b(eng, o=out_ap, i0=in0_ap, s=scalar, i1=in1_ap, op0=op0, op1=op1):
            return eng.scalar_tensor_tensor(out=o, in0=i0, scalar=s, in1=i1,
                                            op0=op0, op1=op1)
        return p.op(engine, b)

    def shuffle(out_ap, in_ap):
        def b(eng, o=out_ap, i=in_ap):
            return eng.stream_shuffle(out=o, in_=i, mask=swapmask)
        return p.op("ve", b)

    def vmax8(out_ap, in_ap):
        def b(eng, o=out_ap, i=in_ap):
            return eng.max(out=o, in_=i)
        return p.op("ve", b)

    def vmaxidx(out_ap, max_ap, in_ap):
        def b(eng, o=out_ap, m=max_ap, i=in_ap):
            return eng.max_index(out=o, in_max=m, in_values=i)
        return p.op("ve", b)

    def vmatchrep(out_ap, rep_ap, in_ap, immv):
        def b(eng, o=out_ap, r=rep_ap, i=in_ap, v=immv):
            return eng.match_replace(out=o, in_to_replace=r, in_values=i, imm_value=v)
        return p.op("ve", b)

    def reduce_sum_x(out_ap, in_ap):
        def b(eng, o=out_ap, i=in_ap):
            return eng.reduce_sum(o, i, mybir.AxisListType.X)
        return p.op("ve", b)

    def recip(out_ap, in_ap):
        def b(eng, o=out_ap, i=in_ap):
            return eng.reciprocal_approx_fast(o, i)
        return p.op("ve", b)

    def pbcast(out_ap, row_ap):
        def b(eng, o=out_ap, r=row_ap):
            return eng.partition_broadcast(o, r)
        return p.op("gp", b)

    def transpose_te(out_psum, in_ap, sem=None):
        def b(eng, o=out_psum, i=in_ap):
            return eng.transpose(o, i, BVB[:, 0:128])
        return p.op("te", b, sem)

    def dbg_dump(name, src_ap, f32stage=None):
        if name not in dbg:
            return
        p.wait("gp", sVE, sVE.val)
        p.wait("gp", sSE, sSE.val)
        p.wait("gp", sDBG, sDBG.val)
        p.dma("gp", dbg[name], src_ap, sDBG)
        p.wait("ve", sDBG, sDBG.val)

    p.dma("sy", SMALL[:], vecs_in, sLS)
    p.dma("sy", BVB[:], bv0, sLS)
    p.dma("sy", ONES1[:], ones_c[:, 0:1], sLS, slow_ok=True)
    p.dma("sy", IOTA16[:], iota16_in, sLS)
    p.dma("gp", VT[:].rearrange("p (a c) -> p a c", c=65)[:, :, 64:65],
          ones_c[:, 0:32].unsqueeze(2), sLG, slow_ok=True)
    p.dma("gp", USH[:], ush, sLG, slow_ok=True)
    p.dma("gp", ONESB[:], ones_c[:, 0:1], sLG, slow_ok=True)

    BQ0 = SMALL[:, 0:8]
    BK0 = SMALL[:, 8:16]
    BO0 = SMALL[:, 16:24]
    G1V = SMALL[:, 24:32]
    B1V = SMALL[:, 32:40]
    LNW1 = SMALL[:, 40:48]
    LNB1 = SMALL[:, 48:56]

    p.dma("gp", wgs, wg0, sLG)
    p.dma("gp", wus, wu0, sLG)
    p.dma("gp", wds, wd0, sLG)
    keysa_bf = dram("keysa_bf", [PKM_H, 128, PKM_S], BF16, kind="Internal")
    keysb_bf = dram("keysb_bf", [PKM_H, 128, PKM_S], BF16, kind="Internal")
    p.dma("gp", keysa_bf, keysa, sLG)
    keys_cast_mark = p.dma("gp", keysb_bf, keysb, sLG)


    def emit_ln_stats_finish():
        MS = STG2[0:1, :]
        p.wait("se", sTE, sTE.val)
        act(MS, PBIG[0:1, :], AF.Identity, scale=1.0 / D)
        act(F32C[0:1, :], PBIG[32:33, :], AF.Identity, scale=1.0 / D)
        p.wait("ve", sSE, sSE.val)
        tt(F32B[0:1, :], MS, MS, ALU.mult)
        tt(F32B[0:1, :], F32C[0:1, :], F32B[0:1, :], ALU.subtract)
        ts(F32B[0:1, :], F32B[0:1, :], float(EPS), None, ALU.add)
        recip(F32C[0:1, :], F32B[0:1, :])
        p.wait("se", sVE, sVE.val)
        act(F32C[0:1, :], F32C[0:1, :], AF.Sqrt)
        p.wait("ve", sSE, sSE.val)
        stt(F32B[0:1, :], MS, -1.0, F32C[0:1, :], ALU.mult, ALU.mult)
        p.wait("gp", sVE, sVE.val)
        p.wait("gp", sSE, sSE.val)
        p.wait("gp", sXS, sXS.val)
        pbcast(F32A[:], F32C[0:1, :])
        pbcast(F32B[:], F32B[0:1, :])

    def emit_stats_mms(src_f32_ap, c, qb=None, mark=False, ones=None):
        ones = ONES1 if ones is None else ones
        r = None
        if qb is None:
            for q in range(NQB):
                r = mm(PBIG[0:1, q * QB:(q + 1) * QB], ones[:],
                       src_f32_ap[:, q * QB:(q + 1) * QB],
                       start=(c == 0), stop=(c == NC8 - 1),
                       sem=sTE if (mark and q == NQB - 1) else None)
        else:
            r = mm(PBIG[0:1, qb * QB:(qb + 1) * QB], ones[:], src_f32_ap,
                   start=(c == 0), stop=(c == NC8 - 1),
                   sem=sTE if mark else None)
        return r

    def emit_stats_sq_mms(sq_f32_ap, c, qb=None, mark=False, ones=None):
        ones = ONES1 if ones is None else ones
        r = None
        if qb is None:
            for q in range(NQB):
                r = mm(PBIG[32:33, q * QB:(q + 1) * QB], ones[:],
                       sq_f32_ap[:, q * QB:(q + 1) * QB],
                       start=(c == 0), stop=(c == NC8 - 1),
                       sem=sTE if (mark and q == NQB - 1) else None)
        else:
            r = mm(PBIG[32:33, qb * QB:(qb + 1) * QB], ones[:], sq_f32_ap,
                   start=(c == 0), stop=(c == NC8 - 1),
                   sem=sTE if mark else None)
        return r

    def emit_x_stats(seq, src_dram):
        marks = {}
        p.wait("sy", sSE, sSE.val)
        p.wait("sy", sVE, sVE.val)
        p.wait("sy", sTE, sTE.val)
        p.wait("se", sSTS, sSTS.val)
        p.wait("ve", sSTS, sSTS.val)
        for c in range(NC8):
            S = STG if c % 2 == 0 else STG2
            sS = sST1 if c % 2 == 0 else sST2
            if c >= 2:
                p.wait("sy", sTE, marks[c - 2])
            t = p.dma("sy", S[:], src_dram[seq, c], sS)
            p.wait("se", sS, t)
            if c >= 1:
                p.wait("se", sTE, marks[c - 1])
            act(F32C[:], S[:], AF.Square)
            p.wait("te", sSE, sSE.val)
            p.wait("te", sS, t)
            emit_stats_mms(S, c, mark=False)
            marks[c] = emit_stats_sq_mms(F32C, c, mark=True)

    def emit_xn_apply(seq, src_dram):
        p.wait("ve", sGP, sGP.val)
        p.wait("sy", sGP, sGP.val)
        p.wait("sy", sSE, sSE.val)
        p.wait("sy", sVE, sVE.val)
        p.wait("sy", sTE, sTE.val)
        for c in range(NC8):
            S = STG if c % 2 == 0 else STG2
            sS = sST1 if c % 2 == 0 else sST2
            if c >= 2:
                p.wait("sy", sVE, sVE.val)
            t = p.dma("sy", S[:], src_dram[seq, c], sS)
            p.wait("ve", sS, t)
            tt(F32C[:], S[:], F32A[:], ALU.mult)
            tt(XN[:, c * T:(c + 1) * T], F32C[:], F32B[:], ALU.add)

    def emit_ffn0(seq):
        SCALE = 1.0 / 127.0
        p.wait("gp", sLG, sLG.val)
        wsem = {0: sWB0, 1: sWB1}
        for qb in range(NQB):
            xn_sl = lambda c: XN[:, c * T + qb * QB: c * T + qb * QB + QB]
            hb_te_done = {}
            load_marks = {}

            def issue_pair(hb):
                pair = hb % 2
                if hb >= 2:
                    p.wait("gp", sTE, hb_te_done[hb - 2])
                else:
                    p.wait("gp", sTE, sTE.val)
                t1 = p.dma("gp", WB[:, pair * 2048:pair * 2048 + 1024], wgs[hb],
                           wsem[pair])
                t2 = p.dma("gp", WB[:, pair * 2048 + 1024:pair * 2048 + 2048],
                           wus[hb], wsem[pair])
                load_marks[hb] = t2

            issue_pair(0)
            if NHB > 1:
                issue_pair(1)
            for hb in range(NHB):
                pair = hb % 2
                wg_sl = WB[:, pair * 2048:pair * 2048 + 1024]
                wu_sl = WB[:, pair * 2048 + 1024:pair * 2048 + 2048]
                psl = PSC[:, (hb % 2) * QB:(hb % 2) * QB + QB]
                p.wait("te", wsem[pair], load_marks[hb])
                p.wait("te", sSE, sSE.val)
                p.wait("te", sVE, sVE.val)
                for c in range(NC8):
                    mm(psl, wg_sl[:, c * 128:(c + 1) * 128], xn_sl(c),
                       start=(c == 0), stop=(c == NC8 - 1),
                       sem=sTE if c == NC8 - 1 else None)
                p.wait("se", sTE, sTE.val)
                act(SG[:], psl, AF.Silu, scale=SCALE)
                p.wait("te", sSE, sSE.val)
                for c in range(NC8):
                    mm(psl, wu_sl[:, c * 128:(c + 1) * 128], xn_sl(c),
                       start=(c == 0), stop=(c == NC8 - 1),
                       sem=sTE if c == NC8 - 1 else None)
                hb_te_done[hb] = sTE.val
                if hb + 2 < NHB:
                    issue_pair(hb + 2)
                p.wait("ve", sTE, sTE.val)
                p.wait("ve", sSE, sSE.val)
                def hb_mult(eng, hb=hb, psl=psl):
                    return eng.scalar_tensor_tensor(
                        out=BIG[:, hb * QB:(hb + 1) * QB], in0=psl,
                        scalar=SCALE, in1=SG[:], op0=ALU.mult, op1=ALU.mult)
                p.op("ve", hb_mult)
            dn_te_done = {}
            dn_marks = {}
            def issue_down(ob):
                if ob >= 2:
                    p.wait("gp", sTE, dn_te_done[ob - 2])
                else:
                    p.wait("gp", sTE, hb_te_done[NHB - 1] if ob == 0 else dn_marks.get("first_gate", sTE.val))
                t = p.dma("gp", WBD[:, (ob % 2) * NHB * 128:(ob % 2 + 1) * NHB * 128],
                          wds[ob], sWBD if ob % 2 == 0 else sWBD2)
                dn_marks[ob] = t
            issue_down(0)
            issue_down(1)
            for ob in range(NC8):
                wd_sl = WBD[:, (ob % 2) * NHB * 128:(ob % 2 + 1) * NHB * 128]
                p.wait("te", sWBD if ob % 2 == 0 else sWBD2, dn_marks[ob])
                p.wait("te", sVE, sVE.val)
                p.wait("te", sSE, sSE.val)
                for hb in range(NHB):
                    mm(PV[:], wd_sl[:, hb * 128:(hb + 1) * 128],
                       BIG[:, hb * QB:(hb + 1) * QB],
                       start=(hb == 0), stop=(hb == NHB - 1),
                       sem=sTE if hb == NHB - 1 else None)
                dn_te_done[ob] = sTE.val
                if ob + 2 < NC8:
                    issue_down(ob + 2)
                p.wait("se", sTE, sTE.val)
                act(GB[:, ob * T + qb * QB: ob * T + qb * QB + QB],
                    PV[:], AF.Silu, scale=SCALE)
        attn_state["wb_free"][0] = attn_state["wb_free"][1] = sTE.val

    attn_state = {"te_after_vproj": 0, "wb_free": {0: 0, 1: 0}, "marks": {},
                  "bank_free": {0: 0, 1: 0}, "proj_evac": 0, "rope_done": 0,
                  "evv": 0, "den_free": 0, "norm_done": 0}

    def attn_issue_loads(layer, hp):
        if layer == 0:
            wq, wk, wv = wq0, wk0, wv0
        else:
            wq, wk, wv = wq1, wk1, wv1
        par = hp % 2
        p.wait("gp", sTE, attn_state["wb_free"][par])
        base = par * 2048
        tq = p.dma("gp", WB[:, base:base + 1024], wq[hp], sWB0 if par == 0 else sWB1)
        tk = p.dma("gp", WB[:, base + 1024:base + 2048], wk[hp],
                   sWB0 if par == 0 else sWB1)
        tv = p.dma("gp", WBD[:, par * NHB * 128:par * NHB * 128 + 1024], wv[hp],
                   sWBD if par == 0 else sWBD2)
        attn_state["marks"][hp] = (tk, tv)

    def emit_attn_hp(seq, layer, hp):
        if layer == 0:
            wq, wk, wv = wq0, wk0, wv0
            plain = False
        else:
            wq, wk, wv = wq1, wk1, wv1
            plain = True
            p.wait("se", sVE, sVE.val)
        scale = 0.125

        if layer == 0 and hp == 0:
            p.wait("gp", sVE, sVE.val)
            p.wait("gp", sTE, sTE.val)
            tbv = p.dma("gp", BVB[:], bv0, sLG)
            p.wait("ve", sLG, tbv)
        if hp == 0:
            attn_issue_loads(layer, 0)
        par = hp % 2
        tk, tv = attn_state["marks"][hp]
        p.wait("te", sWB0 if par == 0 else sWB1, tk)
        p.wait("te", sWBD if par == 0 else sWBD2, tv)
        base = par * 2048

        for which in ("q", "k"):
            W = WB[:, base:base + 1024] if which == "q" else WB[:, base + 1024:base + 2048]
            OUTR = QR if which == "q" else KR
            OUTP = QP if which == "q" else KP
            p.wait("te", sVE, attn_state["rope_done"])
            for bk in (0, 1):
                if attn_state["bank_free"][bk] > 0:
                    p.wait("te", sVE, attn_state["bank_free"][bk])
            for half in range(2):
                p.wait("te", sSE, attn_state["proj_evac"])
                if attn_state["norm_done"] > 0:
                    p.wait("se", sVE, attn_state["norm_done"])
                for qs_ in range(2):
                    col = half * 1024 + qs_ * QB
                    for c in range(NC8):
                        mm(PBIG[:, qs_ * QB:(qs_ + 1) * QB],
                           W[:, c * 128:(c + 1) * 128],
                           XN[:, c * T + col: c * T + col + QB],
                           start=(c == 0), stop=(c == NC8 - 1),
                           sem=sTE if c == NC8 - 1 else None)
                p.wait("se", sTE, sTE.val)
                if layer == 0:
                    B = BQ0[:, hp:hp + 1] if which == "q" else BK0[:, hp:hp + 1]
                    m = act(F32C[:, half * 1024:(half + 1) * 1024], PBIG[:, 0:1024],
                            AF.Identity, bias=B)
                else:
                    m = act(F32C[:, half * 1024:(half + 1) * 1024], PBIG[:, 0:1024],
                            AF.Identity)
                attn_state["proj_evac"] = m
            p.wait("ve", sSE, attn_state["proj_evac"])
            if plain:
                cp("ve", OUTP[:], F32C[:])
            shuffle(F32B[:], F32C[:])
            tt(F32C[:], F32C[:], COS[:], ALU.mult)
            tt(F32B[:], F32B[:], SIN[:], ALU.mult)
            attn_state["rope_done"] = tt(OUTR[:], F32C[:], F32B[:], ALU.add)

        vbase = par * NHB * 128
        for kc in range(NKC):
            if attn_state["evv"] > 0:
                p.wait("te", sVE, attn_state["evv"])
            for c in range(NC8):
                mm(PV[:, 0:128],
                   XN[:, c * T + kc * 128: c * T + kc * 128 + 128],
                   WBD[:, vbase + c * 128:vbase + (c + 1) * 128],
                   start=(c == 0), stop=(c == NC8 - 1),
                   sem=sTE if c == NC8 - 1 else None)
            p.wait("ve", sTE, sTE.val)
            dst = VT[:].rearrange("p (h k c) -> p h k c", h=2, k=NKC)[:, :, kc, 0:64]
            if layer == 0:
                attn_state["evv"] = tt(
                    dst, PV[:, 0:128].rearrange("p (h c) -> p h c", h=2),
                    BVB[:, hp * 128:(hp + 1) * 128].rearrange("p (h c) -> p h c", h=2),
                    ALU.add)
            else:
                attn_state["evv"] = cp(
                    "ve", dst, PV[:, 0:128].rearrange("p (h c) -> p h c", h=2))

        p.wait("te", sVE, attn_state["evv"])
        attn_state["te_after_vproj"] = sTE.val
        attn_state["wb_free"][par] = sTE.val
        if hp + 1 < 8:
            attn_issue_loads(layer, hp + 1)

        for h01 in range(2):
            h = hp * 2 + h01
            rows = slice(h01 * 64, h01 * 64 + 64)
            for qb in range(NQB):
                bank = qb % 2
                avsl = PBIG[:, bank * QB:(bank + 1) * QB]
                expR = {}
                expP = {}
                NPAIR = NKC // 2
                if attn_state["bank_free"][bank] > 0:
                    p.wait("te", sVE, attn_state["bank_free"][bank])
                if plain and attn_state["den_free"] > 0:
                    p.wait("te", sVE, attn_state["den_free"])
                def av_pair(pr, qb=qb, h01=h01, avsl=avsl, expR=expR, expP=expP):
                    p.wait("te", sSE, (expP if plain else expR)[pr])
                    att_r = ATT if pr % 2 == 0 else ATT2
                    att_p = ATT3 if pr % 2 == 0 else ATT4
                    for j in range(2):
                        kc = 2 * pr + j
                        vt_base = (h01 * NKC + kc) * 65
                        if not plain:
                            mm(avsl[0:65, :],
                               VT[:, vt_base:vt_base + 65],
                               att_r[:, j * QB:(j + 1) * QB],
                               start=(kc == 0), stop=(kc == NKC - 1),
                               sem=sTE if kc == NKC - 1 else None)
                        else:
                            mm(avsl[0:64, :],
                               VT[:, vt_base:vt_base + 64],
                               att_r[:, j * QB:(j + 1) * QB],
                               start=(kc == 0), stop=(kc == NKC - 1))
                            mm(PBIG[0:1, 1024:1024 + QB],
                               VT[:, vt_base + 64:vt_base + 65],
                               att_p[:, j * QB:(j + 1) * QB],
                               start=(kc == 0), stop=(kc == NKC - 1),
                               sem=sTE if kc == NKC - 1 else None)
                if not plain:
                    PB23 = PBIG[:, 1024:2048]
                    for pr in range(NPAIR):
                        k0 = 2 * pr
                        psl = PSC[:, :] if pr % 2 == 0 else PB23
                        if pr >= 2:
                            p.wait("te", sSE, expR[pr - 2])
                        mm(psl[:, 0:QB], KR[rows, k0 * 128:k0 * 128 + 128],
                           QR[rows, qb * QB:qb * QB + QB], True, True)
                        mm(psl[:, QB:2 * QB],
                           KR[rows, k0 * 128 + 128:k0 * 128 + 256],
                           QR[rows, qb * QB:qb * QB + QB], True, True, sem=sTE)
                        p.wait("se", sTE, sTE.val)
                        def expf(eng, psl=psl, dst=(ATT if pr % 2 == 0 else ATT2)):
                            return eng.activation(out=dst[:, :], in_=psl[:, :],
                                                  func=AF.Exp, scale=scale)
                        expR[pr] = p.op("se", expf)
                        if pr >= 1:
                            av_pair(pr - 1)
                    av_pair(NPAIR - 1)
                    p.wait("ve", sTE, sTE.val)
                    evm = cp("ve", F32B[0:64, qb * QB:(qb + 1) * QB], avsl[0:64, :])
                    evm = cp("ve", STG2[0:1, qb * QB:(qb + 1) * QB], avsl[64:65, :])
                    attn_state["bank_free"][bank] = evm
                    pkm_feed(4)
                    continue
                for pr in range(NPAIR):
                    k0 = 2 * pr
                    if pr >= 1:
                        p.wait("te", sSE, expR[pr - 1])
                    mm(PSC[:, 0:QB], KR[rows, k0 * 128:k0 * 128 + 128],
                       QR[rows, qb * QB:qb * QB + QB], True, True)
                    mm(PSC[:, QB:2 * QB],
                       KR[rows, k0 * 128 + 128:k0 * 128 + 256],
                       QR[rows, qb * QB:qb * QB + QB], True, True, sem=sTE)
                    p.wait("se", sTE, sTE.val)
                    def expf(eng, dst=(ATT if pr % 2 == 0 else ATT2)):
                        return eng.activation(out=dst[:, :], in_=PSC[:],
                                              func=AF.Exp, scale=scale)
                    expR[pr] = p.op("se", expf)
                    if pr >= 1:
                        p.wait("te", sSE, expP[pr - 1])
                    mm(PST[:, 0:QB], KP[rows, k0 * 128:k0 * 128 + 128],
                       QP[rows, qb * QB:qb * QB + QB], True, True)
                    mm(PV[:, 0:QB],
                       KP[rows, k0 * 128 + 128:k0 * 128 + 256],
                       QP[rows, qb * QB:qb * QB + QB], True, True, sem=sTE)
                    p.wait("se", sTE, sTE.val)
                    attp = ATT3 if pr % 2 == 0 else ATT4
                    def expf2a(eng, dst=attp):
                        return eng.activation(out=dst[:, 0:QB], in_=PST[:, :],
                                              func=AF.Exp, scale=scale)
                    p.op("se", expf2a)
                    def expf2b(eng, dst=attp):
                        return eng.activation(out=dst[:, QB:2 * QB], in_=PV[:, :],
                                              func=AF.Exp, scale=scale)
                    expP[pr] = p.op("se", expf2b)
                    if pr >= 1:
                        av_pair(pr - 1)
                av_pair(NPAIR - 1)
                p.wait("ve", sTE, sTE.val)
                evm = cp("ve", F32B[0:64, qb * QB:(qb + 1) * QB], avsl[0:64, :])
                if plain:
                    attn_state["den_free"] = cp(
                        "ve", STG2[0:1, qb * QB:(qb + 1) * QB], PBIG[0:1, 1024:1024 + QB])
                else:
                    evm = cp("ve", STG2[0:1, qb * QB:(qb + 1) * QB], avsl[64:65, :])
                attn_state["bank_free"][bank] = evm if not plain else max(
                    evm, attn_state["den_free"]) if False else evm
                pkm_feed(4)
            rm = recip(STG2[0:1, :], STG2[0:1, :])
            p.wait("gp", sVE, rm)
            pb = pbcast(F32C[:], STG2[0:1, :])
            p.wait("ve", sGP, pb)
            nm = tt(F32B[0:64, :], F32B[0:64, :], F32C[0:64, :], ALU.mult)
            attn_state["norm_done"] = nm
            if layer == 0:
                ychunk = h // 2
                yrows = slice((h % 2) * 64, (h % 2) * 64 + 64)
                cp("ve", BIG[yrows, ychunk * T:(ychunk + 1) * T], F32B[0:64, :])
            else:
                p.wait("gp", sVE, nm)
                p.dma("gp", yscr[h], F32B[0:64, :], sXS)
                p.wait("ve", sXS, sXS.val)

    def emit_wo_update0(seq):
        for ob in range(NC8):
            p.wait("gp", sTE, sTE.val)
            tw = p.dma("gp", WB[:, 0:1024], wo0[ob], sWB0)
            p.wait("te", sWB0, tw)
            for qb in range(NQB):
                if not (ob == 0 and qb == 0):
                    p.wait("sy", sVE, sVE.val)
                else:
                    p.wait("sy", sVE, sVE.val)
                    p.wait("sy", sSE, sSE.val)
                    p.wait("sy", sTE, sTE.val)
                t0 = p.dma("sy", STG[:, 0:QB],
                           x_in[seq, ob][:, qb * QB:(qb + 1) * QB], sST1)
                p.wait("te", sSE, sSE.val)
                p.wait("te", sVE, sVE.val)
                for c in range(NC8):
                    mm(PV[:], WB[:, c * 128:(c + 1) * 128],
                       BIG[:, c * T + qb * QB: c * T + qb * QB + QB],
                       start=(c == 0), stop=(c == NC8 - 1),
                       sem=sTE if c == NC8 - 1 else None)
                p.wait("se", sTE, sTE.val)
                act(F32C[:, 0:QB], PV[:], AF.Identity, bias=BO0[:, ob:ob + 1])
                p.wait("ve", sSE, sSE.val)
                p.wait("ve", sST1, t0)
                p.wait("ve", sTE, sTE.val)
                p.wait("ve", sXS, sXS.val)
                tt(F32A[:, 0:QB], F32C[:, 0:QB],
                   GB[:, ob * T + qb * QB: ob * T + qb * QB + QB], ALU.mult)
                tt(F32A[:, 0:QB], F32A[:, 0:QB], STG[:, 0:QB], ALU.add)
                p.wait("gp", sVE, sVE.val)
                p.dma("gp", x1scr[seq, ob][:, qb * QB:(qb + 1) * QB],
                      F32A[:, 0:QB], sXS)
                p.wait("se", sVE, sVE.val)
                act(F32B[:, 0:QB], F32A[:, 0:QB], AF.Square)
                p.wait("te", sSE, sSE.val)
                emit_stats_mms(F32A[:, 0:QB], ob, qb)
                emit_stats_sq_mms(F32B[:, 0:QB], ob, qb, mark=True)
        attn_state["wb_free"][0] = attn_state["wb_free"][1] = sTE.val

    def emit_pkm_hproj(seq, dbg_on=False):
        p.wait("gp", sVE, sVE.val)
        p.wait("gp", sTE, sTE.val)
        tid = p.dma("gp", BVB[:, 0:128], ident_in, sLG)
        p.wait("te", sLG, tid)
        for hd in range(PKM_H):
            par = hd % 2
            p.wait("gp", sTE, sTE.val)
            tw = p.dma("gp", WB[:, par * 1024:(par + 1) * 1024], wi1[hd],
                       sWB0 if par == 0 else sWB1)
            p.wait("te", sWB0 if par == 0 else sWB1, tw)
            for qb in range(NQB):
                p.wait("te", sSE, sSE.val)
                for c in range(NC8):
                    mm(PV[:], WB[:, par * 1024 + c * 128:par * 1024 + (c + 1) * 128],
                       XN[:, c * T + qb * QB: c * T + qb * QB + QB],
                       start=(c == 0), stop=(c == NC8 - 1),
                       sem=sTE if c == NC8 - 1 else None)
                p.wait("se", sTE, sTE.val)
                act(BIG[:, hd * T + qb * QB: hd * T + qb * QB + QB], PV[:],
                    AF.Identity)
        attn_state["wb_free"][0] = attn_state["wb_free"][1] = sTE.val
        if dbg_on:
            dbg_dump("hpkm", BIG[:])

    def emit_pkm_ugate(seq):
        p.wait("te", sSE, sSE.val)
        for tb in range(NTB):
            for hd in range(PKM_H):
                mm(PST[:, hd:hd + 1],
                   BIG[:, hd * T + tb * 128: hd * T + tb * 128 + 128],
                   USH[:], start=True, stop=True,
                   sem=sTE if hd == PKM_H - 1 else None)
            p.wait("se", sTE, sTE.val)
            act(ACTT[:, tb * 8:(tb + 1) * 8], PST[:, 0:8], AF.Sigmoid)
            p.wait("te", sSE, sSE.val)

    pkm_state = {"keys_free": {0: 0, 1: 0}, "slot": 0, "evac": {0: 0, 1: 0},
                 "topk": {0: 0, 1: 0}, "gt_comb": {0: 0, 1: 0}, "comb": {},
                 "ig_mark": {}, "ig_done_tb": {0: None, 1: None},
                 "tp_mark": {0: 0, 1: 0}, "pending": [], "combq": []}

    def emit_pkm_tb(seq, tb, dbg_on=False):
        for hd in range(PKM_H):
            ksl = hd % 2
            if pkm_state["keys_free"][ksl] > 0:
                p.wait("sy", sTE, pkm_state["keys_free"][ksl])
            ka = KA[:, ksl * PKM_S:(ksl + 1) * PKM_S]
            kb = KB_[:, ksl * PKM_S:(ksl + 1) * PKM_S]
            semA = sKA if ksl == 0 else sKA2
            semB = sKB if ksl == 0 else sKB2
            tk1 = p.dma("sy", ka, keysa_bf[hd], semA)
            tk2 = p.dma("sy", kb, keysb_bf[hd], semB)
            p.wait("te", semA, tk1)
            p.wait("te", semB, tk2)
            hsl = BIG[:, hd * T + tb * 128: hd * T + tb * 128 + 128]
            for ab in range(2):
                K = ka if ab == 0 else kb
                sl = pkm_state["slot"]
                pkm_state["slot"] ^= 1
                scf_sl = SCF[:, sl * PKM_S:(sl + 1) * PKM_S]
                if pkm_state["evac"][sl ^ 1] > 0:
                    p.wait("te", sSE, pkm_state["evac"][sl ^ 1])
                if pkm_state["topk"][sl] > 0:
                    p.wait("se", sVE, pkm_state["topk"][sl])
                sc_mark = mm(PBIG[:, 1536:2048], hsl, K, start=True, stop=True,
                             sem=sTE)
                if ab == 1:
                    pkm_state["keys_free"][ksl] = sc_mark
                p.wait("se", sTE, sc_mark)
                pkm_state["evac"][sl] = act(scf_sl, PBIG[:, 1536:2048], AF.Identity)
                p.wait("ve", sSE, pkm_state["evac"][sl])
                TS_ = TAS if ab == 0 else TBS
                TI_ = TAI if ab == 0 else TBI
                vmax8(TS_[:, hd * 16:hd * 16 + 8], scf_sl)
                vmaxidx(TI_[:, hd * 16:hd * 16 + 8],
                        TS_[:, hd * 16:hd * 16 + 8], scf_sl)
                vmatchrep(SCZ[:], TS_[:, hd * 16:hd * 16 + 8], scf_sl, -3e38)
                vmax8(TS_[:, hd * 16 + 8:hd * 16 + 16], SCZ[:])
                pkm_state["topk"][sl] = vmaxidx(
                    TI_[:, hd * 16 + 8:hd * 16 + 16],
                    TS_[:, hd * 16 + 8:hd * 16 + 16], SCZ[:])
        emit_pkm_stage_c(seq, tb, dbg_on)

    def emit_pkm_wout(seq):
        p.wait("te", sVE, sVE.val)
        wout_marks = {}
        wout_done = {}
        def issue_wout(ob):
            if ob >= 2:
                p.wait("gp", sTE, wout_done[ob - 2])
            else:
                p.wait("gp", sTE, sTE.val)
            t = p.dma("gp", WB[:, (ob % 2) * 1024:(ob % 2) * 1024 + 1024],
                      wout1[ob], sWB0 if ob % 2 == 0 else sWB1)
            wout_marks[ob] = t
        issue_wout(0)
        issue_wout(1)
        for ob in range(NC8):
            p.wait("te", sWB0 if ob % 2 == 0 else sWB1, wout_marks[ob])
            for qb in range(NQB):
                p.wait("te", sSE, sSE.val)
                for c in range(NC8):
                    mm(PV[:], WB[:, (ob % 2) * 1024 + c * 128:(ob % 2) * 1024 + (c + 1) * 128],
                       BIG[:, c * T + qb * QB: c * T + qb * QB + QB],
                       start=(c == 0), stop=(c == NC8 - 1),
                       sem=sTE if c == NC8 - 1 else None)
                p.wait("se", sTE, sTE.val)
                act(GB[:, ob * T + qb * QB: ob * T + qb * QB + QB], PV[:],
                    AF.Identity)
            wout_done[ob] = sTE.val
            if ob + 2 < NC8:
                issue_wout(ob + 2)
        attn_state["wb_free"][0] = attn_state["wb_free"][1] = sTE.val

    def emit_pkm_stage_c(seq, tb, dbg_on=False):
        eids_t = EIDS if tb % 2 == 0 else EIDSb
        w16_t = W16 if tb % 2 == 0 else W16b
        prev_ig = pkm_state["ig_done_tb"][tb % 2]
        if prev_ig is not None:
            p.wait("ve", sIG, prev_ig[0])
            p.wait("ve", sIG2, prev_ig[1])
        cp("ve", TAIF[:], TAI[:])
        cp("ve", TBIF[:], TBI[:])
        tt(CAND[:].rearrange("p (h i j) -> p h i j", h=8, i=16),
           TAS[:].rearrange("p (h i) -> p h i", h=8).unsqueeze(3).to_broadcast(
               [128, 8, 16, 16]),
           TBS[:].rearrange("p (h j) -> p h j", h=8).unsqueeze(2).to_broadcast(
               [128, 8, 16, 16]),
           ALU.add)
        for hd in range(PKM_H):
            csl = CAND[:, hd * 256:(hd + 1) * 256]
            vmax8(TS16[:, hd * 16:hd * 16 + 8], csl)
            vmaxidx(POS[:, hd * 16:hd * 16 + 8], TS16[:, hd * 16:hd * 16 + 8], csl)
            vmatchrep(CZ[:], TS16[:, hd * 16:hd * 16 + 8], csl, -3e38)
            vmax8(TS16[:, hd * 16 + 8:hd * 16 + 16], CZ[:])
            vmaxidx(POS[:, hd * 16 + 8:hd * 16 + 16],
                    TS16[:, hd * 16 + 8:hd * 16 + 16], CZ[:])
        ts(IJ2[:], POS[:], 4, None, ALU.logical_shift_right)
        cp("ve", IJF[:], IJ2[:])
        tt(OH[:].rearrange("p (h m i) -> p h m i", h=8, m=16),
           IJF[:].rearrange("p (h m) -> p h m", h=8).unsqueeze(3).to_broadcast(
               [128, 8, 16, 16]),
           IOTA16[:].unsqueeze(1).unsqueeze(1).to_broadcast([128, 8, 16, 16]),
           ALU.is_equal)
        tt(OH[:].rearrange("p (h m i) -> p h m i", h=8, m=16),
           OH[:].rearrange("p (h m i) -> p h m i", h=8, m=16),
           TAIF[:].rearrange("p (h i) -> p h i", h=8).unsqueeze(2).to_broadcast(
               [128, 8, 16, 16]),
           ALU.mult)
        reduce_sum_x(SLOTA[:].rearrange("p (h m) -> p h m", h=8),
                     OH[:].rearrange("p (h m i) -> p h m i", h=8, m=16))
        ts(EIDF[:], SLOTA[:], 512.0, None, ALU.mult)
        ts(IJ2[:], POS[:], 15, None, ALU.bitwise_and)
        cp("ve", IJF[:], IJ2[:])
        tt(OH[:].rearrange("p (h m i) -> p h m i", h=8, m=16),
           IJF[:].rearrange("p (h m) -> p h m", h=8).unsqueeze(3).to_broadcast(
               [128, 8, 16, 16]),
           IOTA16[:].unsqueeze(1).unsqueeze(1).to_broadcast([128, 8, 16, 16]),
           ALU.is_equal)
        tt(OH[:].rearrange("p (h m i) -> p h m i", h=8, m=16),
           OH[:].rearrange("p (h m i) -> p h m i", h=8, m=16),
           TBIF[:].rearrange("p (h i) -> p h i", h=8).unsqueeze(2).to_broadcast(
               [128, 8, 16, 16]),
           ALU.mult)
        reduce_sum_x(SLOTA[:].rearrange("p (h m) -> p h m", h=8),
                     OH[:].rearrange("p (h m i) -> p h m i", h=8, m=16))
        tt(EIDF[:], EIDF[:], SLOTA[:], ALU.add)
        cp("ve", eids_t[:, :], EIDF[:])
        negm = ts(WSUM[:], TS16[:].rearrange("p (h m) -> p h m", h=8)[:, :, 0:1]
                  .rearrange("p h m -> p (h m)"), -1.0, None, ALU.mult)
        p.wait("se", sVE, negm)
        em = 0
        for hd in range(PKM_H):
            em = act(w16_t[:, hd * 16:(hd + 1) * 16], TS16[:, hd * 16:(hd + 1) * 16],
                     AF.Exp, bias=WSUM[:, hd:hd + 1])
        p.wait("ve", sSE, em)
        reduce_sum_x(WSUM[:].unsqueeze(2), w16_t[:, :].rearrange("p (h m) -> p h m", h=8))
        recip(WSUM[:], WSUM[:])
        tt(w16_t[:, :].rearrange("p (h m) -> p h m", h=8),
           w16_t[:, :].rearrange("p (h m) -> p h m", h=8),
           WSUM[:].unsqueeze(2).to_broadcast([128, 8, 16]), ALU.mult)
        wready = tt(w16_t[:, :].rearrange("p (h m) -> p h m", h=8),
                    w16_t[:, :].rearrange("p (h m) -> p h m", h=8),
                    ACTT[:, tb * 8:(tb + 1) * 8].unsqueeze(2).to_broadcast([128, 8, 16]),
                    ALU.mult)
        if dbg_on and tb == 0:
            dbg_dump("eids", eids_t[:, :])
            dbg_dump("w16", w16_t[:, :])
        for hd in range(PKM_H):
            def iss(hd=hd, tb=tb, wr=wready, eids_t=eids_t):
                gts = GT if hd % 2 == 0 else GT2
                igsem = sIG if hd % 2 == 0 else sIG2
                if hd == 0:
                    p.wait("gp", sVE, wr)
                if pkm_state["gt_comb"][hd % 2] > 0:
                    p.wait("gp", sVE, pkm_state["gt_comb"][hd % 2])
                for m in range(16):
                    p.indirect(gts[:, m * 128:(m + 1) * 128], expv,
                               eids_t[:, hd * 16 + m:hd * 16 + m + 1], igsem)
                pkm_state["ig_mark"][(tb % 2, hd)] = igsem.val
                if hd == PKM_H - 1:
                    pkm_state["ig_done_tb"][tb % 2] = (sIG.val, sIG2.val)

            def comb(hd=hd, tb=tb, w16_t=w16_t):
                gts = GT if hd % 2 == 0 else GT2
                igsem = sIG if hd % 2 == 0 else sIG2
                mgt = MGT if tb % 2 == 0 else MGT2
                if hd == 0 and pkm_state["tp_mark"][tb % 2] > 0:
                    p.wait("ve", sTE, pkm_state["tp_mark"][tb % 2])
                p.wait("ve", igsem, pkm_state["ig_mark"][(tb % 2, hd)])
                ts(mgt[:, hd * 128:(hd + 1) * 128], gts[:, 0:128],
                   w16_t[:, hd * 16:hd * 16 + 1], None, ALU.mult)
                cm = 0
                for m in range(1, 16):
                    cm = stt(mgt[:, hd * 128:(hd + 1) * 128],
                             gts[:, m * 128:(m + 1) * 128],
                             w16_t[:, hd * 16 + m:hd * 16 + m + 1],
                             mgt[:, hd * 128:(hd + 1) * 128], ALU.mult, ALU.add)
                pkm_state["gt_comb"][hd % 2] = cm
                pkm_state["comb"][tb] = cm

            pend = pkm_state["pending"]
            pend.append(iss)
            pkm_state["combq"].append(comb)
            if len(pkm_state["combq"]) > 1:
                pend.append(pkm_state["combq"].pop(0))
        if tb % 2 == 1:
            pend = pkm_state["pending"]
            while pkm_state["combq"]:
                pend.append(pkm_state["combq"].pop(0))

    def pkm_feed(n=1):
        for _ in range(n):
            if pkm_state["pending"]:
                pkm_state["pending"].pop(0)()

    def pkm_flush():
        while pkm_state["pending"]:
            pkm_state["pending"].pop(0)()

    def emit_pkm_transposes(seq, tb):
        mgt = MGT if tb % 2 == 0 else MGT2
        p.wait("te", sVE, pkm_state["comb"][tb])
        p.wait("te", sSE, sSE.val)
        tp_evac = pkm_state.setdefault("tp_evac", {0: 0, 1: 0})
        tpm = 0
        for hd in range(PKM_H):
            psl = PST[:, 0:128] if hd % 2 == 0 else PV[:, 0:128]
            if tp_evac[hd % 2] > 0:
                p.wait("te", sVE, tp_evac[hd % 2])
            tpm = transpose_te(psl, mgt[:, hd * 128:(hd + 1) * 128], sem=sTE)
            p.wait("ve", sTE, tpm)
            tp_evac[hd % 2] = cp(
                "ve", BIG[:, hd * T + tb * 128: hd * T + tb * 128 + 128], psl)
        pkm_state["tp_mark"][tb % 2] = tpm
        attn_state["evv"] = tp_evac[1]

    def emit_pkm_lnsilu(seq):
        marks = {}
        p.wait("te", sSE, sSE.val)
        for c in range(NC8):
            p.wait("se", sVE, sVE.val)
            if c >= 1:
                p.wait("se", sTE, marks[c - 1])
            act(F32C[:], GB[:, c * T:(c + 1) * T], AF.Square)
            emit_stats_mms(GB[:, c * T:(c + 1) * T], c, mark=False, ones=ONESB)
            p.wait("te", sSE, sSE.val)
            marks[c] = emit_stats_sq_mms(F32C, c, mark=True)
        emit_ln_stats_finish()
        p.wait("ve", sGP, sGP.val)
        for c in range(NC8):
            tt(F32C[:], GB[:, c * T:(c + 1) * T], F32A[:], ALU.mult)
            tt(F32C[:], F32C[:], F32B[:], ALU.add)
            ts(F32C[:], F32C[:], LNW1[:, c:c + 1], LNB1[:, c:c + 1],
               ALU.mult, ALU.add)
            p.wait("se", sVE, sVE.val)
            act(GB[:, c * T:(c + 1) * T], F32C[:], AF.Silu)
            p.wait("ve", sSE, sSE.val)

    def emit_wo1_ln(seq):
        YB = BIG[:, 0:4096]
        for qb in range(NQB):
            p.wait("sy", sTE, sTE.val)
            ymark = 0
            for c in range(NC8):
                p.dma("sy", YB[0:64, c * QB:(c + 1) * QB],
                      yscr[2 * c, :, qb * QB:(qb + 1) * QB], sLS)
                ymark = p.dma("sy", YB[64:128, c * QB:(c + 1) * QB],
                              yscr[2 * c + 1, :, qb * QB:(qb + 1) * QB], sLS)
            wmarks = {}
            for ob in range(NC8):
                par = ob % 2
                if ob == 0 and qb == 0:
                    p.wait("gp", sTE, sTE.val)
                    wmarks[0] = p.dma("gp", WB[:, 0:1024], wo1[0], sWB0)
                    p.wait("gp", sTE, sTE.val)
                    wmarks[1] = p.dma("gp", WB[:, 1024:2048], wo1[1], sWB1)
                elif ob == 0:
                    p.wait("gp", sTE, sTE.val)
                    wmarks[0] = p.dma("gp", WB[:, 0:1024], wo1[0], sWB0)
                    wmarks[1] = p.dma("gp", WB[:, 1024:2048], wo1[1], sWB1)
                p.wait("te", sWB0 if par == 0 else sWB1, wmarks[ob])
                p.wait("te", sLS, ymark)
                p.wait("te", sSE, sSE.val)
                for c in range(NC8):
                    mm(PV[:], WB[:, par * 1024 + c * 128:par * 1024 + (c + 1) * 128],
                       YB[:, c * QB:(c + 1) * QB],
                       start=(c == 0), stop=(c == NC8 - 1),
                       sem=sTE if c == NC8 - 1 else None)
                if ob + 2 < NC8:
                    p.wait("gp", sTE, sTE.val)
                    wmarks[ob + 2] = p.dma(
                        "gp", WB[:, par * 1024:par * 1024 + 1024], wo1[ob + 2],
                        sWB0 if par == 0 else sWB1)
                p.wait("se", sTE, sTE.val)
                p.wait("se", sVE, sVE.val)
                act(F32A[:, 0:QB], PV[:], AF.Identity)
                p.wait("ve", sSE, sSE.val)
                p.wait("ve", sTE, sTE.val)
                tt(F32A[:, 0:QB], F32A[:, 0:QB],
                   XN[:, ob * T + qb * QB: ob * T + qb * QB + QB], ALU.add)
                p.wait("se", sVE, sVE.val)
                act(F32B[:, 0:QB], F32A[:, 0:QB], AF.Square)
                p.wait("te", sVE, sVE.val)
                p.wait("te", sSE, sSE.val)
                emit_stats_mms(F32A[:, 0:QB], ob, qb)
                emit_stats_sq_mms(F32B[:, 0:QB], ob, qb, mark=True)
                cp("ve", XN[:, ob * T + qb * QB: ob * T + qb * QB + QB],
                   F32A[:, 0:QB])
        attn_state["wb_free"][0] = attn_state["wb_free"][1] = sTE.val
        emit_ln_stats_finish()
        p.wait("ve", sGP, sGP.val)
        for c in range(NC8):
            tt(F32C[:], XN[:, c * T:(c + 1) * T], F32A[:], ALU.mult)
            tt(F32C[:], F32C[:], F32B[:], ALU.add)
            ts(BIG[:, c * T:(c + 1) * T], F32C[:], G1V[:, c:c + 1], B1V[:, c:c + 1],
               ALU.mult, ALU.add)

    def emit_final_update(seq):
        p.wait("sy", sXS, sXS.val)
        p.wait("sy", sSE, sSE.val)
        p.wait("sy", sVE, sVE.val)
        p.wait("sy", sTE, sTE.val)
        for c in range(NC8):
            if c > 0:
                p.wait("sy", sVE, sVE.val)
            t0 = p.dma("sy", STG[:], x1scr[seq, c], sST1)
            p.wait("ve", sST1, t0)
            tt(F32C[:], BIG[:, c * T:(c + 1) * T], GB[:, c * T:(c + 1) * T],
               ALU.mult)
            tt(F32C[:], F32C[:], STG[:], ALU.add)
            p.wait("sy", sVE, sVE.val)
            p.dma("sy", out_hbm[seq, c], F32C[:], sSTS)
            p.wait("ve", sSTS, sSTS.val)

    for seq in range(NSEQ):
        if seq == 0:
            p.wait("te", sLS, sLS.val)
            p.wait("ve", sLS, sLS.val)
            p.wait("se", sLS, sLS.val)
        emit_x_stats(seq, x_in)
        tc = p.dma("sy", COS[:], cos0, sLS)
        tsn = p.dma("sy", SIN[:], sin0, sLS)
        emit_ln_stats_finish()
        emit_xn_apply(seq, x_in)
        if "xn" in dbg and seq == 0:
            dbg_dump("xn", XN[:])
        emit_ffn0(seq)
        if "g" in dbg and seq == 0:
            dbg_dump("g", GB[:])
        p.wait("ve", sLS, tsn)
        for hp in range(8):
            emit_attn_hp(seq, 0, hp)
        if "y" in dbg and seq == 0:
            dbg_dump("y", BIG[:])
        emit_wo_update0(seq)
        emit_ln_stats_finish()
        p.wait("sy", sXS, sXS.val)
        emit_xn_apply(seq, x1scr)
        tc = p.dma("sy", COS[:], cos1, sLS)
        tsn = p.dma("sy", SIN[:], sin1, sLS)
        p.wait("ve", sLS, tsn)
        emit_pkm_hproj(seq, dbg_on=("hpkm" in dbg and seq == 0))
        emit_pkm_ugate(seq)
        p.wait("sy", sLG, keys_cast_mark)
        for hp in range(8):
            if hp >= 2:
                emit_pkm_transposes(seq, 2 * (hp - 2))
                emit_pkm_transposes(seq, 2 * (hp - 2) + 1)
            emit_attn_hp(seq, 1, hp)
            emit_pkm_tb(seq, 2 * hp, dbg_on=(seq == 0))
            emit_pkm_tb(seq, 2 * hp + 1, dbg_on=(seq == 0))
        emit_pkm_transposes(seq, 12)
        emit_pkm_transposes(seq, 13)
        pkm_flush()
        emit_pkm_transposes(seq, 14)
        emit_pkm_transposes(seq, 15)
        emit_pkm_wout(seq)
        emit_pkm_lnsilu(seq)
        emit_wo1_ln(seq)
        if "s1" in dbg and seq == 0:
            dbg_dump("s1", BIG[:])
        emit_final_update(seq)

    p.wait("gp", sSTS, sSTS.val)
    p.wait("gp", sIG, sIG.val)
    p.wait("gp", sIG2, sIG2.val)

    for s in p.sems.values():
        s.handle = ctx.enter_context(nc.semaphore(s.name))
    p.run(nc)
    ctx.close()
    return nc, dbg



def tile_proj(W, nob=None):
    outdim, indim = W.shape
    nob = outdim // 128 if nob is None else nob
    nic = indim // 128
    wt = W.T.reshape(nic, 128, outdim)
    out = np.empty((nob, 128, nic * 128), W.dtype)
    for ob in range(nob):
        sl = wt[:, :, ob * 128:(ob + 1) * 128]
        out[ob] = sl.transpose(1, 0, 2).reshape(128, nic * 128)
    return np.ascontiguousarray(out)


def rope_tables_l0():
    HD = 64
    inv = 1.0 / 10000.0 ** (np.arange(0, HD, 2, dtype=np.float32) / HD)
    t = np.arange(T, dtype=np.float32)
    emb = np.concatenate([np.outer(t, inv)] * 2, axis=-1)
    cos = np.cos(emb); sin = np.sin(emb)
    cos_t = np.zeros((128, T), np.float32)
    sin_t = np.zeros((128, T), np.float32)
    for r in range(128):
        d = r % 64
        cos_t[r] = cos[:, d]
        sin_t[r] = sin[:, d] * (-1.0 if d % 2 == 0 else 1.0)
    return cos_t, sin_t


def rope_tables_l1():
    HD = 64
    i = np.arange(1, HD // 2 + 1, dtype=np.float32)
    thetas = np.repeat(1000.0 ** (-2.0 * i / HD), 2)
    pos = np.arange(1, T + 1, dtype=np.float32)
    args = pos[:, None] * thetas[None, :]
    sin = np.sin(args); cos = np.cos(args)
    cos_t = np.zeros((128, T), np.float32)
    sin_t = np.zeros((128, T), np.float32)
    for r in range(128):
        d = r % 64
        cos_t[r] = cos[:, d]
        sin_t[r] = sin[:, d] * (-1.0 if d % 2 == 0 else 1.0)
    return cos_t, sin_t


def pack_inputs(core, inp):
    d = {}
    x = inp["L0_attn_x"]
    xs = x[2 * core:2 * core + 2]
    d["x_in"] = np.ascontiguousarray(xs.transpose(0, 2, 1)).reshape(NSEQ, NC8, 128, T)

    d["wq0"] = tile_proj(inp["L0_attn_Wq"])
    d["wk0"] = tile_proj(inp["L0_attn_Wk"])
    d["wv0"] = tile_proj(inp["L0_attn_Wv"])
    d["wo0"] = tile_proj(inp["L0_attn_Wo"])
    d["bq0"] = np.ascontiguousarray(inp["L0_attn_bq"].reshape(8, 128).T)
    d["bk0"] = np.ascontiguousarray(inp["L0_attn_bk"].reshape(8, 128).T)
    d["bv0"] = np.ascontiguousarray(
        np.broadcast_to(inp["L0_attn_bv"][None, :], (128, D)))
    vecs = np.zeros((128, 64), np.float32)
    vecs[:, 0:8] = inp["L0_attn_bq"].reshape(8, 128).T
    vecs[:, 8:16] = inp["L0_attn_bk"].reshape(8, 128).T
    vecs[:, 16:24] = inp["L0_attn_bo"].reshape(8, 128).T
    vecs[:, 24:32] = inp["L1_attn_ln_gamma"].reshape(8, 128).T
    vecs[:, 32:40] = inp["L1_attn_ln_beta"].reshape(8, 128).T
    vecs[:, 40:48] = inp["L1_ffn_ln_w"].reshape(8, 128).T
    vecs[:, 48:56] = inp["L1_ffn_ln_b"].reshape(8, 128).T
    d["vecs_in"] = vecs
    d["bo0"] = vecs[:, 16:24].copy()
    import ml_dtypes
    c0, s0 = rope_tables_l0()
    d["cos0"], d["sin0"] = c0.astype(ml_dtypes.bfloat16), s0.astype(ml_dtypes.bfloat16)
    c1, s1 = rope_tables_l1()
    d["cos1"], d["sin1"] = c1.astype(ml_dtypes.bfloat16), s1.astype(ml_dtypes.bfloat16)

    d["wg0"] = tile_proj(inp["L0_ffn_w_gate"])
    d["wu0"] = tile_proj(inp["L0_ffn_w_up"])
    d["wd0"] = tile_proj(inp["L0_ffn_w_down"])

    w_in = inp["L1_attn_w_in"]
    d["wq1"] = tile_proj(w_in[0:D])
    d["wk1"] = tile_proj(w_in[D:2 * D])
    d["wv1"] = tile_proj(w_in[2 * D:3 * D])
    d["wo1"] = tile_proj(inp["L1_attn_w_out"])

    d["wi1"] = tile_proj(inp["L1_ffn_W_in"])
    d["keysa"] = np.ascontiguousarray(inp["L1_ffn_keys_a"].transpose(0, 2, 1))
    d["keysb"] = np.ascontiguousarray(inp["L1_ffn_keys_b"].transpose(0, 2, 1))
    d["ush"] = np.ascontiguousarray(inp["L1_ffn_u_shared"].reshape(128, 1))
    d["expv"] = inp["L1_ffn_expert_v"]
    d["wout1"] = tile_proj(inp["L1_ffn_W_out"])
    d["ones_c"] = np.ones((128, 32), np.float32)
    d["iota16"] = np.broadcast_to(np.arange(16, dtype=np.float32)[None, :],
                                  (128, 16)).copy()
    d["ident_in"] = np.eye(128, dtype=np.float32)
    return d


_CACHED = {}


def kernel(**inputs):
    if "nc" not in _CACHED:
        nc, _ = build_program()
        nc.finalize()
        _CACHED["nc"] = nc
    nc = _CACHED["nc"]
    in_maps = [pack_inputs(c, inputs) for c in range(8)]
    res = run_bass_kernel_spmd(nc, in_maps, core_ids=list(range(8)))
    out = np.empty((16, T, D), np.float32)
    for c in range(8):
        o = res.results[c]["out"]
        o = o.reshape(NSEQ, D, T).transpose(0, 2, 1)
        out[2 * c:2 * c + 2] = o
    return out


# version 13
# speedup vs baseline: 1.8485x; 1.0225x over previous
import numpy as np
import concourse.bacc as bacc
import concourse.bass as bass
import concourse.mybir as mybir
from concourse.bass_utils import run_bass_kernel_spmd

F32 = mybir.dt.float32
BF16 = mybir.dt.bfloat16
U32 = mybir.dt.uint32
U16 = mybir.dt.uint16
I32 = mybir.dt.int32
AF = mybir.ActivationFunctionType
ALU = mybir.AluOpType

T = 2048
D = 1024
NC8 = 8
QB = 512
NQB = T // QB
NKC = T // 128
NSEQ = 2
NHB = 24
PKM_H = 8
PKM_S = 512
EPS = 1e-5
NTB = T // 128


class Sem:
    def __init__(self, name):
        self.name = name
        self.val = 0
        self.handle = None

    def inc(self, n=1):
        self.val += n
        return self.val


class Plan:
    AUTO_SELF = ("ve", "se", "gp")

    def __init__(self):
        self.ops = []
        self.sems = {}
        self.auto = {e: self.sem(e) for e in self.AUTO_SELF}

    def sem(self, name):
        s = self.sems.get(name)
        if s is None:
            s = Sem(name)
            self.sems[name] = s
        return s

    def emit(self, engine, fn):
        self.ops.append((engine, fn))

    def op(self, engine, build, sem=None, n=1):
        auto = self.auto.get(engine)
        prev = auto.val if auto is not None else 0
        if auto is not None:
            auto.inc(1)
            if sem is auto:
                sem = None
        if sem is not None:
            sem.inc(n)

        def fn(eng, build=build, sem=sem, n=n, auto=auto, prev=prev):
            if auto is not None and prev > 0:
                eng.wait_ge(auto.handle, prev)
            ins = build(eng)
            assert ins is not None, "builder must return its instruction"
            if auto is not None:
                ins.then_inc(auto.handle, 1)
            if sem is not None:
                ins.then_inc(sem.handle, n)
        self.emit(engine, fn)
        return sem.val if sem is not None else (auto.val if auto else None)

    def wait(self, engine, sem, val=None):
        if val is None:
            val = sem.val
        def fn(eng, sem=sem, val=val):
            eng.wait_ge(sem.handle, val)
        self.emit(engine, fn)

    def dma(self, engine, out, in_, sem, slow_ok=False):
        tgt = sem.inc(16)
        def fn(eng, out=out, in_=in_, sem=sem, slow_ok=slow_ok):
            eng.dma_start(out=out, in_=in_,
                          allow_slow_non_contiguous=slow_ok).then_inc(sem.handle, 16)
        self.emit(engine, fn)
        return tgt

    def indirect(self, out, in_, idx_ap, sem):
        tgt = sem.inc(16)
        def fn(eng, out=out, in_=in_, idx_ap=idx_ap, sem=sem):
            eng.indirect_dma_start(
                out=out, out_offset=None, in_=in_,
                in_offset=bass.IndirectOffsetOnAxis(ap=idx_ap, axis=0),
            ).then_inc(sem.handle, 16)
        self.emit("gp", fn)
        return tgt

    def run(self, nc):
        with nc.Block() as block:
            def make(engine_attr, key):
                def body(eng):
                    for e, fn in self.ops:
                        if e == key:
                            fn(eng)
                engine_attr(body)
            make(block.tensor, "te")
            make(block.vector, "ve")
            make(block.scalar, "se")
            make(block.gpsimd, "gp")
            make(block.sync, "sy")


def build_program(debug_names=()):
    nc = bacc.Bacc("TRN2", target_bir_lowering=False)
    p = Plan()
    from contextlib import ExitStack
    ctx = ExitStack()

    dram = lambda name, shape, dt=F32, kind="ExternalInput": nc.dram_tensor(
        name, shape, dt, kind=kind).ap()

    x_in = dram("x_in", [NSEQ, NC8, 128, T])
    out_hbm = dram("out", [NSEQ, NC8, 128, T], kind="ExternalOutput")
    x1scr = dram("x1scr", [NSEQ, NC8, 128, T], kind="Internal")
    yscr = dram("yscr", [16, 64, T], BF16, kind="Internal")

    wq0 = dram("wq0", [8, 128, NC8 * 128])
    wk0 = dram("wk0", [8, 128, NC8 * 128])
    wv0 = dram("wv0", [8, 128, NC8 * 128])
    wo0 = dram("wo0", [NC8, 128, NC8 * 128])
    bq0 = dram("bq0", [128, 8])
    bk0 = dram("bk0", [128, 8])
    bv0 = dram("bv0", [128, NC8 * 128])
    bo0 = dram("bo0", [128, NC8])
    cos0 = dram("cos0", [128, T], BF16)
    sin0 = dram("sin0", [128, T], BF16)
    wg0 = dram("wg0", [NHB, 128, NC8 * 128], I32)
    wu0 = dram("wu0", [NHB, 128, NC8 * 128], I32)
    wd0 = dram("wd0", [NC8, 128, NHB * 128], I32)
    wgs = dram("wgs", [NHB, 128, NC8 * 128], BF16, kind="Internal")
    wus = dram("wus", [NHB, 128, NC8 * 128], BF16, kind="Internal")
    wds = dram("wds", [NC8, 128, NHB * 128], BF16, kind="Internal")
    wq1 = dram("wq1", [8, 128, NC8 * 128])
    wk1 = dram("wk1", [8, 128, NC8 * 128])
    wv1 = dram("wv1", [8, 128, NC8 * 128])
    wo1 = dram("wo1", [NC8, 128, NC8 * 128])
    cos1 = dram("cos1", [128, T], BF16)
    sin1 = dram("sin1", [128, T], BF16)
    wi1 = dram("wi1", [8, 128, NC8 * 128])
    keysa = dram("keysa", [PKM_H, 128, PKM_S])
    keysb = dram("keysb", [PKM_H, 128, PKM_S])
    ush = dram("ush", [128, 1])
    expv = dram("expv", [PKM_S * PKM_S, 128])
    wout1 = dram("wout1", [NC8, 128, NC8 * 128])
    vecs_in = dram("vecs_in", [128, 64])
    ones_c = dram("ones_c", [128, 32])
    iota16_in = dram("iota16", [128, 16])

    dbg_specs = {
        "xn": [128, NC8 * T], "g": [128, NC8 * T], "y": [128, NC8 * T],
        "x1": [NC8, 128, T], "s1": [128, NC8 * T],
        "eids": [128, 8 * 16], "w16": [128, 8 * 16], "mgt": [128, 8 * 128],
        "hpkm": [128, NC8 * T],
    }
    dbg = {k: dram("dbg_" + k, dbg_specs[k], kind="ExternalOutput")
           for k in debug_names}

    sLG = p.sem("ldg")
    sLS = p.sem("lds")
    sST1 = p.sem("st1")
    sST2 = p.sem("st2")
    sWB0 = p.sem("wb0")
    sWB1 = p.sem("wb1")
    sWBD = p.sem("wbd")
    sWBD2 = p.sem("wbd2")
    sKA = p.sem("ka")
    sKB = p.sem("kb")
    sKA2 = p.sem("ka2")
    sKB2 = p.sem("kb2")
    sXS = p.sem("xs")
    sDBG = p.sem("dbg")
    sTE = p.sem("te_")
    sVE = p.auto["ve"]
    sSE = p.auto["se"]
    sGP = p.auto["gp"]
    sSTS = p.sem("sts")
    sIG = p.sem("ig")
    sIG2 = p.sem("ig2")

    sb = lambda name, shape, dt=BF16: ctx.enter_context(nc.sbuf_tensor(name, shape, dt))
    psum = lambda name, shape: ctx.enter_context(nc.psum_tensor(name, shape, F32))

    XN = sb("XN", [128, NC8 * T])
    GB = sb("GB", [128, NC8 * T])
    BIG = sb("BIG", [128, NC8 * T])
    F32A = sb("F32A", [128, T], F32)
    F32B = sb("F32B", [128, T], F32)
    F32C = sb("F32C", [128, T], F32)
    STG = sb("STG", [128, T], F32)
    STG2 = sb("STG2", [128, T], F32)
    QR = sb("QR", [128, T])
    KR = sb("KR", [128, T])
    QP = sb("QP", [128, T])
    KP = sb("KP", [128, T])
    VT = sb("VT", [128, 2 * NKC * 65])
    ATT = sb("ATT", [128, 2 * QB])
    ATT2 = sb("ATT2", [128, 2 * QB])
    SG = sb("SG", [128, QB])
    WB = sb("WB", [128, 4 * NC8 * 128])
    WBD = sb("WBD", [128, 2 * NHB * 128])
    COS = sb("COS", [128, T])
    SIN = sb("SIN", [128, T])
    SMALL = sb("SMALL", [128, 64], F32)
    BVB = sb("BVB", [128, NC8 * 128], F32)
    ONES1 = sb("ONES1", [128, 1], F32)
    ONESB = sb("ONESB", [128, 1])
    IOTA16 = sb("IOTA16", [128, 16], F32)
    KA = sb("KA", [128, 2 * PKM_S])
    KB_ = sb("KB_", [128, 2 * PKM_S])
    USH = sb("USH", [128, 1])
    ACTT = sb("ACTT", [128, NTB * 8], F32)
    SCF = sb("SCF", [128, 2 * PKM_S], F32)
    TAS = sb("TAS", [128, 8 * 16], F32)
    TBS = sb("TBS", [128, 8 * 16], F32)
    TAI = sb("TAI", [128, 8 * 16], U16)
    TBI = sb("TBI", [128, 8 * 16], U16)
    POS = sb("POS", [128, 8 * 16], U16)
    IJ2 = sb("IJ2", [128, 8 * 16], U16)
    EIDS = sb("EIDS", [128, 8 * 16], U32)
    W16 = sb("W16", [128, 8 * 16], F32)
    WSUM = sb("WSUM", [128, 8], F32)
    ident_in = dram("ident_in", [128, 128])

    PBIG = psum("PBIG", [128, 2048])
    PSC = psum("PSC", [128, 1024])
    PV = psum("PV", [128, 512])
    PST = psum("PST", [128, 512])

    CAND = STG[:, :]
    SCZ = GB[:, 0:1024].bitcast(F32)
    OH = GB[:, 0:4096].bitcast(F32)
    ATT3 = GB[:, 0:1024]
    ATT4 = GB[:, 1024:2048]
    GT = GB[:, 4096:6144]
    GT2 = GB[:, 8192:10240]
    MGT = GB[:, 6144:8192].bitcast(F32)
    MGT2 = GB[:, 10240:12288].bitcast(F32)
    F32V = GB[:, 12288:16384].bitcast(F32)
    TAIF = F32V[:, 0:128]
    TBIF = F32V[:, 128:256]
    TS16 = F32V[:, 256:384]
    IJF = F32V[:, 384:512]
    SLOTA = F32V[:, 512:640]
    EIDF = F32V[:, 640:768]
    CZ = F32V[:, 768:1024]
    EIDSb = GB[:, 14336:14592].bitcast(U32)
    W16b = F32V[:, 1152:1280]

    swapmask = []
    for i in range(0, 32, 2):
        swapmask += [i + 1, i]

    def mm(out_ap, lhsT_ap, rhs_ap, start, stop, sem=None):
        def b(eng, o=out_ap, l=lhsT_ap, r=rhs_ap, st=start, sp=stop):
            return eng.matmul(o, l, r, start=st, stop=sp)
        return p.op("te", b, sem)

    def cp(engine, out_ap, in_ap):
        def b(eng, o=out_ap, i=in_ap):
            return eng.tensor_copy(o, i)
        return p.op(engine, b)

    def act(out_ap, in_ap, func, bias=0.0, scale=1.0):
        def b(eng, o=out_ap, i=in_ap, f=func, bi=bias, sc=scale):
            return eng.activation(out=o, in_=i, func=f, bias=bi, scale=sc)
        return p.op("se", b)

    def tt(out_ap, a_ap, b_ap, alu, engine="ve"):
        def b(eng, o=out_ap, a=a_ap, bb=b_ap, alu=alu):
            return eng.tensor_tensor(o, a, bb, alu)
        return p.op(engine, b)

    def ts(out_ap, in_ap, s1, s2=None, op0=ALU.mult, op1=ALU.bypass, engine="ve"):
        def b(eng, o=out_ap, i=in_ap, s1=s1, s2=s2, op0=op0, op1=op1):
            return eng.tensor_scalar(out=o, in0=i, scalar1=s1, scalar2=s2,
                                     op0=op0, op1=op1)
        return p.op(engine, b)

    def stt(out_ap, in0_ap, scalar, in1_ap, op0, op1, engine="ve"):
        def b(eng, o=out_ap, i0=in0_ap, s=scalar, i1=in1_ap, op0=op0, op1=op1):
            return eng.scalar_tensor_tensor(out=o, in0=i0, scalar=s, in1=i1,
                                            op0=op0, op1=op1)
        return p.op(engine, b)

    def shuffle(out_ap, in_ap):
        def b(eng, o=out_ap, i=in_ap):
            return eng.stream_shuffle(out=o, in_=i, mask=swapmask)
        return p.op("ve", b)

    def vmax8(out_ap, in_ap):
        def b(eng, o=out_ap, i=in_ap):
            return eng.max(out=o, in_=i)
        return p.op("ve", b)

    def vmaxidx(out_ap, max_ap, in_ap):
        def b(eng, o=out_ap, m=max_ap, i=in_ap):
            return eng.max_index(out=o, in_max=m, in_values=i)
        return p.op("ve", b)

    def vmatchrep(out_ap, rep_ap, in_ap, immv):
        def b(eng, o=out_ap, r=rep_ap, i=in_ap, v=immv):
            return eng.match_replace(out=o, in_to_replace=r, in_values=i, imm_value=v)
        return p.op("ve", b)

    def reduce_sum_x(out_ap, in_ap):
        def b(eng, o=out_ap, i=in_ap):
            return eng.reduce_sum(o, i, mybir.AxisListType.X)
        return p.op("ve", b)

    def recip(out_ap, in_ap):
        def b(eng, o=out_ap, i=in_ap):
            return eng.reciprocal_approx_fast(o, i)
        return p.op("ve", b)

    def pbcast(out_ap, row_ap):
        def b(eng, o=out_ap, r=row_ap):
            return eng.partition_broadcast(o, r)
        return p.op("gp", b)

    def transpose_te(out_psum, in_ap, sem=None):
        def b(eng, o=out_psum, i=in_ap):
            return eng.transpose(o, i, BVB[:, 0:128])
        return p.op("te", b, sem)

    def dbg_dump(name, src_ap, f32stage=None):
        if name not in dbg:
            return
        p.wait("gp", sVE, sVE.val)
        p.wait("gp", sSE, sSE.val)
        p.wait("gp", sDBG, sDBG.val)
        p.dma("gp", dbg[name], src_ap, sDBG)
        p.wait("ve", sDBG, sDBG.val)

    p.dma("sy", SMALL[:], vecs_in, sLS)
    p.dma("sy", BVB[:], bv0, sLS)
    p.dma("sy", ONES1[:], ones_c[:, 0:1], sLS, slow_ok=True)
    p.dma("sy", IOTA16[:], iota16_in, sLS)
    p.dma("gp", VT[:].rearrange("p (a c) -> p a c", c=65)[:, :, 64:65],
          ones_c[:, 0:32].unsqueeze(2), sLG, slow_ok=True)
    p.dma("gp", USH[:], ush, sLG, slow_ok=True)
    p.dma("gp", ONESB[:], ones_c[:, 0:1], sLG, slow_ok=True)

    BQ0 = SMALL[:, 0:8]
    BK0 = SMALL[:, 8:16]
    BO0 = SMALL[:, 16:24]
    G1V = SMALL[:, 24:32]
    B1V = SMALL[:, 32:40]
    LNW1 = SMALL[:, 40:48]
    LNB1 = SMALL[:, 48:56]

    p.dma("gp", wgs, wg0, sLG)
    p.dma("gp", wus, wu0, sLG)
    p.dma("gp", wds, wd0, sLG)
    keysa_bf = dram("keysa_bf", [PKM_H, 128, PKM_S], BF16, kind="Internal")
    keysb_bf = dram("keysb_bf", [PKM_H, 128, PKM_S], BF16, kind="Internal")
    p.dma("gp", keysa_bf, keysa, sLG)
    keys_cast_mark = p.dma("gp", keysb_bf, keysb, sLG)


    def emit_ln_stats_finish():
        MS = STG2[0:1, :]
        p.wait("se", sTE, sTE.val)
        act(MS, PBIG[0:1, :], AF.Identity, scale=1.0 / D)
        act(F32C[0:1, :], PBIG[32:33, :], AF.Identity, scale=1.0 / D)
        p.wait("ve", sSE, sSE.val)
        tt(F32B[0:1, :], MS, MS, ALU.mult)
        tt(F32B[0:1, :], F32C[0:1, :], F32B[0:1, :], ALU.subtract)
        ts(F32B[0:1, :], F32B[0:1, :], float(EPS), None, ALU.add)
        recip(F32C[0:1, :], F32B[0:1, :])
        p.wait("se", sVE, sVE.val)
        act(F32C[0:1, :], F32C[0:1, :], AF.Sqrt)
        p.wait("ve", sSE, sSE.val)
        stt(F32B[0:1, :], MS, -1.0, F32C[0:1, :], ALU.mult, ALU.mult)
        p.wait("gp", sVE, sVE.val)
        p.wait("gp", sSE, sSE.val)
        p.wait("gp", sXS, sXS.val)
        pbcast(F32A[:], F32C[0:1, :])
        pbcast(F32B[:], F32B[0:1, :])

    def emit_stats_mms(src_f32_ap, c, qb=None, mark=False, ones=None):
        ones = ONES1 if ones is None else ones
        r = None
        if qb is None:
            for q in range(NQB):
                r = mm(PBIG[0:1, q * QB:(q + 1) * QB], ones[:],
                       src_f32_ap[:, q * QB:(q + 1) * QB],
                       start=(c == 0), stop=(c == NC8 - 1),
                       sem=sTE if (mark and q == NQB - 1) else None)
        else:
            r = mm(PBIG[0:1, qb * QB:(qb + 1) * QB], ones[:], src_f32_ap,
                   start=(c == 0), stop=(c == NC8 - 1),
                   sem=sTE if mark else None)
        return r

    def emit_stats_sq_mms(sq_f32_ap, c, qb=None, mark=False, ones=None):
        ones = ONES1 if ones is None else ones
        r = None
        if qb is None:
            for q in range(NQB):
                r = mm(PBIG[32:33, q * QB:(q + 1) * QB], ones[:],
                       sq_f32_ap[:, q * QB:(q + 1) * QB],
                       start=(c == 0), stop=(c == NC8 - 1),
                       sem=sTE if (mark and q == NQB - 1) else None)
        else:
            r = mm(PBIG[32:33, qb * QB:(qb + 1) * QB], ones[:], sq_f32_ap,
                   start=(c == 0), stop=(c == NC8 - 1),
                   sem=sTE if mark else None)
        return r

    def emit_x_stats(seq, src_dram):
        marks = {}
        p.wait("sy", sSE, sSE.val)
        p.wait("sy", sVE, sVE.val)
        p.wait("sy", sTE, sTE.val)
        p.wait("se", sSTS, sSTS.val)
        p.wait("ve", sSTS, sSTS.val)
        for c in range(NC8):
            S = STG if c % 2 == 0 else STG2
            sS = sST1 if c % 2 == 0 else sST2
            if c >= 2:
                p.wait("sy", sTE, marks[c - 2])
            t = p.dma("sy", S[:], src_dram[seq, c], sS)
            p.wait("se", sS, t)
            if c >= 1:
                p.wait("se", sTE, marks[c - 1])
            act(F32C[:], S[:], AF.Square)
            p.wait("te", sSE, sSE.val)
            p.wait("te", sS, t)
            emit_stats_mms(S, c, mark=False)
            marks[c] = emit_stats_sq_mms(F32C, c, mark=True)

    def emit_xn_apply(seq, src_dram):
        p.wait("ve", sGP, sGP.val)
        p.wait("sy", sGP, sGP.val)
        p.wait("sy", sSE, sSE.val)
        p.wait("sy", sVE, sVE.val)
        p.wait("sy", sTE, sTE.val)
        for c in range(NC8):
            S = STG if c % 2 == 0 else STG2
            sS = sST1 if c % 2 == 0 else sST2
            if c >= 2:
                p.wait("sy", sVE, sVE.val)
            t = p.dma("sy", S[:], src_dram[seq, c], sS)
            p.wait("ve", sS, t)
            tt(F32C[:], S[:], F32A[:], ALU.mult)
            tt(XN[:, c * T:(c + 1) * T], F32C[:], F32B[:], ALU.add)

    def emit_ffn0(seq):
        SCALE = 1.0 / 127.0
        p.wait("gp", sLG, sLG.val)
        wsem = {0: sWB0, 1: sWB1}
        for qb in range(NQB):
            xn_sl = lambda c: XN[:, c * T + qb * QB: c * T + qb * QB + QB]
            hb_te_done = {}
            load_marks = {}

            def issue_pair(hb):
                pair = hb % 2
                if hb >= 2:
                    p.wait("gp", sTE, hb_te_done[hb - 2])
                else:
                    p.wait("gp", sTE, sTE.val)
                t1 = p.dma("gp", WB[:, pair * 2048:pair * 2048 + 1024], wgs[hb],
                           wsem[pair])
                t2 = p.dma("gp", WB[:, pair * 2048 + 1024:pair * 2048 + 2048],
                           wus[hb], wsem[pair])
                load_marks[hb] = t2

            issue_pair(0)
            if NHB > 1:
                issue_pair(1)
            for hb in range(NHB):
                pair = hb % 2
                wg_sl = WB[:, pair * 2048:pair * 2048 + 1024]
                wu_sl = WB[:, pair * 2048 + 1024:pair * 2048 + 2048]
                psl = PSC[:, (hb % 2) * QB:(hb % 2) * QB + QB]
                p.wait("te", wsem[pair], load_marks[hb])
                p.wait("te", sSE, sSE.val)
                p.wait("te", sVE, sVE.val)
                for c in range(NC8):
                    mm(psl, wg_sl[:, c * 128:(c + 1) * 128], xn_sl(c),
                       start=(c == 0), stop=(c == NC8 - 1),
                       sem=sTE if c == NC8 - 1 else None)
                p.wait("se", sTE, sTE.val)
                act(SG[:], psl, AF.Silu, scale=SCALE)
                p.wait("te", sSE, sSE.val)
                for c in range(NC8):
                    mm(psl, wu_sl[:, c * 128:(c + 1) * 128], xn_sl(c),
                       start=(c == 0), stop=(c == NC8 - 1),
                       sem=sTE if c == NC8 - 1 else None)
                hb_te_done[hb] = sTE.val
                if hb + 2 < NHB:
                    issue_pair(hb + 2)
                p.wait("ve", sTE, sTE.val)
                p.wait("ve", sSE, sSE.val)
                def hb_mult(eng, hb=hb, psl=psl):
                    return eng.scalar_tensor_tensor(
                        out=BIG[:, hb * QB:(hb + 1) * QB], in0=psl,
                        scalar=SCALE, in1=SG[:], op0=ALU.mult, op1=ALU.mult)
                p.op("ve", hb_mult)
            dn_te_done = {}
            dn_marks = {}
            def issue_down(ob):
                if ob >= 2:
                    p.wait("gp", sTE, dn_te_done[ob - 2])
                else:
                    p.wait("gp", sTE, hb_te_done[NHB - 1] if ob == 0 else dn_marks.get("first_gate", sTE.val))
                t = p.dma("gp", WBD[:, (ob % 2) * NHB * 128:(ob % 2 + 1) * NHB * 128],
                          wds[ob], sWBD if ob % 2 == 0 else sWBD2)
                dn_marks[ob] = t
            issue_down(0)
            issue_down(1)
            for ob in range(NC8):
                wd_sl = WBD[:, (ob % 2) * NHB * 128:(ob % 2 + 1) * NHB * 128]
                p.wait("te", sWBD if ob % 2 == 0 else sWBD2, dn_marks[ob])
                p.wait("te", sVE, sVE.val)
                p.wait("te", sSE, sSE.val)
                for hb in range(NHB):
                    mm(PV[:], wd_sl[:, hb * 128:(hb + 1) * 128],
                       BIG[:, hb * QB:(hb + 1) * QB],
                       start=(hb == 0), stop=(hb == NHB - 1),
                       sem=sTE if hb == NHB - 1 else None)
                dn_te_done[ob] = sTE.val
                if ob + 2 < NC8:
                    issue_down(ob + 2)
                p.wait("se", sTE, sTE.val)
                act(GB[:, ob * T + qb * QB: ob * T + qb * QB + QB],
                    PV[:], AF.Silu, scale=SCALE)
        attn_state["wb_free"][0] = attn_state["wb_free"][1] = sTE.val

    attn_state = {"te_after_vproj": 0, "wb_free": {0: 0, 1: 0}, "marks": {},
                  "bank_free": {0: 0, 1: 0}, "proj_evac": 0, "rope_done": 0,
                  "evv": 0, "den_free": 0, "norm_done": 0}

    def attn_issue_loads(layer, hp):
        if layer == 0:
            wq, wk, wv = wq0, wk0, wv0
        else:
            wq, wk, wv = wq1, wk1, wv1
        par = hp % 2
        p.wait("gp", sTE, attn_state["wb_free"][par])
        base = par * 2048
        tq = p.dma("gp", WB[:, base:base + 1024], wq[hp], sWB0 if par == 0 else sWB1)
        tk = p.dma("gp", WB[:, base + 1024:base + 2048], wk[hp],
                   sWB0 if par == 0 else sWB1)
        tv = p.dma("gp", WBD[:, par * NHB * 128:par * NHB * 128 + 1024], wv[hp],
                   sWBD if par == 0 else sWBD2)
        attn_state["marks"][hp] = (tk, tv)

    def emit_attn_hp(seq, layer, hp):
        if layer == 0:
            wq, wk, wv = wq0, wk0, wv0
            plain = False
        else:
            wq, wk, wv = wq1, wk1, wv1
            plain = True
            p.wait("se", sVE, sVE.val)
        scale = 0.125

        if layer == 0 and hp == 0:
            p.wait("gp", sVE, sVE.val)
            p.wait("gp", sTE, sTE.val)
            tbv = p.dma("gp", BVB[:], bv0, sLG)
            p.wait("ve", sLG, tbv)
        if hp == 0:
            attn_issue_loads(layer, 0)
        par = hp % 2
        tk, tv = attn_state["marks"][hp]
        p.wait("te", sWB0 if par == 0 else sWB1, tk)
        p.wait("te", sWBD if par == 0 else sWBD2, tv)
        base = par * 2048

        for which in ("q", "k"):
            W = WB[:, base:base + 1024] if which == "q" else WB[:, base + 1024:base + 2048]
            OUTR = QR if which == "q" else KR
            OUTP = QP if which == "q" else KP
            p.wait("te", sVE, attn_state["rope_done"])
            for bk in (0, 1):
                if attn_state["bank_free"][bk] > 0:
                    p.wait("te", sVE, attn_state["bank_free"][bk])
            for half in range(2):
                p.wait("te", sSE, attn_state["proj_evac"])
                if attn_state["norm_done"] > 0:
                    p.wait("se", sVE, attn_state["norm_done"])
                for qs_ in range(2):
                    col = half * 1024 + qs_ * QB
                    for c in range(NC8):
                        mm(PBIG[:, qs_ * QB:(qs_ + 1) * QB],
                           W[:, c * 128:(c + 1) * 128],
                           XN[:, c * T + col: c * T + col + QB],
                           start=(c == 0), stop=(c == NC8 - 1),
                           sem=sTE if c == NC8 - 1 else None)
                p.wait("se", sTE, sTE.val)
                if layer == 0:
                    B = BQ0[:, hp:hp + 1] if which == "q" else BK0[:, hp:hp + 1]
                    m = act(F32C[:, half * 1024:(half + 1) * 1024], PBIG[:, 0:1024],
                            AF.Identity, bias=B)
                else:
                    m = act(F32C[:, half * 1024:(half + 1) * 1024], PBIG[:, 0:1024],
                            AF.Identity)
                attn_state["proj_evac"] = m
            p.wait("ve", sSE, attn_state["proj_evac"])
            if plain:
                cp("ve", OUTP[:], F32C[:])
            shuffle(F32B[:], F32C[:])
            tt(F32C[:], F32C[:], COS[:], ALU.mult)
            tt(F32B[:], F32B[:], SIN[:], ALU.mult)
            attn_state["rope_done"] = tt(OUTR[:], F32C[:], F32B[:], ALU.add)

        vbase = par * NHB * 128
        for kc in range(NKC):
            if attn_state["evv"] > 0:
                p.wait("te", sVE, attn_state["evv"])
            for c in range(NC8):
                mm(PV[:, 0:128],
                   XN[:, c * T + kc * 128: c * T + kc * 128 + 128],
                   WBD[:, vbase + c * 128:vbase + (c + 1) * 128],
                   start=(c == 0), stop=(c == NC8 - 1),
                   sem=sTE if c == NC8 - 1 else None)
            p.wait("ve", sTE, sTE.val)
            dst = VT[:].rearrange("p (h k c) -> p h k c", h=2, k=NKC)[:, :, kc, 0:64]
            if layer == 0:
                attn_state["evv"] = tt(
                    dst, PV[:, 0:128].rearrange("p (h c) -> p h c", h=2),
                    BVB[:, hp * 128:(hp + 1) * 128].rearrange("p (h c) -> p h c", h=2),
                    ALU.add)
            else:
                attn_state["evv"] = cp(
                    "ve", dst, PV[:, 0:128].rearrange("p (h c) -> p h c", h=2))

        p.wait("te", sVE, attn_state["evv"])
        attn_state["te_after_vproj"] = sTE.val
        attn_state["wb_free"][par] = sTE.val
        if hp + 1 < 8:
            attn_issue_loads(layer, hp + 1)

        for h01 in range(2):
            h = hp * 2 + h01
            rows = slice(h01 * 64, h01 * 64 + 64)
            for qb in range(NQB):
                bank = qb % 2
                avsl = PBIG[:, bank * QB:(bank + 1) * QB]
                expR = {}
                expP = {}
                NPAIR = NKC // 2
                if attn_state["bank_free"][bank] > 0:
                    p.wait("te", sVE, attn_state["bank_free"][bank])
                if plain and attn_state["den_free"] > 0:
                    p.wait("te", sVE, attn_state["den_free"])
                def av_pair(pr, qb=qb, h01=h01, avsl=avsl, expR=expR, expP=expP):
                    p.wait("te", sSE, (expP if plain else expR)[pr])
                    att_r = ATT if pr % 2 == 0 else ATT2
                    att_p = ATT3 if pr % 2 == 0 else ATT4
                    for j in range(2):
                        kc = 2 * pr + j
                        vt_base = (h01 * NKC + kc) * 65
                        if not plain:
                            mm(avsl[0:65, :],
                               VT[:, vt_base:vt_base + 65],
                               att_r[:, j * QB:(j + 1) * QB],
                               start=(kc == 0), stop=(kc == NKC - 1),
                               sem=sTE if kc == NKC - 1 else None)
                        else:
                            mm(avsl[0:64, :],
                               VT[:, vt_base:vt_base + 64],
                               att_r[:, j * QB:(j + 1) * QB],
                               start=(kc == 0), stop=(kc == NKC - 1))
                            mm(PBIG[0:1, 1024:1024 + QB],
                               VT[:, vt_base + 64:vt_base + 65],
                               att_p[:, j * QB:(j + 1) * QB],
                               start=(kc == 0), stop=(kc == NKC - 1),
                               sem=sTE if kc == NKC - 1 else None)
                if not plain:
                    PB23 = PBIG[:, 1024:2048]
                    for pr in range(NPAIR):
                        k0 = 2 * pr
                        psl = PSC[:, :] if pr % 2 == 0 else PB23
                        if pr >= 2:
                            p.wait("te", sSE, expR[pr - 2])
                        mm(psl[:, 0:QB], KR[rows, k0 * 128:k0 * 128 + 128],
                           QR[rows, qb * QB:qb * QB + QB], True, True)
                        mm(psl[:, QB:2 * QB],
                           KR[rows, k0 * 128 + 128:k0 * 128 + 256],
                           QR[rows, qb * QB:qb * QB + QB], True, True, sem=sTE)
                        p.wait("se", sTE, sTE.val)
                        def expf(eng, psl=psl, dst=(ATT if pr % 2 == 0 else ATT2)):
                            return eng.activation(out=dst[:, :], in_=psl[:, :],
                                                  func=AF.Exp, scale=scale)
                        expR[pr] = p.op("se", expf)
                        if pr >= 1:
                            av_pair(pr - 1)
                    av_pair(NPAIR - 1)
                    p.wait("ve", sTE, sTE.val)
                    evm = cp("ve", F32B[0:64, qb * QB:(qb + 1) * QB], avsl[0:64, :])
                    evm = cp("ve", STG2[0:1, qb * QB:(qb + 1) * QB], avsl[64:65, :])
                    attn_state["bank_free"][bank] = evm
                    pkm_feed(4)
                    continue
                for pr in range(NPAIR):
                    k0 = 2 * pr
                    if pr >= 1:
                        p.wait("te", sSE, expR[pr - 1])
                    mm(PSC[:, 0:QB], KR[rows, k0 * 128:k0 * 128 + 128],
                       QR[rows, qb * QB:qb * QB + QB], True, True)
                    mm(PSC[:, QB:2 * QB],
                       KR[rows, k0 * 128 + 128:k0 * 128 + 256],
                       QR[rows, qb * QB:qb * QB + QB], True, True, sem=sTE)
                    p.wait("se", sTE, sTE.val)
                    def expf(eng, dst=(ATT if pr % 2 == 0 else ATT2)):
                        return eng.activation(out=dst[:, :], in_=PSC[:],
                                              func=AF.Exp, scale=scale)
                    expR[pr] = p.op("se", expf)
                    if pr >= 1:
                        p.wait("te", sSE, expP[pr - 1])
                    mm(PST[:, 0:QB], KP[rows, k0 * 128:k0 * 128 + 128],
                       QP[rows, qb * QB:qb * QB + QB], True, True)
                    mm(PV[:, 0:QB],
                       KP[rows, k0 * 128 + 128:k0 * 128 + 256],
                       QP[rows, qb * QB:qb * QB + QB], True, True, sem=sTE)
                    p.wait("se", sTE, sTE.val)
                    attp = ATT3 if pr % 2 == 0 else ATT4
                    def expf2a(eng, dst=attp):
                        return eng.activation(out=dst[:, 0:QB], in_=PST[:, :],
                                              func=AF.Exp, scale=scale)
                    p.op("se", expf2a)
                    def expf2b(eng, dst=attp):
                        return eng.activation(out=dst[:, QB:2 * QB], in_=PV[:, :],
                                              func=AF.Exp, scale=scale)
                    expP[pr] = p.op("se", expf2b)
                    if pr >= 1:
                        av_pair(pr - 1)
                av_pair(NPAIR - 1)
                p.wait("ve", sTE, sTE.val)
                evm = cp("ve", F32B[0:64, qb * QB:(qb + 1) * QB], avsl[0:64, :])
                if plain:
                    attn_state["den_free"] = cp(
                        "ve", STG2[0:1, qb * QB:(qb + 1) * QB], PBIG[0:1, 1024:1024 + QB])
                else:
                    evm = cp("ve", STG2[0:1, qb * QB:(qb + 1) * QB], avsl[64:65, :])
                attn_state["bank_free"][bank] = evm if not plain else max(
                    evm, attn_state["den_free"]) if False else evm
                pkm_feed(4)
            rm = recip(STG2[0:1, :], STG2[0:1, :])
            p.wait("gp", sVE, rm)
            pb = pbcast(F32C[:], STG2[0:1, :])
            p.wait("ve", sGP, pb)
            nm = tt(F32B[0:64, :], F32B[0:64, :], F32C[0:64, :], ALU.mult)
            attn_state["norm_done"] = nm
            if layer == 0:
                ychunk = h // 2
                yrows = slice((h % 2) * 64, (h % 2) * 64 + 64)
                cp("ve", BIG[yrows, ychunk * T:(ychunk + 1) * T], F32B[0:64, :])
            else:
                p.wait("gp", sVE, nm)
                p.dma("gp", yscr[h], F32B[0:64, :], sXS)
                p.wait("ve", sXS, sXS.val)

    def emit_wo_update0(seq):
        for ob in range(NC8):
            p.wait("gp", sTE, sTE.val)
            tw = p.dma("gp", WB[:, 0:1024], wo0[ob], sWB0)
            p.wait("te", sWB0, tw)
            for qb in range(NQB):
                if not (ob == 0 and qb == 0):
                    p.wait("sy", sVE, sVE.val)
                else:
                    p.wait("sy", sVE, sVE.val)
                    p.wait("sy", sSE, sSE.val)
                    p.wait("sy", sTE, sTE.val)
                t0 = p.dma("sy", STG[:, 0:QB],
                           x_in[seq, ob][:, qb * QB:(qb + 1) * QB], sST1)
                p.wait("te", sSE, sSE.val)
                p.wait("te", sVE, sVE.val)
                for c in range(NC8):
                    mm(PV[:], WB[:, c * 128:(c + 1) * 128],
                       BIG[:, c * T + qb * QB: c * T + qb * QB + QB],
                       start=(c == 0), stop=(c == NC8 - 1),
                       sem=sTE if c == NC8 - 1 else None)
                p.wait("se", sTE, sTE.val)
                act(F32C[:, 0:QB], PV[:], AF.Identity, bias=BO0[:, ob:ob + 1])
                p.wait("ve", sSE, sSE.val)
                p.wait("ve", sST1, t0)
                p.wait("ve", sTE, sTE.val)
                p.wait("ve", sXS, sXS.val)
                tt(F32A[:, 0:QB], F32C[:, 0:QB],
                   GB[:, ob * T + qb * QB: ob * T + qb * QB + QB], ALU.mult)
                tt(F32A[:, 0:QB], F32A[:, 0:QB], STG[:, 0:QB], ALU.add)
                p.wait("gp", sVE, sVE.val)
                p.dma("gp", x1scr[seq, ob][:, qb * QB:(qb + 1) * QB],
                      F32A[:, 0:QB], sXS)
                p.wait("se", sVE, sVE.val)
                act(F32B[:, 0:QB], F32A[:, 0:QB], AF.Square)
                p.wait("te", sSE, sSE.val)
                emit_stats_mms(F32A[:, 0:QB], ob, qb)
                emit_stats_sq_mms(F32B[:, 0:QB], ob, qb, mark=True)
        attn_state["wb_free"][0] = attn_state["wb_free"][1] = sTE.val

    def emit_pkm_hproj(seq, dbg_on=False):
        p.wait("gp", sVE, sVE.val)
        p.wait("gp", sTE, sTE.val)
        tid = p.dma("gp", BVB[:, 0:128], ident_in, sLG)
        p.wait("te", sLG, tid)
        for hd in range(PKM_H):
            par = hd % 2
            p.wait("gp", sTE, sTE.val)
            tw = p.dma("gp", WB[:, par * 1024:(par + 1) * 1024], wi1[hd],
                       sWB0 if par == 0 else sWB1)
            p.wait("te", sWB0 if par == 0 else sWB1, tw)
            for qb in range(NQB):
                p.wait("te", sSE, sSE.val)
                for c in range(NC8):
                    mm(PV[:], WB[:, par * 1024 + c * 128:par * 1024 + (c + 1) * 128],
                       XN[:, c * T + qb * QB: c * T + qb * QB + QB],
                       start=(c == 0), stop=(c == NC8 - 1),
                       sem=sTE if c == NC8 - 1 else None)
                p.wait("se", sTE, sTE.val)
                act(BIG[:, hd * T + qb * QB: hd * T + qb * QB + QB], PV[:],
                    AF.Identity)
        attn_state["wb_free"][0] = attn_state["wb_free"][1] = sTE.val
        if dbg_on:
            dbg_dump("hpkm", BIG[:])

    def emit_pkm_ugate(seq):
        p.wait("te", sSE, sSE.val)
        for tb in range(NTB):
            for hd in range(PKM_H):
                mm(PST[:, hd:hd + 1],
                   BIG[:, hd * T + tb * 128: hd * T + tb * 128 + 128],
                   USH[:], start=True, stop=True,
                   sem=sTE if hd == PKM_H - 1 else None)
            p.wait("se", sTE, sTE.val)
            act(ACTT[:, tb * 8:(tb + 1) * 8], PST[:, 0:8], AF.Sigmoid)
            p.wait("te", sSE, sSE.val)

    pkm_state = {"keys_free": {0: 0, 1: 0}, "slot": 0, "evac": {0: 0, 1: 0},
                 "topk": {0: 0, 1: 0}, "gt_comb": {0: 0, 1: 0}, "comb": {},
                 "ig_mark": {}, "ig_done_tb": {0: None, 1: None},
                 "tp_mark": {0: 0, 1: 0}, "pending": [], "combq": []}

    def emit_pkm_tb(seq, tb, dbg_on=False):
        for hd in range(PKM_H):
            ksl = hd % 2
            if pkm_state["keys_free"][ksl] > 0:
                p.wait("sy", sTE, pkm_state["keys_free"][ksl])
            ka = KA[:, ksl * PKM_S:(ksl + 1) * PKM_S]
            kb = KB_[:, ksl * PKM_S:(ksl + 1) * PKM_S]
            semA = sKA if ksl == 0 else sKA2
            semB = sKB if ksl == 0 else sKB2
            tk1 = p.dma("sy", ka, keysa_bf[hd], semA)
            tk2 = p.dma("sy", kb, keysb_bf[hd], semB)
            p.wait("te", semA, tk1)
            p.wait("te", semB, tk2)
            hsl = BIG[:, hd * T + tb * 128: hd * T + tb * 128 + 128]
            for ab in range(2):
                K = ka if ab == 0 else kb
                sl = pkm_state["slot"]
                pkm_state["slot"] ^= 1
                scf_sl = SCF[:, sl * PKM_S:(sl + 1) * PKM_S]
                if pkm_state["evac"][sl ^ 1] > 0:
                    p.wait("te", sSE, pkm_state["evac"][sl ^ 1])
                if pkm_state["topk"][sl] > 0:
                    p.wait("se", sVE, pkm_state["topk"][sl])
                sc_mark = mm(PBIG[:, 1536:2048], hsl, K, start=True, stop=True,
                             sem=sTE)
                if ab == 1:
                    pkm_state["keys_free"][ksl] = sc_mark
                p.wait("se", sTE, sc_mark)
                pkm_state["evac"][sl] = act(scf_sl, PBIG[:, 1536:2048], AF.Identity)
                p.wait("ve", sSE, pkm_state["evac"][sl])
                TS_ = TAS if ab == 0 else TBS
                TI_ = TAI if ab == 0 else TBI
                vmax8(TS_[:, hd * 16:hd * 16 + 8], scf_sl)
                vmaxidx(TI_[:, hd * 16:hd * 16 + 8],
                        TS_[:, hd * 16:hd * 16 + 8], scf_sl)
                vmatchrep(SCZ[:], TS_[:, hd * 16:hd * 16 + 8], scf_sl, -3e38)
                vmax8(TS_[:, hd * 16 + 8:hd * 16 + 16], SCZ[:])
                pkm_state["topk"][sl] = vmaxidx(
                    TI_[:, hd * 16 + 8:hd * 16 + 16],
                    TS_[:, hd * 16 + 8:hd * 16 + 16], SCZ[:])
        emit_pkm_stage_c(seq, tb, dbg_on)

    def emit_pkm_wout(seq):
        p.wait("te", sVE, sVE.val)
        wout_marks = {}
        wout_done = {}
        def issue_wout(ob):
            if ob >= 2:
                p.wait("gp", sTE, wout_done[ob - 2])
            else:
                p.wait("gp", sTE, sTE.val)
            t = p.dma("gp", WB[:, (ob % 2) * 1024:(ob % 2) * 1024 + 1024],
                      wout1[ob], sWB0 if ob % 2 == 0 else sWB1)
            wout_marks[ob] = t
        issue_wout(0)
        issue_wout(1)
        for ob in range(NC8):
            p.wait("te", sWB0 if ob % 2 == 0 else sWB1, wout_marks[ob])
            for qb in range(NQB):
                p.wait("te", sSE, sSE.val)
                for c in range(NC8):
                    mm(PV[:], WB[:, (ob % 2) * 1024 + c * 128:(ob % 2) * 1024 + (c + 1) * 128],
                       BIG[:, c * T + qb * QB: c * T + qb * QB + QB],
                       start=(c == 0), stop=(c == NC8 - 1),
                       sem=sTE if c == NC8 - 1 else None)
                p.wait("se", sTE, sTE.val)
                act(GB[:, ob * T + qb * QB: ob * T + qb * QB + QB], PV[:],
                    AF.Identity)
            wout_done[ob] = sTE.val
            if ob + 2 < NC8:
                issue_wout(ob + 2)
        attn_state["wb_free"][0] = attn_state["wb_free"][1] = sTE.val

    def emit_pkm_stage_c(seq, tb, dbg_on=False):
        eids_t = EIDS if tb % 2 == 0 else EIDSb
        w16_t = W16 if tb % 2 == 0 else W16b
        prev_ig = pkm_state["ig_done_tb"][tb % 2]
        if prev_ig is not None:
            p.wait("ve", sIG, prev_ig[0])
            p.wait("ve", sIG2, prev_ig[1])
        cp("ve", TAIF[:], TAI[:])
        cp("ve", TBIF[:], TBI[:])
        tt(CAND[:].rearrange("p (h i j) -> p h i j", h=8, i=16),
           TAS[:].rearrange("p (h i) -> p h i", h=8).unsqueeze(3).to_broadcast(
               [128, 8, 16, 16]),
           TBS[:].rearrange("p (h j) -> p h j", h=8).unsqueeze(2).to_broadcast(
               [128, 8, 16, 16]),
           ALU.add)
        for hd in range(PKM_H):
            csl = CAND[:, hd * 256:(hd + 1) * 256]
            vmax8(TS16[:, hd * 16:hd * 16 + 8], csl)
            vmaxidx(POS[:, hd * 16:hd * 16 + 8], TS16[:, hd * 16:hd * 16 + 8], csl)
            vmatchrep(CZ[:], TS16[:, hd * 16:hd * 16 + 8], csl, -3e38)
            vmax8(TS16[:, hd * 16 + 8:hd * 16 + 16], CZ[:])
            vmaxidx(POS[:, hd * 16 + 8:hd * 16 + 16],
                    TS16[:, hd * 16 + 8:hd * 16 + 16], CZ[:])
        ts(IJ2[:], POS[:], 4, None, ALU.logical_shift_right)
        cp("ve", IJF[:], IJ2[:])
        tt(OH[:].rearrange("p (h m i) -> p h m i", h=8, m=16),
           IJF[:].rearrange("p (h m) -> p h m", h=8).unsqueeze(3).to_broadcast(
               [128, 8, 16, 16]),
           IOTA16[:].unsqueeze(1).unsqueeze(1).to_broadcast([128, 8, 16, 16]),
           ALU.is_equal)
        tt(OH[:].rearrange("p (h m i) -> p h m i", h=8, m=16),
           OH[:].rearrange("p (h m i) -> p h m i", h=8, m=16),
           TAIF[:].rearrange("p (h i) -> p h i", h=8).unsqueeze(2).to_broadcast(
               [128, 8, 16, 16]),
           ALU.mult)
        reduce_sum_x(SLOTA[:].rearrange("p (h m) -> p h m", h=8),
                     OH[:].rearrange("p (h m i) -> p h m i", h=8, m=16))
        ts(EIDF[:], SLOTA[:], 512.0, None, ALU.mult)
        ts(IJ2[:], POS[:], 15, None, ALU.bitwise_and)
        cp("ve", IJF[:], IJ2[:])
        tt(OH[:].rearrange("p (h m i) -> p h m i", h=8, m=16),
           IJF[:].rearrange("p (h m) -> p h m", h=8).unsqueeze(3).to_broadcast(
               [128, 8, 16, 16]),
           IOTA16[:].unsqueeze(1).unsqueeze(1).to_broadcast([128, 8, 16, 16]),
           ALU.is_equal)
        tt(OH[:].rearrange("p (h m i) -> p h m i", h=8, m=16),
           OH[:].rearrange("p (h m i) -> p h m i", h=8, m=16),
           TBIF[:].rearrange("p (h i) -> p h i", h=8).unsqueeze(2).to_broadcast(
               [128, 8, 16, 16]),
           ALU.mult)
        reduce_sum_x(SLOTA[:].rearrange("p (h m) -> p h m", h=8),
                     OH[:].rearrange("p (h m i) -> p h m i", h=8, m=16))
        tt(EIDF[:], EIDF[:], SLOTA[:], ALU.add)
        cp("ve", eids_t[:, :], EIDF[:])
        negm = ts(WSUM[:], TS16[:].rearrange("p (h m) -> p h m", h=8)[:, :, 0:1]
                  .rearrange("p h m -> p (h m)"), -1.0, None, ALU.mult)
        p.wait("se", sVE, negm)
        em = 0
        for hd in range(PKM_H):
            em = act(w16_t[:, hd * 16:(hd + 1) * 16], TS16[:, hd * 16:(hd + 1) * 16],
                     AF.Exp, bias=WSUM[:, hd:hd + 1])
        p.wait("ve", sSE, em)
        reduce_sum_x(WSUM[:].unsqueeze(2), w16_t[:, :].rearrange("p (h m) -> p h m", h=8))
        recip(WSUM[:], WSUM[:])
        tt(w16_t[:, :].rearrange("p (h m) -> p h m", h=8),
           w16_t[:, :].rearrange("p (h m) -> p h m", h=8),
           WSUM[:].unsqueeze(2).to_broadcast([128, 8, 16]), ALU.mult)
        wready = tt(w16_t[:, :].rearrange("p (h m) -> p h m", h=8),
                    w16_t[:, :].rearrange("p (h m) -> p h m", h=8),
                    ACTT[:, tb * 8:(tb + 1) * 8].unsqueeze(2).to_broadcast([128, 8, 16]),
                    ALU.mult)
        if dbg_on and tb == 0:
            dbg_dump("eids", eids_t[:, :])
            dbg_dump("w16", w16_t[:, :])
        for hd in range(PKM_H):
            def iss(hd=hd, tb=tb, wr=wready, eids_t=eids_t):
                gts = GT if hd % 2 == 0 else GT2
                igsem = sIG if hd % 2 == 0 else sIG2
                if hd == 0:
                    p.wait("gp", sVE, wr)
                if pkm_state["gt_comb"][hd % 2] > 0:
                    p.wait("gp", sVE, pkm_state["gt_comb"][hd % 2])
                for m in range(16):
                    p.indirect(gts[:, m * 128:(m + 1) * 128], expv,
                               eids_t[:, hd * 16 + m:hd * 16 + m + 1], igsem)
                pkm_state["ig_mark"][(tb % 2, hd)] = igsem.val
                if hd == PKM_H - 1:
                    pkm_state["ig_done_tb"][tb % 2] = (sIG.val, sIG2.val)

            def comb(hd=hd, tb=tb, w16_t=w16_t):
                gts = GT if hd % 2 == 0 else GT2
                igsem = sIG if hd % 2 == 0 else sIG2
                mgt = MGT if tb % 2 == 0 else MGT2
                if hd == 0 and pkm_state["tp_mark"][tb % 2] > 0:
                    p.wait("ve", sTE, pkm_state["tp_mark"][tb % 2])
                p.wait("ve", igsem, pkm_state["ig_mark"][(tb % 2, hd)])
                ts(mgt[:, hd * 128:(hd + 1) * 128], gts[:, 0:128],
                   w16_t[:, hd * 16:hd * 16 + 1], None, ALU.mult)
                cm = 0
                for m in range(1, 16):
                    cm = stt(mgt[:, hd * 128:(hd + 1) * 128],
                             gts[:, m * 128:(m + 1) * 128],
                             w16_t[:, hd * 16 + m:hd * 16 + m + 1],
                             mgt[:, hd * 128:(hd + 1) * 128], ALU.mult, ALU.add)
                pkm_state["gt_comb"][hd % 2] = cm
                pkm_state["comb"][tb] = cm

            pend = pkm_state["pending"]
            pend.append(iss)
            pkm_state["combq"].append(comb)
            if len(pkm_state["combq"]) > 1:
                pend.append(pkm_state["combq"].pop(0))
        if tb % 2 == 1:
            pend = pkm_state["pending"]
            while pkm_state["combq"]:
                pend.append(pkm_state["combq"].pop(0))

    def pkm_feed(n=1):
        for _ in range(n):
            if pkm_state["pending"]:
                pkm_state["pending"].pop(0)()

    def pkm_flush():
        while pkm_state["pending"]:
            pkm_state["pending"].pop(0)()

    def emit_pkm_transposes(seq, tb):
        mgt = MGT if tb % 2 == 0 else MGT2
        p.wait("te", sVE, pkm_state["comb"][tb])
        p.wait("te", sSE, sSE.val)
        tp_evac = pkm_state.setdefault("tp_evac", {0: 0, 1: 0})
        tpm = 0
        for hd in range(PKM_H):
            psl = PST[:, 0:128] if hd % 2 == 0 else PV[:, 0:128]
            if tp_evac[hd % 2] > 0:
                p.wait("te", sVE, tp_evac[hd % 2])
            tpm = transpose_te(psl, mgt[:, hd * 128:(hd + 1) * 128], sem=sTE)
            p.wait("ve", sTE, tpm)
            tp_evac[hd % 2] = cp(
                "ve", BIG[:, hd * T + tb * 128: hd * T + tb * 128 + 128], psl)
        pkm_state["tp_mark"][tb % 2] = tpm
        attn_state["evv"] = tp_evac[1]

    def emit_pkm_lnsilu(seq):
        marks = {}
        p.wait("te", sSE, sSE.val)
        for c in range(NC8):
            p.wait("se", sVE, sVE.val)
            if c >= 1:
                p.wait("se", sTE, marks[c - 1])
            act(F32C[:], GB[:, c * T:(c + 1) * T], AF.Square)
            emit_stats_mms(GB[:, c * T:(c + 1) * T], c, mark=False, ones=ONESB)
            p.wait("te", sSE, sSE.val)
            marks[c] = emit_stats_sq_mms(F32C, c, mark=True)
        emit_ln_stats_finish()
        p.wait("ve", sGP, sGP.val)
        for c in range(NC8):
            tt(F32C[:], GB[:, c * T:(c + 1) * T], F32A[:], ALU.mult)
            tt(F32C[:], F32C[:], F32B[:], ALU.add)
            ts(F32C[:], F32C[:], LNW1[:, c:c + 1], LNB1[:, c:c + 1],
               ALU.mult, ALU.add)
            p.wait("se", sVE, sVE.val)
            act(GB[:, c * T:(c + 1) * T], F32C[:], AF.Silu)
            p.wait("ve", sSE, sSE.val)

    def emit_wo1_ln(seq):
        YB = BIG[:, 0:4096]
        for qb in range(NQB):
            p.wait("sy", sTE, sTE.val)
            ymark = 0
            for c in range(NC8):
                p.dma("sy", YB[0:64, c * QB:(c + 1) * QB],
                      yscr[2 * c, :, qb * QB:(qb + 1) * QB], sLS)
                ymark = p.dma("sy", YB[64:128, c * QB:(c + 1) * QB],
                              yscr[2 * c + 1, :, qb * QB:(qb + 1) * QB], sLS)
            wmarks = {}
            for ob in range(NC8):
                par = ob % 2
                if ob == 0 and qb == 0:
                    p.wait("gp", sTE, sTE.val)
                    wmarks[0] = p.dma("gp", WB[:, 0:1024], wo1[0], sWB0)
                    p.wait("gp", sTE, sTE.val)
                    wmarks[1] = p.dma("gp", WB[:, 1024:2048], wo1[1], sWB1)
                elif ob == 0:
                    p.wait("gp", sTE, sTE.val)
                    wmarks[0] = p.dma("gp", WB[:, 0:1024], wo1[0], sWB0)
                    wmarks[1] = p.dma("gp", WB[:, 1024:2048], wo1[1], sWB1)
                p.wait("te", sWB0 if par == 0 else sWB1, wmarks[ob])
                p.wait("te", sLS, ymark)
                p.wait("te", sSE, sSE.val)
                for c in range(NC8):
                    mm(PV[:], WB[:, par * 1024 + c * 128:par * 1024 + (c + 1) * 128],
                       YB[:, c * QB:(c + 1) * QB],
                       start=(c == 0), stop=(c == NC8 - 1),
                       sem=sTE if c == NC8 - 1 else None)
                if ob + 2 < NC8:
                    p.wait("gp", sTE, sTE.val)
                    wmarks[ob + 2] = p.dma(
                        "gp", WB[:, par * 1024:par * 1024 + 1024], wo1[ob + 2],
                        sWB0 if par == 0 else sWB1)
                p.wait("se", sTE, sTE.val)
                p.wait("se", sVE, sVE.val)
                act(F32A[:, 0:QB], PV[:], AF.Identity)
                p.wait("ve", sSE, sSE.val)
                p.wait("ve", sTE, sTE.val)
                tt(F32A[:, 0:QB], F32A[:, 0:QB],
                   XN[:, ob * T + qb * QB: ob * T + qb * QB + QB], ALU.add)
                p.wait("se", sVE, sVE.val)
                act(F32B[:, 0:QB], F32A[:, 0:QB], AF.Square)
                p.wait("te", sVE, sVE.val)
                p.wait("te", sSE, sSE.val)
                emit_stats_mms(F32A[:, 0:QB], ob, qb)
                emit_stats_sq_mms(F32B[:, 0:QB], ob, qb, mark=True)
                cp("ve", XN[:, ob * T + qb * QB: ob * T + qb * QB + QB],
                   F32A[:, 0:QB])
        attn_state["wb_free"][0] = attn_state["wb_free"][1] = sTE.val
        emit_ln_stats_finish()
        p.wait("ve", sGP, sGP.val)
        for c in range(NC8):
            tt(F32C[:], XN[:, c * T:(c + 1) * T], F32A[:], ALU.mult)
            tt(F32C[:], F32C[:], F32B[:], ALU.add)
            ts(BIG[:, c * T:(c + 1) * T], F32C[:], G1V[:, c:c + 1], B1V[:, c:c + 1],
               ALU.mult, ALU.add)

    def emit_final_update(seq):
        p.wait("sy", sXS, sXS.val)
        p.wait("sy", sSE, sSE.val)
        p.wait("sy", sVE, sVE.val)
        p.wait("sy", sTE, sTE.val)
        for c in range(NC8):
            if c > 0:
                p.wait("sy", sVE, sVE.val)
            t0 = p.dma("sy", STG[:], x1scr[seq, c], sST1)
            p.wait("ve", sST1, t0)
            tt(F32C[:], BIG[:, c * T:(c + 1) * T], GB[:, c * T:(c + 1) * T],
               ALU.mult)
            tt(F32C[:], F32C[:], STG[:], ALU.add)
            p.wait("sy", sVE, sVE.val)
            p.dma("sy", out_hbm[seq, c], F32C[:], sSTS)
            p.wait("ve", sSTS, sSTS.val)

    for seq in range(NSEQ):
        if seq == 0:
            p.wait("te", sLS, sLS.val)
            p.wait("ve", sLS, sLS.val)
            p.wait("se", sLS, sLS.val)
        emit_x_stats(seq, x_in)
        tc = p.dma("sy", COS[:], cos0, sLS)
        tsn = p.dma("sy", SIN[:], sin0, sLS)
        emit_ln_stats_finish()
        emit_xn_apply(seq, x_in)
        if "xn" in dbg and seq == 0:
            dbg_dump("xn", XN[:])
        emit_ffn0(seq)
        if "g" in dbg and seq == 0:
            dbg_dump("g", GB[:])
        p.wait("ve", sLS, tsn)
        for hp in range(8):
            emit_attn_hp(seq, 0, hp)
        if "y" in dbg and seq == 0:
            dbg_dump("y", BIG[:])
        emit_wo_update0(seq)
        emit_ln_stats_finish()
        p.wait("sy", sXS, sXS.val)
        emit_xn_apply(seq, x1scr)
        tc = p.dma("sy", COS[:], cos1, sLS)
        tsn = p.dma("sy", SIN[:], sin1, sLS)
        p.wait("ve", sLS, tsn)
        emit_pkm_hproj(seq, dbg_on=("hpkm" in dbg and seq == 0))
        emit_pkm_ugate(seq)
        p.wait("sy", sLG, keys_cast_mark)
        for hp in range(8):
            if hp >= 1:
                emit_pkm_transposes(seq, 2 * (hp - 1))
                emit_pkm_transposes(seq, 2 * (hp - 1) + 1)
            emit_pkm_tb(seq, 2 * hp, dbg_on=(seq == 0))
            emit_pkm_tb(seq, 2 * hp + 1, dbg_on=(seq == 0))
            emit_attn_hp(seq, 1, hp)
        pkm_flush()
        emit_pkm_transposes(seq, 14)
        emit_pkm_transposes(seq, 15)
        emit_pkm_wout(seq)
        emit_pkm_lnsilu(seq)
        emit_wo1_ln(seq)
        if "s1" in dbg and seq == 0:
            dbg_dump("s1", BIG[:])
        emit_final_update(seq)

    p.wait("gp", sSTS, sSTS.val)
    p.wait("gp", sIG, sIG.val)
    p.wait("gp", sIG2, sIG2.val)

    for s in p.sems.values():
        s.handle = ctx.enter_context(nc.semaphore(s.name))
    p.run(nc)
    ctx.close()
    return nc, dbg



def tile_proj(W, nob=None):
    outdim, indim = W.shape
    nob = outdim // 128 if nob is None else nob
    nic = indim // 128
    wt = W.T.reshape(nic, 128, outdim)
    out = np.empty((nob, 128, nic * 128), W.dtype)
    for ob in range(nob):
        sl = wt[:, :, ob * 128:(ob + 1) * 128]
        out[ob] = sl.transpose(1, 0, 2).reshape(128, nic * 128)
    return np.ascontiguousarray(out)


def rope_tables_l0():
    HD = 64
    inv = 1.0 / 10000.0 ** (np.arange(0, HD, 2, dtype=np.float32) / HD)
    t = np.arange(T, dtype=np.float32)
    emb = np.concatenate([np.outer(t, inv)] * 2, axis=-1)
    cos = np.cos(emb); sin = np.sin(emb)
    cos_t = np.zeros((128, T), np.float32)
    sin_t = np.zeros((128, T), np.float32)
    for r in range(128):
        d = r % 64
        cos_t[r] = cos[:, d]
        sin_t[r] = sin[:, d] * (-1.0 if d % 2 == 0 else 1.0)
    return cos_t, sin_t


def rope_tables_l1():
    HD = 64
    i = np.arange(1, HD // 2 + 1, dtype=np.float32)
    thetas = np.repeat(1000.0 ** (-2.0 * i / HD), 2)
    pos = np.arange(1, T + 1, dtype=np.float32)
    args = pos[:, None] * thetas[None, :]
    sin = np.sin(args); cos = np.cos(args)
    cos_t = np.zeros((128, T), np.float32)
    sin_t = np.zeros((128, T), np.float32)
    for r in range(128):
        d = r % 64
        cos_t[r] = cos[:, d]
        sin_t[r] = sin[:, d] * (-1.0 if d % 2 == 0 else 1.0)
    return cos_t, sin_t


def pack_inputs(core, inp):
    d = {}
    x = inp["L0_attn_x"]
    xs = x[2 * core:2 * core + 2]
    d["x_in"] = np.ascontiguousarray(xs.transpose(0, 2, 1)).reshape(NSEQ, NC8, 128, T)

    d["wq0"] = tile_proj(inp["L0_attn_Wq"])
    d["wk0"] = tile_proj(inp["L0_attn_Wk"])
    d["wv0"] = tile_proj(inp["L0_attn_Wv"])
    d["wo0"] = tile_proj(inp["L0_attn_Wo"])
    d["bq0"] = np.ascontiguousarray(inp["L0_attn_bq"].reshape(8, 128).T)
    d["bk0"] = np.ascontiguousarray(inp["L0_attn_bk"].reshape(8, 128).T)
    d["bv0"] = np.ascontiguousarray(
        np.broadcast_to(inp["L0_attn_bv"][None, :], (128, D)))
    vecs = np.zeros((128, 64), np.float32)
    vecs[:, 0:8] = inp["L0_attn_bq"].reshape(8, 128).T
    vecs[:, 8:16] = inp["L0_attn_bk"].reshape(8, 128).T
    vecs[:, 16:24] = inp["L0_attn_bo"].reshape(8, 128).T
    vecs[:, 24:32] = inp["L1_attn_ln_gamma"].reshape(8, 128).T
    vecs[:, 32:40] = inp["L1_attn_ln_beta"].reshape(8, 128).T
    vecs[:, 40:48] = inp["L1_ffn_ln_w"].reshape(8, 128).T
    vecs[:, 48:56] = inp["L1_ffn_ln_b"].reshape(8, 128).T
    d["vecs_in"] = vecs
    d["bo0"] = vecs[:, 16:24].copy()
    import ml_dtypes
    c0, s0 = rope_tables_l0()
    d["cos0"], d["sin0"] = c0.astype(ml_dtypes.bfloat16), s0.astype(ml_dtypes.bfloat16)
    c1, s1 = rope_tables_l1()
    d["cos1"], d["sin1"] = c1.astype(ml_dtypes.bfloat16), s1.astype(ml_dtypes.bfloat16)

    d["wg0"] = tile_proj(inp["L0_ffn_w_gate"])
    d["wu0"] = tile_proj(inp["L0_ffn_w_up"])
    d["wd0"] = tile_proj(inp["L0_ffn_w_down"])

    w_in = inp["L1_attn_w_in"]
    d["wq1"] = tile_proj(w_in[0:D])
    d["wk1"] = tile_proj(w_in[D:2 * D])
    d["wv1"] = tile_proj(w_in[2 * D:3 * D])
    d["wo1"] = tile_proj(inp["L1_attn_w_out"])

    d["wi1"] = tile_proj(inp["L1_ffn_W_in"])
    d["keysa"] = np.ascontiguousarray(inp["L1_ffn_keys_a"].transpose(0, 2, 1))
    d["keysb"] = np.ascontiguousarray(inp["L1_ffn_keys_b"].transpose(0, 2, 1))
    d["ush"] = np.ascontiguousarray(inp["L1_ffn_u_shared"].reshape(128, 1))
    d["expv"] = inp["L1_ffn_expert_v"]
    d["wout1"] = tile_proj(inp["L1_ffn_W_out"])
    d["ones_c"] = np.ones((128, 32), np.float32)
    d["iota16"] = np.broadcast_to(np.arange(16, dtype=np.float32)[None, :],
                                  (128, 16)).copy()
    d["ident_in"] = np.eye(128, dtype=np.float32)
    return d


_CACHED = {}


def kernel(**inputs):
    if "nc" not in _CACHED:
        nc, _ = build_program()
        nc.finalize()
        _CACHED["nc"] = nc
    nc = _CACHED["nc"]
    in_maps = [pack_inputs(c, inputs) for c in range(8)]
    res = run_bass_kernel_spmd(nc, in_maps, core_ids=list(range(8)))
    out = np.empty((16, T, D), np.float32)
    for c in range(8):
        o = res.results[c]["out"]
        o = o.reshape(NSEQ, D, T).transpose(0, 2, 1)
        out[2 * c:2 * c + 2] = o
    return out


# version 14 (final)
# speedup vs baseline: 1.8583x; 1.0053x over previous
import numpy as np
import concourse.bacc as bacc
import concourse.bass as bass
import concourse.mybir as mybir
from concourse.bass_utils import run_bass_kernel_spmd

F32 = mybir.dt.float32
BF16 = mybir.dt.bfloat16
U32 = mybir.dt.uint32
U16 = mybir.dt.uint16
I32 = mybir.dt.int32
AF = mybir.ActivationFunctionType
ALU = mybir.AluOpType

T = 2048
D = 1024
NC8 = 8
QB = 512
NQB = T // QB
NKC = T // 128
NSEQ = 2
NHB = 24
PKM_H = 8
PKM_S = 512
EPS = 1e-5
NTB = T // 128


class Sem:
    def __init__(self, name):
        self.name = name
        self.val = 0
        self.handle = None

    def inc(self, n=1):
        self.val += n
        return self.val


class Plan:
    AUTO_SELF = ("ve", "se", "gp")

    def __init__(self):
        self.ops = []
        self.sems = {}
        self.auto = {e: self.sem(e) for e in self.AUTO_SELF}

    def sem(self, name):
        s = self.sems.get(name)
        if s is None:
            s = Sem(name)
            self.sems[name] = s
        return s

    def emit(self, engine, fn):
        self.ops.append((engine, fn))

    def op(self, engine, build, sem=None, n=1):
        auto = self.auto.get(engine)
        prev = auto.val if auto is not None else 0
        if auto is not None:
            auto.inc(1)
            if sem is auto:
                sem = None
        if sem is not None:
            sem.inc(n)

        def fn(eng, build=build, sem=sem, n=n, auto=auto, prev=prev):
            if auto is not None and prev > 0:
                eng.wait_ge(auto.handle, prev)
            ins = build(eng)
            assert ins is not None, "builder must return its instruction"
            if auto is not None:
                ins.then_inc(auto.handle, 1)
            if sem is not None:
                ins.then_inc(sem.handle, n)
        self.emit(engine, fn)
        return sem.val if sem is not None else (auto.val if auto else None)

    def wait(self, engine, sem, val=None):
        if val is None:
            val = sem.val
        def fn(eng, sem=sem, val=val):
            eng.wait_ge(sem.handle, val)
        self.emit(engine, fn)

    def dma(self, engine, out, in_, sem, slow_ok=False):
        tgt = sem.inc(16)
        def fn(eng, out=out, in_=in_, sem=sem, slow_ok=slow_ok):
            eng.dma_start(out=out, in_=in_,
                          allow_slow_non_contiguous=slow_ok).then_inc(sem.handle, 16)
        self.emit(engine, fn)
        return tgt

    def indirect(self, out, in_, idx_ap, sem):
        tgt = sem.inc(16)
        def fn(eng, out=out, in_=in_, idx_ap=idx_ap, sem=sem):
            eng.indirect_dma_start(
                out=out, out_offset=None, in_=in_,
                in_offset=bass.IndirectOffsetOnAxis(ap=idx_ap, axis=0),
            ).then_inc(sem.handle, 16)
        self.emit("gp", fn)
        return tgt

    def run(self, nc):
        with nc.Block() as block:
            def make(engine_attr, key):
                def body(eng):
                    for e, fn in self.ops:
                        if e == key:
                            fn(eng)
                engine_attr(body)
            make(block.tensor, "te")
            make(block.vector, "ve")
            make(block.scalar, "se")
            make(block.gpsimd, "gp")
            make(block.sync, "sy")


def build_program(debug_names=()):
    nc = bacc.Bacc("TRN2", target_bir_lowering=False)
    p = Plan()
    from contextlib import ExitStack
    ctx = ExitStack()

    dram = lambda name, shape, dt=F32, kind="ExternalInput": nc.dram_tensor(
        name, shape, dt, kind=kind).ap()

    x_in = dram("x_in", [NSEQ, NC8, 128, T])
    out_hbm = dram("out", [NSEQ, NC8, 128, T], kind="ExternalOutput")
    x1scr = dram("x1scr", [NSEQ, NC8, 128, T], kind="Internal")
    yscr = dram("yscr", [16, 64, T], BF16, kind="Internal")

    wq0 = dram("wq0", [8, 128, NC8 * 128])
    wk0 = dram("wk0", [8, 128, NC8 * 128])
    wv0 = dram("wv0", [8, 128, NC8 * 128])
    wo0 = dram("wo0", [NC8, 128, NC8 * 128])
    bq0 = dram("bq0", [128, 8])
    bk0 = dram("bk0", [128, 8])
    bv0 = dram("bv0", [128, NC8 * 128])
    bo0 = dram("bo0", [128, NC8])
    cos0 = dram("cos0", [128, T], BF16)
    sin0 = dram("sin0", [128, T], BF16)
    wg0 = dram("wg0", [NHB, 128, NC8 * 128], I32)
    wu0 = dram("wu0", [NHB, 128, NC8 * 128], I32)
    wd0 = dram("wd0", [NC8, 128, NHB * 128], I32)
    wgs = dram("wgs", [NHB, 128, NC8 * 128], BF16, kind="Internal")
    wus = dram("wus", [NHB, 128, NC8 * 128], BF16, kind="Internal")
    wds = dram("wds", [NC8, 128, NHB * 128], BF16, kind="Internal")
    wq1 = dram("wq1", [8, 128, NC8 * 128])
    wk1 = dram("wk1", [8, 128, NC8 * 128])
    wv1 = dram("wv1", [8, 128, NC8 * 128])
    wo1 = dram("wo1", [NC8, 128, NC8 * 128])
    cos1 = dram("cos1", [128, T], BF16)
    sin1 = dram("sin1", [128, T], BF16)
    wi1 = dram("wi1", [8, 128, NC8 * 128])
    keysa = dram("keysa", [PKM_H, 128, PKM_S])
    keysb = dram("keysb", [PKM_H, 128, PKM_S])
    ush = dram("ush", [128, 1])
    expv = dram("expv", [PKM_S * PKM_S, 128])
    wout1 = dram("wout1", [NC8, 128, NC8 * 128])
    vecs_in = dram("vecs_in", [128, 64])
    ones_c = dram("ones_c", [128, 32])
    iota16_in = dram("iota16", [128, 16])

    dbg_specs = {
        "xn": [128, NC8 * T], "g": [128, NC8 * T], "y": [128, NC8 * T],
        "x1": [NC8, 128, T], "s1": [128, NC8 * T],
        "eids": [128, 8 * 16], "w16": [128, 8 * 16], "mgt": [128, 8 * 128],
        "hpkm": [128, NC8 * T],
    }
    dbg = {k: dram("dbg_" + k, dbg_specs[k], kind="ExternalOutput")
           for k in debug_names}

    sLG = p.sem("ldg")
    sLS = p.sem("lds")
    sST1 = p.sem("st1")
    sST2 = p.sem("st2")
    sWB0 = p.sem("wb0")
    sWB1 = p.sem("wb1")
    sWBD = p.sem("wbd")
    sWBD2 = p.sem("wbd2")
    sKA = p.sem("ka")
    sKB = p.sem("kb")
    sKA2 = p.sem("ka2")
    sKB2 = p.sem("kb2")
    sXS = p.sem("xs")
    sDBG = p.sem("dbg")
    sTE = p.sem("te_")
    sVE = p.auto["ve"]
    sSE = p.auto["se"]
    sGP = p.auto["gp"]
    sSTS = p.sem("sts")
    sIG = p.sem("ig")
    sIG2 = p.sem("ig2")

    sb = lambda name, shape, dt=BF16: ctx.enter_context(nc.sbuf_tensor(name, shape, dt))
    psum = lambda name, shape: ctx.enter_context(nc.psum_tensor(name, shape, F32))

    XN = sb("XN", [128, NC8 * T])
    GB = sb("GB", [128, NC8 * T])
    BIG = sb("BIG", [128, NC8 * T])
    F32A = sb("F32A", [128, T], F32)
    F32B = sb("F32B", [128, T], F32)
    F32C = sb("F32C", [128, T], F32)
    STG = sb("STG", [128, T], F32)
    STG2 = sb("STG2", [128, T], F32)
    QR = sb("QR", [128, T])
    KR = sb("KR", [128, T])
    QP = sb("QP", [128, T])
    KP = sb("KP", [128, T])
    VT = sb("VT", [128, 2 * NKC * 65])
    ATT = sb("ATT", [128, 2 * QB])
    ATT2 = sb("ATT2", [128, 2 * QB])
    SG = sb("SG", [128, QB])
    WB = sb("WB", [128, 4 * NC8 * 128])
    WBD = sb("WBD", [128, 2 * NHB * 128])
    COS = sb("COS", [128, T])
    SIN = sb("SIN", [128, T])
    SMALL = sb("SMALL", [128, 64], F32)
    BVB = sb("BVB", [128, NC8 * 128], F32)
    ONES1 = sb("ONES1", [128, 1], F32)
    ONESB = sb("ONESB", [128, 1])
    IOTA16 = sb("IOTA16", [128, 16], F32)
    KA = sb("KA", [128, 2 * PKM_S])
    KB_ = sb("KB_", [128, 2 * PKM_S])
    USH = sb("USH", [128, 1])
    ACTT = sb("ACTT", [128, NTB * 8], F32)
    SCF = sb("SCF", [128, 2 * PKM_S], F32)
    TAS = sb("TAS", [128, 8 * 16], F32)
    TBS = sb("TBS", [128, 8 * 16], F32)
    TAI = sb("TAI", [128, 8 * 16], U16)
    TBI = sb("TBI", [128, 8 * 16], U16)
    POS = sb("POS", [128, 8 * 16], U16)
    IJ2 = sb("IJ2", [128, 8 * 16], U16)
    EIDS = sb("EIDS", [128, 8 * 16], U32)
    W16 = sb("W16", [128, 8 * 16], F32)
    WSUM = sb("WSUM", [128, 8], F32)
    ident_in = dram("ident_in", [128, 128])

    PBIG = psum("PBIG", [128, 2048])
    PSC = psum("PSC", [128, 1024])
    PV = psum("PV", [128, 512])
    PST = psum("PST", [128, 512])

    CAND = STG[:, :]
    SCZ = GB[:, 0:1024].bitcast(F32)
    OH = GB[:, 0:4096].bitcast(F32)
    ATT3 = GB[:, 0:1024]
    ATT4 = GB[:, 1024:2048]
    GT = GB[:, 4096:6144]
    GT2 = GB[:, 8192:10240]
    MGT = GB[:, 6144:8192].bitcast(F32)
    MGT2 = GB[:, 10240:12288].bitcast(F32)
    F32V = GB[:, 12288:16384].bitcast(F32)
    TAIF = F32V[:, 0:128]
    TBIF = F32V[:, 128:256]
    TS16 = F32V[:, 256:384]
    IJF = F32V[:, 384:512]
    SLOTA = F32V[:, 512:640]
    EIDF = F32V[:, 640:768]
    CZ = F32V[:, 768:1024]
    EIDSb = GB[:, 14336:14592].bitcast(U32)
    W16b = F32V[:, 1152:1280]

    swapmask = []
    for i in range(0, 32, 2):
        swapmask += [i + 1, i]

    def mm(out_ap, lhsT_ap, rhs_ap, start, stop, sem=None):
        def b(eng, o=out_ap, l=lhsT_ap, r=rhs_ap, st=start, sp=stop):
            return eng.matmul(o, l, r, start=st, stop=sp)
        return p.op("te", b, sem)

    def cp(engine, out_ap, in_ap):
        def b(eng, o=out_ap, i=in_ap):
            return eng.tensor_copy(o, i)
        return p.op(engine, b)

    def act(out_ap, in_ap, func, bias=0.0, scale=1.0):
        def b(eng, o=out_ap, i=in_ap, f=func, bi=bias, sc=scale):
            return eng.activation(out=o, in_=i, func=f, bias=bi, scale=sc)
        return p.op("se", b)

    def tt(out_ap, a_ap, b_ap, alu, engine="ve"):
        def b(eng, o=out_ap, a=a_ap, bb=b_ap, alu=alu):
            return eng.tensor_tensor(o, a, bb, alu)
        return p.op(engine, b)

    def ts(out_ap, in_ap, s1, s2=None, op0=ALU.mult, op1=ALU.bypass, engine="ve"):
        def b(eng, o=out_ap, i=in_ap, s1=s1, s2=s2, op0=op0, op1=op1):
            return eng.tensor_scalar(out=o, in0=i, scalar1=s1, scalar2=s2,
                                     op0=op0, op1=op1)
        return p.op(engine, b)

    def stt(out_ap, in0_ap, scalar, in1_ap, op0, op1, engine="ve"):
        def b(eng, o=out_ap, i0=in0_ap, s=scalar, i1=in1_ap, op0=op0, op1=op1):
            return eng.scalar_tensor_tensor(out=o, in0=i0, scalar=s, in1=i1,
                                            op0=op0, op1=op1)
        return p.op(engine, b)

    def shuffle(out_ap, in_ap):
        def b(eng, o=out_ap, i=in_ap):
            return eng.stream_shuffle(out=o, in_=i, mask=swapmask)
        return p.op("ve", b)

    def vmax8(out_ap, in_ap):
        def b(eng, o=out_ap, i=in_ap):
            return eng.max(out=o, in_=i)
        return p.op("ve", b)

    def vmaxidx(out_ap, max_ap, in_ap):
        def b(eng, o=out_ap, m=max_ap, i=in_ap):
            return eng.max_index(out=o, in_max=m, in_values=i)
        return p.op("ve", b)

    def vmatchrep(out_ap, rep_ap, in_ap, immv):
        def b(eng, o=out_ap, r=rep_ap, i=in_ap, v=immv):
            return eng.match_replace(out=o, in_to_replace=r, in_values=i, imm_value=v)
        return p.op("ve", b)

    def reduce_sum_x(out_ap, in_ap):
        def b(eng, o=out_ap, i=in_ap):
            return eng.reduce_sum(o, i, mybir.AxisListType.X)
        return p.op("ve", b)

    def recip(out_ap, in_ap):
        def b(eng, o=out_ap, i=in_ap):
            return eng.reciprocal_approx_fast(o, i)
        return p.op("ve", b)

    def pbcast(out_ap, row_ap):
        def b(eng, o=out_ap, r=row_ap):
            return eng.partition_broadcast(o, r)
        return p.op("gp", b)

    def transpose_te(out_psum, in_ap, sem=None):
        def b(eng, o=out_psum, i=in_ap):
            return eng.transpose(o, i, BVB[:, 0:128])
        return p.op("te", b, sem)

    def dbg_dump(name, src_ap, f32stage=None):
        if name not in dbg:
            return
        p.wait("gp", sVE, sVE.val)
        p.wait("gp", sSE, sSE.val)
        p.wait("gp", sDBG, sDBG.val)
        p.dma("gp", dbg[name], src_ap, sDBG)
        p.wait("ve", sDBG, sDBG.val)

    p.dma("sy", SMALL[:], vecs_in, sLS)
    p.dma("sy", BVB[:], bv0, sLS)
    p.dma("sy", ONES1[:], ones_c[:, 0:1], sLS, slow_ok=True)
    p.dma("sy", IOTA16[:], iota16_in, sLS)
    p.dma("gp", VT[:].rearrange("p (a c) -> p a c", c=65)[:, :, 64:65],
          ones_c[:, 0:32].unsqueeze(2), sLG, slow_ok=True)
    p.dma("gp", USH[:], ush, sLG, slow_ok=True)
    p.dma("gp", ONESB[:], ones_c[:, 0:1], sLG, slow_ok=True)

    BQ0 = SMALL[:, 0:8]
    BK0 = SMALL[:, 8:16]
    BO0 = SMALL[:, 16:24]
    G1V = SMALL[:, 24:32]
    B1V = SMALL[:, 32:40]
    LNW1 = SMALL[:, 40:48]
    LNB1 = SMALL[:, 48:56]

    p.dma("gp", wgs, wg0, sLG)
    p.dma("gp", wus, wu0, sLG)
    p.dma("gp", wds, wd0, sLG)
    keysa_bf = dram("keysa_bf", [PKM_H, 128, PKM_S], BF16, kind="Internal")
    keysb_bf = dram("keysb_bf", [PKM_H, 128, PKM_S], BF16, kind="Internal")
    p.dma("gp", keysa_bf, keysa, sLG)
    keys_cast_mark = p.dma("gp", keysb_bf, keysb, sLG)


    def emit_ln_stats_finish():
        MS = STG2[0:1, :]
        p.wait("se", sTE, sTE.val)
        act(MS, PBIG[0:1, :], AF.Identity, scale=1.0 / D)
        act(F32C[0:1, :], PBIG[32:33, :], AF.Identity, scale=1.0 / D)
        p.wait("ve", sSE, sSE.val)
        tt(F32B[0:1, :], MS, MS, ALU.mult)
        tt(F32B[0:1, :], F32C[0:1, :], F32B[0:1, :], ALU.subtract)
        ts(F32B[0:1, :], F32B[0:1, :], float(EPS), None, ALU.add)
        recip(F32C[0:1, :], F32B[0:1, :])
        p.wait("se", sVE, sVE.val)
        act(F32C[0:1, :], F32C[0:1, :], AF.Sqrt)
        p.wait("ve", sSE, sSE.val)
        stt(F32B[0:1, :], MS, -1.0, F32C[0:1, :], ALU.mult, ALU.mult)
        p.wait("gp", sVE, sVE.val)
        p.wait("gp", sSE, sSE.val)
        p.wait("gp", sXS, sXS.val)
        pbcast(F32A[:], F32C[0:1, :])
        pbcast(F32B[:], F32B[0:1, :])

    def emit_stats_mms(src_f32_ap, c, qb=None, mark=False, ones=None):
        ones = ONES1 if ones is None else ones
        r = None
        if qb is None:
            for q in range(NQB):
                r = mm(PBIG[0:1, q * QB:(q + 1) * QB], ones[:],
                       src_f32_ap[:, q * QB:(q + 1) * QB],
                       start=(c == 0), stop=(c == NC8 - 1),
                       sem=sTE if (mark and q == NQB - 1) else None)
        else:
            r = mm(PBIG[0:1, qb * QB:(qb + 1) * QB], ones[:], src_f32_ap,
                   start=(c == 0), stop=(c == NC8 - 1),
                   sem=sTE if mark else None)
        return r

    def emit_stats_sq_mms(sq_f32_ap, c, qb=None, mark=False, ones=None):
        ones = ONES1 if ones is None else ones
        r = None
        if qb is None:
            for q in range(NQB):
                r = mm(PBIG[32:33, q * QB:(q + 1) * QB], ones[:],
                       sq_f32_ap[:, q * QB:(q + 1) * QB],
                       start=(c == 0), stop=(c == NC8 - 1),
                       sem=sTE if (mark and q == NQB - 1) else None)
        else:
            r = mm(PBIG[32:33, qb * QB:(qb + 1) * QB], ones[:], sq_f32_ap,
                   start=(c == 0), stop=(c == NC8 - 1),
                   sem=sTE if mark else None)
        return r

    def emit_x_stats(seq, src_dram):
        marks = {}
        p.wait("sy", sSE, sSE.val)
        p.wait("sy", sVE, sVE.val)
        p.wait("sy", sTE, sTE.val)
        p.wait("se", sSTS, sSTS.val)
        p.wait("ve", sSTS, sSTS.val)
        for c in range(NC8):
            S = STG if c % 2 == 0 else STG2
            sS = sST1 if c % 2 == 0 else sST2
            if c >= 2:
                p.wait("sy", sTE, marks[c - 2])
            t = p.dma("sy", S[:], src_dram[seq, c], sS)
            p.wait("se", sS, t)
            if c >= 1:
                p.wait("se", sTE, marks[c - 1])
            act(F32C[:], S[:], AF.Square)
            p.wait("te", sSE, sSE.val)
            p.wait("te", sS, t)
            emit_stats_mms(S, c, mark=False)
            marks[c] = emit_stats_sq_mms(F32C, c, mark=True)

    def emit_xn_apply(seq, src_dram):
        p.wait("ve", sGP, sGP.val)
        p.wait("sy", sGP, sGP.val)
        p.wait("sy", sSE, sSE.val)
        p.wait("sy", sVE, sVE.val)
        p.wait("sy", sTE, sTE.val)
        for c in range(NC8):
            S = STG if c % 2 == 0 else STG2
            sS = sST1 if c % 2 == 0 else sST2
            if c >= 2:
                p.wait("sy", sVE, sVE.val)
            t = p.dma("sy", S[:], src_dram[seq, c], sS)
            p.wait("ve", sS, t)
            tt(F32C[:], S[:], F32A[:], ALU.mult)
            tt(XN[:, c * T:(c + 1) * T], F32C[:], F32B[:], ALU.add)

    def emit_ffn0(seq):
        SCALE = 1.0 / 127.0
        p.wait("gp", sLG, sLG.val)
        wsem = {0: sWB0, 1: sWB1}
        for qb in range(NQB):
            xn_sl = lambda c: XN[:, c * T + qb * QB: c * T + qb * QB + QB]
            hb_te_done = {}
            load_marks = {}

            def issue_pair(hb):
                pair = hb % 2
                if hb >= 2:
                    p.wait("gp", sTE, hb_te_done[hb - 2])
                else:
                    p.wait("gp", sTE, sTE.val)
                t1 = p.dma("gp", WB[:, pair * 2048:pair * 2048 + 1024], wgs[hb],
                           wsem[pair])
                t2 = p.dma("gp", WB[:, pair * 2048 + 1024:pair * 2048 + 2048],
                           wus[hb], wsem[pair])
                load_marks[hb] = t2

            issue_pair(0)
            if NHB > 1:
                issue_pair(1)
            for hb in range(NHB):
                pair = hb % 2
                wg_sl = WB[:, pair * 2048:pair * 2048 + 1024]
                wu_sl = WB[:, pair * 2048 + 1024:pair * 2048 + 2048]
                psl = PSC[:, (hb % 2) * QB:(hb % 2) * QB + QB]
                p.wait("te", wsem[pair], load_marks[hb])
                p.wait("te", sSE, sSE.val)
                p.wait("te", sVE, sVE.val)
                for c in range(NC8):
                    mm(psl, wg_sl[:, c * 128:(c + 1) * 128], xn_sl(c),
                       start=(c == 0), stop=(c == NC8 - 1),
                       sem=sTE if c == NC8 - 1 else None)
                p.wait("se", sTE, sTE.val)
                act(SG[:], psl, AF.Silu, scale=SCALE)
                p.wait("te", sSE, sSE.val)
                for c in range(NC8):
                    mm(psl, wu_sl[:, c * 128:(c + 1) * 128], xn_sl(c),
                       start=(c == 0), stop=(c == NC8 - 1),
                       sem=sTE if c == NC8 - 1 else None)
                hb_te_done[hb] = sTE.val
                if hb + 2 < NHB:
                    issue_pair(hb + 2)
                p.wait("ve", sTE, sTE.val)
                p.wait("ve", sSE, sSE.val)
                def hb_mult(eng, hb=hb, psl=psl):
                    return eng.scalar_tensor_tensor(
                        out=BIG[:, hb * QB:(hb + 1) * QB], in0=psl,
                        scalar=SCALE, in1=SG[:], op0=ALU.mult, op1=ALU.mult)
                p.op("ve", hb_mult)
            dn_te_done = {}
            dn_marks = {}
            def issue_down(ob):
                if ob >= 2:
                    p.wait("gp", sTE, dn_te_done[ob - 2])
                else:
                    p.wait("gp", sTE, hb_te_done[NHB - 1] if ob == 0 else dn_marks.get("first_gate", sTE.val))
                t = p.dma("gp", WBD[:, (ob % 2) * NHB * 128:(ob % 2 + 1) * NHB * 128],
                          wds[ob], sWBD if ob % 2 == 0 else sWBD2)
                dn_marks[ob] = t
            issue_down(0)
            issue_down(1)
            for ob in range(NC8):
                wd_sl = WBD[:, (ob % 2) * NHB * 128:(ob % 2 + 1) * NHB * 128]
                p.wait("te", sWBD if ob % 2 == 0 else sWBD2, dn_marks[ob])
                p.wait("te", sVE, sVE.val)
                p.wait("te", sSE, sSE.val)
                for hb in range(NHB):
                    mm(PV[:], wd_sl[:, hb * 128:(hb + 1) * 128],
                       BIG[:, hb * QB:(hb + 1) * QB],
                       start=(hb == 0), stop=(hb == NHB - 1),
                       sem=sTE if hb == NHB - 1 else None)
                dn_te_done[ob] = sTE.val
                if ob + 2 < NC8:
                    issue_down(ob + 2)
                p.wait("se", sTE, sTE.val)
                act(GB[:, ob * T + qb * QB: ob * T + qb * QB + QB],
                    PV[:], AF.Silu, scale=SCALE)
        attn_state["wb_free"][0] = attn_state["wb_free"][1] = sTE.val

    attn_state = {"te_after_vproj": 0, "wb_free": {0: 0, 1: 0}, "marks": {},
                  "bank_free": {0: 0, 1: 0}, "proj_evac": 0, "rope_done": 0,
                  "evv": 0, "den_free": 0, "norm_done": 0}

    def attn_issue_loads(layer, hp):
        if layer == 0:
            wq, wk, wv = wq0, wk0, wv0
        else:
            wq, wk, wv = wq1, wk1, wv1
        par = hp % 2
        p.wait("gp", sTE, attn_state["wb_free"][par])
        base = par * 2048
        tq = p.dma("gp", WB[:, base:base + 1024], wq[hp], sWB0 if par == 0 else sWB1)
        tk = p.dma("gp", WB[:, base + 1024:base + 2048], wk[hp],
                   sWB0 if par == 0 else sWB1)
        tv = p.dma("gp", WBD[:, par * NHB * 128:par * NHB * 128 + 1024], wv[hp],
                   sWBD if par == 0 else sWBD2)
        attn_state["marks"][hp] = (tk, tv)

    def emit_attn_hp(seq, layer, hp):
        if layer == 0:
            wq, wk, wv = wq0, wk0, wv0
            plain = False
        else:
            wq, wk, wv = wq1, wk1, wv1
            plain = True
            p.wait("se", sVE, sVE.val)
        scale = 0.125

        if layer == 0 and hp == 0:
            p.wait("gp", sVE, sVE.val)
            p.wait("gp", sTE, sTE.val)
            tbv = p.dma("gp", BVB[:], bv0, sLG)
            p.wait("ve", sLG, tbv)
        if hp == 0:
            attn_issue_loads(layer, 0)
        par = hp % 2
        tk, tv = attn_state["marks"][hp]
        p.wait("te", sWB0 if par == 0 else sWB1, tk)
        p.wait("te", sWBD if par == 0 else sWBD2, tv)
        base = par * 2048

        for which in ("q", "k"):
            W = WB[:, base:base + 1024] if which == "q" else WB[:, base + 1024:base + 2048]
            OUTR = QR if which == "q" else KR
            OUTP = QP if which == "q" else KP
            p.wait("te", sVE, attn_state["rope_done"])
            for bk in (0, 1):
                if attn_state["bank_free"][bk] > 0:
                    p.wait("te", sVE, attn_state["bank_free"][bk])
            for half in range(2):
                p.wait("te", sSE, attn_state["proj_evac"])
                if attn_state["norm_done"] > 0:
                    p.wait("se", sVE, attn_state["norm_done"])
                for qs_ in range(2):
                    col = half * 1024 + qs_ * QB
                    for c in range(NC8):
                        mm(PBIG[:, qs_ * QB:(qs_ + 1) * QB],
                           W[:, c * 128:(c + 1) * 128],
                           XN[:, c * T + col: c * T + col + QB],
                           start=(c == 0), stop=(c == NC8 - 1),
                           sem=sTE if c == NC8 - 1 else None)
                p.wait("se", sTE, sTE.val)
                if layer == 0:
                    B = BQ0[:, hp:hp + 1] if which == "q" else BK0[:, hp:hp + 1]
                    m = act(F32C[:, half * 1024:(half + 1) * 1024], PBIG[:, 0:1024],
                            AF.Identity, bias=B)
                else:
                    m = act(F32C[:, half * 1024:(half + 1) * 1024], PBIG[:, 0:1024],
                            AF.Identity)
                attn_state["proj_evac"] = m
            p.wait("ve", sSE, attn_state["proj_evac"])
            if plain:
                cp("ve", OUTP[:], F32C[:])
            shuffle(F32B[:], F32C[:])
            tt(F32C[:], F32C[:], COS[:], ALU.mult)
            tt(F32B[:], F32B[:], SIN[:], ALU.mult)
            attn_state["rope_done"] = tt(OUTR[:], F32C[:], F32B[:], ALU.add)

        vbase = par * NHB * 128
        for kc in range(NKC):
            if attn_state["evv"] > 0:
                p.wait("te", sVE, attn_state["evv"])
            for c in range(NC8):
                mm(PV[:, 0:128],
                   XN[:, c * T + kc * 128: c * T + kc * 128 + 128],
                   WBD[:, vbase + c * 128:vbase + (c + 1) * 128],
                   start=(c == 0), stop=(c == NC8 - 1),
                   sem=sTE if c == NC8 - 1 else None)
            p.wait("ve", sTE, sTE.val)
            dst = VT[:].rearrange("p (h k c) -> p h k c", h=2, k=NKC)[:, :, kc, 0:64]
            if layer == 0:
                attn_state["evv"] = tt(
                    dst, PV[:, 0:128].rearrange("p (h c) -> p h c", h=2),
                    BVB[:, hp * 128:(hp + 1) * 128].rearrange("p (h c) -> p h c", h=2),
                    ALU.add)
            else:
                attn_state["evv"] = cp(
                    "ve", dst, PV[:, 0:128].rearrange("p (h c) -> p h c", h=2))

        p.wait("te", sVE, attn_state["evv"])
        attn_state["te_after_vproj"] = sTE.val
        attn_state["wb_free"][par] = sTE.val
        if hp + 1 < 8:
            attn_issue_loads(layer, hp + 1)

        for h01 in range(2):
            h = hp * 2 + h01
            rows = slice(h01 * 64, h01 * 64 + 64)
            for qb in range(NQB):
                bank = qb % 2
                avsl = PBIG[:, bank * QB:(bank + 1) * QB]
                expR = {}
                expP = {}
                NPAIR = NKC // 2
                if attn_state["bank_free"][bank] > 0:
                    p.wait("te", sVE, attn_state["bank_free"][bank])
                if plain and attn_state["den_free"] > 0:
                    p.wait("te", sVE, attn_state["den_free"])
                def av_pair(pr, qb=qb, h01=h01, avsl=avsl, expR=expR, expP=expP):
                    p.wait("te", sSE, (expP if plain else expR)[pr])
                    att_r = ATT if pr % 2 == 0 else ATT2
                    att_p = ATT3 if pr % 2 == 0 else ATT4
                    for j in range(2):
                        kc = 2 * pr + j
                        vt_base = (h01 * NKC + kc) * 65
                        if not plain:
                            mm(avsl[0:65, :],
                               VT[:, vt_base:vt_base + 65],
                               att_r[:, j * QB:(j + 1) * QB],
                               start=(kc == 0), stop=(kc == NKC - 1),
                               sem=sTE if kc == NKC - 1 else None)
                        else:
                            mm(avsl[0:64, :],
                               VT[:, vt_base:vt_base + 64],
                               att_r[:, j * QB:(j + 1) * QB],
                               start=(kc == 0), stop=(kc == NKC - 1))
                            mm(PBIG[0:1, 1024:1024 + QB],
                               VT[:, vt_base + 64:vt_base + 65],
                               att_p[:, j * QB:(j + 1) * QB],
                               start=(kc == 0), stop=(kc == NKC - 1),
                               sem=sTE if kc == NKC - 1 else None)
                if not plain:
                    PB23 = PBIG[:, 1024:2048]
                    for pr in range(NPAIR):
                        k0 = 2 * pr
                        psl = PSC[:, :] if pr % 2 == 0 else PB23
                        if pr >= 2:
                            p.wait("te", sSE, expR[pr - 2])
                        mm(psl[:, 0:QB], KR[rows, k0 * 128:k0 * 128 + 128],
                           QR[rows, qb * QB:qb * QB + QB], True, True)
                        mm(psl[:, QB:2 * QB],
                           KR[rows, k0 * 128 + 128:k0 * 128 + 256],
                           QR[rows, qb * QB:qb * QB + QB], True, True, sem=sTE)
                        p.wait("se", sTE, sTE.val)
                        def expf(eng, psl=psl, dst=(ATT if pr % 2 == 0 else ATT2)):
                            return eng.activation(out=dst[:, :], in_=psl[:, :],
                                                  func=AF.Exp, scale=scale)
                        expR[pr] = p.op("se", expf)
                        if pr >= 1:
                            av_pair(pr - 1)
                    av_pair(NPAIR - 1)
                    p.wait("ve", sTE, sTE.val)
                    evm = cp("ve", F32B[0:64, qb * QB:(qb + 1) * QB], avsl[0:64, :])
                    evm = cp("ve", STG2[0:1, qb * QB:(qb + 1) * QB], avsl[64:65, :])
                    attn_state["bank_free"][bank] = evm
                    pkm_feed(4)
                    continue
                for pr in range(NPAIR):
                    k0 = 2 * pr
                    if pr >= 1:
                        p.wait("te", sSE, expR[pr - 1])
                    mm(PSC[:, 0:QB], KR[rows, k0 * 128:k0 * 128 + 128],
                       QR[rows, qb * QB:qb * QB + QB], True, True)
                    mm(PSC[:, QB:2 * QB],
                       KR[rows, k0 * 128 + 128:k0 * 128 + 256],
                       QR[rows, qb * QB:qb * QB + QB], True, True, sem=sTE)
                    p.wait("se", sTE, sTE.val)
                    def expf(eng, dst=(ATT if pr % 2 == 0 else ATT2)):
                        return eng.activation(out=dst[:, :], in_=PSC[:],
                                              func=AF.Exp, scale=scale)
                    expR[pr] = p.op("se", expf)
                    if pr >= 1:
                        p.wait("te", sSE, expP[pr - 1])
                    mm(PST[:, 0:QB], KP[rows, k0 * 128:k0 * 128 + 128],
                       QP[rows, qb * QB:qb * QB + QB], True, True)
                    mm(PV[:, 0:QB],
                       KP[rows, k0 * 128 + 128:k0 * 128 + 256],
                       QP[rows, qb * QB:qb * QB + QB], True, True, sem=sTE)
                    p.wait("se", sTE, sTE.val)
                    attp = ATT3 if pr % 2 == 0 else ATT4
                    def expf2a(eng, dst=attp):
                        return eng.activation(out=dst[:, 0:QB], in_=PST[:, :],
                                              func=AF.Exp, scale=scale)
                    p.op("se", expf2a)
                    def expf2b(eng, dst=attp):
                        return eng.activation(out=dst[:, QB:2 * QB], in_=PV[:, :],
                                              func=AF.Exp, scale=scale)
                    expP[pr] = p.op("se", expf2b)
                    if pr >= 1:
                        av_pair(pr - 1)
                av_pair(NPAIR - 1)
                p.wait("ve", sTE, sTE.val)
                evm = cp("ve", F32B[0:64, qb * QB:(qb + 1) * QB], avsl[0:64, :])
                if plain:
                    attn_state["den_free"] = cp(
                        "ve", STG2[0:1, qb * QB:(qb + 1) * QB], PBIG[0:1, 1024:1024 + QB])
                else:
                    evm = cp("ve", STG2[0:1, qb * QB:(qb + 1) * QB], avsl[64:65, :])
                attn_state["bank_free"][bank] = evm if not plain else max(
                    evm, attn_state["den_free"]) if False else evm
                pkm_feed(4)
            rm = recip(STG2[0:1, :], STG2[0:1, :])
            p.wait("gp", sVE, rm)
            pb = pbcast(F32C[:], STG2[0:1, :])
            p.wait("ve", sGP, pb)
            nm = tt(F32B[0:64, :], F32B[0:64, :], F32C[0:64, :], ALU.mult)
            attn_state["norm_done"] = nm
            if layer == 0:
                ychunk = h // 2
                yrows = slice((h % 2) * 64, (h % 2) * 64 + 64)
                cp("ve", BIG[yrows, ychunk * T:(ychunk + 1) * T], F32B[0:64, :])
            else:
                p.wait("gp", sVE, nm)
                p.dma("gp", yscr[h], F32B[0:64, :], sXS)
                p.wait("ve", sXS, sXS.val)

    def emit_wo_update0(seq):
        for ob in range(NC8):
            p.wait("gp", sTE, sTE.val)
            tw = p.dma("gp", WB[:, 0:1024], wo0[ob], sWB0)
            p.wait("te", sWB0, tw)
            for qb in range(NQB):
                if not (ob == 0 and qb == 0):
                    p.wait("sy", sVE, sVE.val)
                else:
                    p.wait("sy", sVE, sVE.val)
                    p.wait("sy", sSE, sSE.val)
                    p.wait("sy", sTE, sTE.val)
                t0 = p.dma("sy", STG[:, 0:QB],
                           x_in[seq, ob][:, qb * QB:(qb + 1) * QB], sST1)
                p.wait("te", sSE, sSE.val)
                p.wait("te", sVE, sVE.val)
                for c in range(NC8):
                    mm(PV[:], WB[:, c * 128:(c + 1) * 128],
                       BIG[:, c * T + qb * QB: c * T + qb * QB + QB],
                       start=(c == 0), stop=(c == NC8 - 1),
                       sem=sTE if c == NC8 - 1 else None)
                p.wait("se", sTE, sTE.val)
                act(F32C[:, 0:QB], PV[:], AF.Identity, bias=BO0[:, ob:ob + 1])
                p.wait("ve", sSE, sSE.val)
                p.wait("ve", sST1, t0)
                p.wait("ve", sTE, sTE.val)
                p.wait("ve", sXS, sXS.val)
                tt(F32A[:, 0:QB], F32C[:, 0:QB],
                   GB[:, ob * T + qb * QB: ob * T + qb * QB + QB], ALU.mult)
                tt(F32A[:, 0:QB], F32A[:, 0:QB], STG[:, 0:QB], ALU.add)
                p.wait("gp", sVE, sVE.val)
                p.dma("gp", x1scr[seq, ob][:, qb * QB:(qb + 1) * QB],
                      F32A[:, 0:QB], sXS)
                p.wait("se", sVE, sVE.val)
                act(F32B[:, 0:QB], F32A[:, 0:QB], AF.Square)
                p.wait("te", sSE, sSE.val)
                emit_stats_mms(F32A[:, 0:QB], ob, qb)
                emit_stats_sq_mms(F32B[:, 0:QB], ob, qb, mark=True)
        attn_state["wb_free"][0] = attn_state["wb_free"][1] = sTE.val

    def emit_pkm_hproj(seq, dbg_on=False):
        p.wait("gp", sVE, sVE.val)
        p.wait("gp", sTE, sTE.val)
        tid = p.dma("gp", BVB[:, 0:128], ident_in, sLG)
        p.wait("te", sLG, tid)
        for hd in range(PKM_H):
            par = hd % 2
            p.wait("gp", sTE, sTE.val)
            tw = p.dma("gp", WB[:, par * 1024:(par + 1) * 1024], wi1[hd],
                       sWB0 if par == 0 else sWB1)
            p.wait("te", sWB0 if par == 0 else sWB1, tw)
            for qb in range(NQB):
                p.wait("te", sSE, sSE.val)
                for c in range(NC8):
                    mm(PV[:], WB[:, par * 1024 + c * 128:par * 1024 + (c + 1) * 128],
                       XN[:, c * T + qb * QB: c * T + qb * QB + QB],
                       start=(c == 0), stop=(c == NC8 - 1),
                       sem=sTE if c == NC8 - 1 else None)
                p.wait("se", sTE, sTE.val)
                act(BIG[:, hd * T + qb * QB: hd * T + qb * QB + QB], PV[:],
                    AF.Identity)
        attn_state["wb_free"][0] = attn_state["wb_free"][1] = sTE.val
        if dbg_on:
            dbg_dump("hpkm", BIG[:])

    def emit_pkm_ugate(seq):
        p.wait("te", sSE, sSE.val)
        for tb in range(NTB):
            for hd in range(PKM_H):
                mm(PST[:, hd:hd + 1],
                   BIG[:, hd * T + tb * 128: hd * T + tb * 128 + 128],
                   USH[:], start=True, stop=True,
                   sem=sTE if hd == PKM_H - 1 else None)
            p.wait("se", sTE, sTE.val)
            act(ACTT[:, tb * 8:(tb + 1) * 8], PST[:, 0:8], AF.Sigmoid)
            p.wait("te", sSE, sSE.val)

    pkm_state = {"keys_free": {0: 0, 1: 0}, "slot": 0, "evac": {0: 0, 1: 0},
                 "topk": {0: 0, 1: 0}, "gt_comb": {0: 0, 1: 0}, "comb": {},
                 "ig_mark": {}, "ig_done_tb": {0: None, 1: None},
                 "tp_mark": {0: 0, 1: 0}, "pending": [], "combq": []}

    def emit_pkm_tb(seq, tb, dbg_on=False):
        for hd in range(PKM_H):
            ksl = hd % 2
            if pkm_state["keys_free"][ksl] > 0:
                p.wait("sy", sTE, pkm_state["keys_free"][ksl])
            ka = KA[:, ksl * PKM_S:(ksl + 1) * PKM_S]
            kb = KB_[:, ksl * PKM_S:(ksl + 1) * PKM_S]
            semA = sKA if ksl == 0 else sKA2
            semB = sKB if ksl == 0 else sKB2
            tk1 = p.dma("sy", ka, keysa_bf[hd], semA)
            tk2 = p.dma("sy", kb, keysb_bf[hd], semB)
            p.wait("te", semA, tk1)
            p.wait("te", semB, tk2)
            hsl = BIG[:, hd * T + tb * 128: hd * T + tb * 128 + 128]
            for ab in range(2):
                K = ka if ab == 0 else kb
                sl = pkm_state["slot"]
                pkm_state["slot"] ^= 1
                scf_sl = SCF[:, sl * PKM_S:(sl + 1) * PKM_S]
                if pkm_state["evac"][sl ^ 1] > 0:
                    p.wait("te", sSE, pkm_state["evac"][sl ^ 1])
                if pkm_state["topk"][sl] > 0:
                    p.wait("se", sVE, pkm_state["topk"][sl])
                sc_mark = mm(PBIG[:, 1536:2048], hsl, K, start=True, stop=True,
                             sem=sTE)
                if ab == 1:
                    pkm_state["keys_free"][ksl] = sc_mark
                p.wait("se", sTE, sc_mark)
                pkm_state["evac"][sl] = act(scf_sl, PBIG[:, 1536:2048], AF.Identity)
                p.wait("ve", sSE, pkm_state["evac"][sl])
                TS_ = TAS if ab == 0 else TBS
                TI_ = TAI if ab == 0 else TBI
                vmax8(TS_[:, hd * 16:hd * 16 + 8], scf_sl)
                vmaxidx(TI_[:, hd * 16:hd * 16 + 8],
                        TS_[:, hd * 16:hd * 16 + 8], scf_sl)
                vmatchrep(SCZ[:], TS_[:, hd * 16:hd * 16 + 8], scf_sl, -3e38)
                vmax8(TS_[:, hd * 16 + 8:hd * 16 + 16], SCZ[:])
                pkm_state["topk"][sl] = vmaxidx(
                    TI_[:, hd * 16 + 8:hd * 16 + 16],
                    TS_[:, hd * 16 + 8:hd * 16 + 16], SCZ[:])
        emit_pkm_stage_c(seq, tb, dbg_on)

    def emit_pkm_wout(seq):
        p.wait("te", sVE, sVE.val)
        wout_marks = {}
        wout_done = {}
        def issue_wout(ob):
            if ob >= 2:
                p.wait("gp", sTE, wout_done[ob - 2])
            else:
                p.wait("gp", sTE, sTE.val)
            t = p.dma("gp", WB[:, (ob % 2) * 1024:(ob % 2) * 1024 + 1024],
                      wout1[ob], sWB0 if ob % 2 == 0 else sWB1)
            wout_marks[ob] = t
        issue_wout(0)
        issue_wout(1)
        acts = {}
        it = 0
        for ob in range(NC8):
            p.wait("te", sWB0 if ob % 2 == 0 else sWB1, wout_marks[ob])
            for qb in range(NQB):
                psl = PV[:, :] if it % 2 == 0 else PST[:, :]
                if it >= 2:
                    p.wait("te", sSE, acts[it - 2])
                for c in range(NC8):
                    mm(psl, WB[:, (ob % 2) * 1024 + c * 128:(ob % 2) * 1024 + (c + 1) * 128],
                       BIG[:, c * T + qb * QB: c * T + qb * QB + QB],
                       start=(c == 0), stop=(c == NC8 - 1),
                       sem=sTE if c == NC8 - 1 else None)
                p.wait("se", sTE, sTE.val)
                acts[it] = act(GB[:, ob * T + qb * QB: ob * T + qb * QB + QB], psl,
                               AF.Identity)
                it += 1
            wout_done[ob] = sTE.val
            if ob + 2 < NC8:
                issue_wout(ob + 2)
        attn_state["wb_free"][0] = attn_state["wb_free"][1] = sTE.val

    def emit_pkm_stage_c(seq, tb, dbg_on=False):
        eids_t = EIDS if tb % 2 == 0 else EIDSb
        w16_t = W16 if tb % 2 == 0 else W16b
        prev_ig = pkm_state["ig_done_tb"][tb % 2]
        if prev_ig is not None:
            p.wait("ve", sIG, prev_ig[0])
            p.wait("ve", sIG2, prev_ig[1])
        cp("ve", TAIF[:], TAI[:])
        cp("ve", TBIF[:], TBI[:])
        tt(CAND[:].rearrange("p (h i j) -> p h i j", h=8, i=16),
           TAS[:].rearrange("p (h i) -> p h i", h=8).unsqueeze(3).to_broadcast(
               [128, 8, 16, 16]),
           TBS[:].rearrange("p (h j) -> p h j", h=8).unsqueeze(2).to_broadcast(
               [128, 8, 16, 16]),
           ALU.add)
        for hd in range(PKM_H):
            csl = CAND[:, hd * 256:(hd + 1) * 256]
            vmax8(TS16[:, hd * 16:hd * 16 + 8], csl)
            vmaxidx(POS[:, hd * 16:hd * 16 + 8], TS16[:, hd * 16:hd * 16 + 8], csl)
            vmatchrep(CZ[:], TS16[:, hd * 16:hd * 16 + 8], csl, -3e38)
            vmax8(TS16[:, hd * 16 + 8:hd * 16 + 16], CZ[:])
            vmaxidx(POS[:, hd * 16 + 8:hd * 16 + 16],
                    TS16[:, hd * 16 + 8:hd * 16 + 16], CZ[:])
        ts(IJ2[:], POS[:], 4, None, ALU.logical_shift_right)
        cp("ve", IJF[:], IJ2[:])
        tt(OH[:].rearrange("p (h m i) -> p h m i", h=8, m=16),
           IJF[:].rearrange("p (h m) -> p h m", h=8).unsqueeze(3).to_broadcast(
               [128, 8, 16, 16]),
           IOTA16[:].unsqueeze(1).unsqueeze(1).to_broadcast([128, 8, 16, 16]),
           ALU.is_equal)
        tt(OH[:].rearrange("p (h m i) -> p h m i", h=8, m=16),
           OH[:].rearrange("p (h m i) -> p h m i", h=8, m=16),
           TAIF[:].rearrange("p (h i) -> p h i", h=8).unsqueeze(2).to_broadcast(
               [128, 8, 16, 16]),
           ALU.mult)
        reduce_sum_x(SLOTA[:].rearrange("p (h m) -> p h m", h=8),
                     OH[:].rearrange("p (h m i) -> p h m i", h=8, m=16))
        ts(EIDF[:], SLOTA[:], 512.0, None, ALU.mult)
        ts(IJ2[:], POS[:], 15, None, ALU.bitwise_and)
        cp("ve", IJF[:], IJ2[:])
        tt(OH[:].rearrange("p (h m i) -> p h m i", h=8, m=16),
           IJF[:].rearrange("p (h m) -> p h m", h=8).unsqueeze(3).to_broadcast(
               [128, 8, 16, 16]),
           IOTA16[:].unsqueeze(1).unsqueeze(1).to_broadcast([128, 8, 16, 16]),
           ALU.is_equal)
        tt(OH[:].rearrange("p (h m i) -> p h m i", h=8, m=16),
           OH[:].rearrange("p (h m i) -> p h m i", h=8, m=16),
           TBIF[:].rearrange("p (h i) -> p h i", h=8).unsqueeze(2).to_broadcast(
               [128, 8, 16, 16]),
           ALU.mult)
        reduce_sum_x(SLOTA[:].rearrange("p (h m) -> p h m", h=8),
                     OH[:].rearrange("p (h m i) -> p h m i", h=8, m=16))
        tt(EIDF[:], EIDF[:], SLOTA[:], ALU.add)
        cp("ve", eids_t[:, :], EIDF[:])
        negm = ts(WSUM[:], TS16[:].rearrange("p (h m) -> p h m", h=8)[:, :, 0:1]
                  .rearrange("p h m -> p (h m)"), -1.0, None, ALU.mult)
        p.wait("se", sVE, negm)
        em = 0
        for hd in range(PKM_H):
            em = act(w16_t[:, hd * 16:(hd + 1) * 16], TS16[:, hd * 16:(hd + 1) * 16],
                     AF.Exp, bias=WSUM[:, hd:hd + 1])
        p.wait("ve", sSE, em)
        reduce_sum_x(WSUM[:].unsqueeze(2), w16_t[:, :].rearrange("p (h m) -> p h m", h=8))
        recip(WSUM[:], WSUM[:])
        tt(w16_t[:, :].rearrange("p (h m) -> p h m", h=8),
           w16_t[:, :].rearrange("p (h m) -> p h m", h=8),
           WSUM[:].unsqueeze(2).to_broadcast([128, 8, 16]), ALU.mult)
        wready = tt(w16_t[:, :].rearrange("p (h m) -> p h m", h=8),
                    w16_t[:, :].rearrange("p (h m) -> p h m", h=8),
                    ACTT[:, tb * 8:(tb + 1) * 8].unsqueeze(2).to_broadcast([128, 8, 16]),
                    ALU.mult)
        if dbg_on and tb == 0:
            dbg_dump("eids", eids_t[:, :])
            dbg_dump("w16", w16_t[:, :])
        for hd in range(PKM_H):
            def iss(hd=hd, tb=tb, wr=wready, eids_t=eids_t):
                gts = GT if hd % 2 == 0 else GT2
                igsem = sIG if hd % 2 == 0 else sIG2
                if hd == 0:
                    p.wait("gp", sVE, wr)
                if pkm_state["gt_comb"][hd % 2] > 0:
                    p.wait("gp", sVE, pkm_state["gt_comb"][hd % 2])
                for m in range(16):
                    p.indirect(gts[:, m * 128:(m + 1) * 128], expv,
                               eids_t[:, hd * 16 + m:hd * 16 + m + 1], igsem)
                pkm_state["ig_mark"][(tb % 2, hd)] = igsem.val
                if hd == PKM_H - 1:
                    pkm_state["ig_done_tb"][tb % 2] = (sIG.val, sIG2.val)

            def comb(hd=hd, tb=tb, w16_t=w16_t):
                gts = GT if hd % 2 == 0 else GT2
                igsem = sIG if hd % 2 == 0 else sIG2
                mgt = MGT if tb % 2 == 0 else MGT2
                if hd == 0 and pkm_state["tp_mark"][tb % 2] > 0:
                    p.wait("ve", sTE, pkm_state["tp_mark"][tb % 2])
                p.wait("ve", igsem, pkm_state["ig_mark"][(tb % 2, hd)])
                ts(mgt[:, hd * 128:(hd + 1) * 128], gts[:, 0:128],
                   w16_t[:, hd * 16:hd * 16 + 1], None, ALU.mult)
                cm = 0
                for m in range(1, 16):
                    cm = stt(mgt[:, hd * 128:(hd + 1) * 128],
                             gts[:, m * 128:(m + 1) * 128],
                             w16_t[:, hd * 16 + m:hd * 16 + m + 1],
                             mgt[:, hd * 128:(hd + 1) * 128], ALU.mult, ALU.add)
                pkm_state["gt_comb"][hd % 2] = cm
                pkm_state["comb"][tb] = cm

            pend = pkm_state["pending"]
            pend.append(iss)
            pkm_state["combq"].append(comb)
            if len(pkm_state["combq"]) > 1:
                pend.append(pkm_state["combq"].pop(0))
        if tb % 2 == 1:
            pend = pkm_state["pending"]
            while pkm_state["combq"]:
                pend.append(pkm_state["combq"].pop(0))

    def pkm_feed(n=1):
        for _ in range(n):
            if pkm_state["pending"]:
                pkm_state["pending"].pop(0)()

    def pkm_flush():
        while pkm_state["pending"]:
            pkm_state["pending"].pop(0)()

    def emit_pkm_transposes(seq, tb):
        mgt = MGT if tb % 2 == 0 else MGT2
        p.wait("te", sVE, pkm_state["comb"][tb])
        p.wait("te", sSE, sSE.val)
        tp_evac = pkm_state.setdefault("tp_evac", {0: 0, 1: 0})
        tpm = 0
        for hd in range(PKM_H):
            psl = PST[:, 0:128] if hd % 2 == 0 else PV[:, 0:128]
            if tp_evac[hd % 2] > 0:
                p.wait("te", sVE, tp_evac[hd % 2])
            tpm = transpose_te(psl, mgt[:, hd * 128:(hd + 1) * 128], sem=sTE)
            p.wait("ve", sTE, tpm)
            tp_evac[hd % 2] = cp(
                "ve", BIG[:, hd * T + tb * 128: hd * T + tb * 128 + 128], psl)
        pkm_state["tp_mark"][tb % 2] = tpm
        attn_state["evv"] = tp_evac[1]

    def emit_pkm_lnsilu(seq):
        marks = {}
        p.wait("te", sSE, sSE.val)
        for c in range(NC8):
            p.wait("se", sVE, sVE.val)
            if c >= 1:
                p.wait("se", sTE, marks[c - 1])
            act(F32C[:], GB[:, c * T:(c + 1) * T], AF.Square)
            emit_stats_mms(GB[:, c * T:(c + 1) * T], c, mark=False, ones=ONESB)
            p.wait("te", sSE, sSE.val)
            marks[c] = emit_stats_sq_mms(F32C, c, mark=True)
        emit_ln_stats_finish()
        p.wait("ve", sGP, sGP.val)
        for c in range(NC8):
            tt(F32C[:], GB[:, c * T:(c + 1) * T], F32A[:], ALU.mult)
            tt(F32C[:], F32C[:], F32B[:], ALU.add)
            ts(F32C[:], F32C[:], LNW1[:, c:c + 1], LNB1[:, c:c + 1],
               ALU.mult, ALU.add)
            p.wait("se", sVE, sVE.val)
            act(GB[:, c * T:(c + 1) * T], F32C[:], AF.Silu)
            p.wait("ve", sSE, sSE.val)

    def emit_wo1_ln(seq):
        YB = BIG[:, 0:4096]
        for qb in range(NQB):
            p.wait("sy", sTE, sTE.val)
            ymark = 0
            for c in range(NC8):
                p.dma("sy", YB[0:64, c * QB:(c + 1) * QB],
                      yscr[2 * c, :, qb * QB:(qb + 1) * QB], sLS)
                ymark = p.dma("sy", YB[64:128, c * QB:(c + 1) * QB],
                              yscr[2 * c + 1, :, qb * QB:(qb + 1) * QB], sLS)
            wmarks = {}
            for ob in range(NC8):
                par = ob % 2
                if ob == 0 and qb == 0:
                    p.wait("gp", sTE, sTE.val)
                    wmarks[0] = p.dma("gp", WB[:, 0:1024], wo1[0], sWB0)
                    p.wait("gp", sTE, sTE.val)
                    wmarks[1] = p.dma("gp", WB[:, 1024:2048], wo1[1], sWB1)
                elif ob == 0:
                    p.wait("gp", sTE, sTE.val)
                    wmarks[0] = p.dma("gp", WB[:, 0:1024], wo1[0], sWB0)
                    wmarks[1] = p.dma("gp", WB[:, 1024:2048], wo1[1], sWB1)
                p.wait("te", sWB0 if par == 0 else sWB1, wmarks[ob])
                p.wait("te", sLS, ymark)
                p.wait("te", sSE, sSE.val)
                for c in range(NC8):
                    mm(PV[:], WB[:, par * 1024 + c * 128:par * 1024 + (c + 1) * 128],
                       YB[:, c * QB:(c + 1) * QB],
                       start=(c == 0), stop=(c == NC8 - 1),
                       sem=sTE if c == NC8 - 1 else None)
                if ob + 2 < NC8:
                    p.wait("gp", sTE, sTE.val)
                    wmarks[ob + 2] = p.dma(
                        "gp", WB[:, par * 1024:par * 1024 + 1024], wo1[ob + 2],
                        sWB0 if par == 0 else sWB1)
                p.wait("se", sTE, sTE.val)
                p.wait("se", sVE, sVE.val)
                act(F32A[:, 0:QB], PV[:], AF.Identity)
                p.wait("ve", sSE, sSE.val)
                p.wait("ve", sTE, sTE.val)
                tt(F32A[:, 0:QB], F32A[:, 0:QB],
                   XN[:, ob * T + qb * QB: ob * T + qb * QB + QB], ALU.add)
                p.wait("se", sVE, sVE.val)
                act(F32B[:, 0:QB], F32A[:, 0:QB], AF.Square)
                p.wait("te", sVE, sVE.val)
                p.wait("te", sSE, sSE.val)
                emit_stats_mms(F32A[:, 0:QB], ob, qb)
                emit_stats_sq_mms(F32B[:, 0:QB], ob, qb, mark=True)
                cp("ve", XN[:, ob * T + qb * QB: ob * T + qb * QB + QB],
                   F32A[:, 0:QB])
        attn_state["wb_free"][0] = attn_state["wb_free"][1] = sTE.val
        emit_ln_stats_finish()
        p.wait("ve", sGP, sGP.val)
        for c in range(NC8):
            tt(F32C[:], XN[:, c * T:(c + 1) * T], F32A[:], ALU.mult)
            tt(F32C[:], F32C[:], F32B[:], ALU.add)
            ts(BIG[:, c * T:(c + 1) * T], F32C[:], G1V[:, c:c + 1], B1V[:, c:c + 1],
               ALU.mult, ALU.add)

    def emit_final_update(seq):
        p.wait("sy", sXS, sXS.val)
        p.wait("sy", sSE, sSE.val)
        p.wait("sy", sVE, sVE.val)
        p.wait("sy", sTE, sTE.val)
        for c in range(NC8):
            if c > 0:
                p.wait("sy", sVE, sVE.val)
            t0 = p.dma("sy", STG[:], x1scr[seq, c], sST1)
            p.wait("ve", sST1, t0)
            tt(F32C[:], BIG[:, c * T:(c + 1) * T], GB[:, c * T:(c + 1) * T],
               ALU.mult)
            tt(F32C[:], F32C[:], STG[:], ALU.add)
            p.wait("sy", sVE, sVE.val)
            p.dma("sy", out_hbm[seq, c], F32C[:], sSTS)
            p.wait("ve", sSTS, sSTS.val)

    for seq in range(NSEQ):
        if seq == 0:
            p.wait("te", sLS, sLS.val)
            p.wait("ve", sLS, sLS.val)
            p.wait("se", sLS, sLS.val)
        emit_x_stats(seq, x_in)
        tc = p.dma("sy", COS[:], cos0, sLS)
        tsn = p.dma("sy", SIN[:], sin0, sLS)
        emit_ln_stats_finish()
        emit_xn_apply(seq, x_in)
        if "xn" in dbg and seq == 0:
            dbg_dump("xn", XN[:])
        emit_ffn0(seq)
        if "g" in dbg and seq == 0:
            dbg_dump("g", GB[:])
        p.wait("ve", sLS, tsn)
        for hp in range(8):
            emit_attn_hp(seq, 0, hp)
        if "y" in dbg and seq == 0:
            dbg_dump("y", BIG[:])
        emit_wo_update0(seq)
        emit_ln_stats_finish()
        p.wait("sy", sXS, sXS.val)
        emit_xn_apply(seq, x1scr)
        tc = p.dma("sy", COS[:], cos1, sLS)
        tsn = p.dma("sy", SIN[:], sin1, sLS)
        p.wait("ve", sLS, tsn)
        emit_pkm_hproj(seq, dbg_on=("hpkm" in dbg and seq == 0))
        emit_pkm_ugate(seq)
        p.wait("sy", sLG, keys_cast_mark)
        for hp in range(8):
            if hp >= 1:
                emit_pkm_transposes(seq, 2 * (hp - 1))
                emit_pkm_transposes(seq, 2 * (hp - 1) + 1)
            emit_pkm_tb(seq, 2 * hp, dbg_on=(seq == 0))
            emit_pkm_tb(seq, 2 * hp + 1, dbg_on=(seq == 0))
            emit_attn_hp(seq, 1, hp)
        pkm_flush()
        emit_pkm_transposes(seq, 14)
        emit_pkm_transposes(seq, 15)
        emit_pkm_wout(seq)
        emit_pkm_lnsilu(seq)
        emit_wo1_ln(seq)
        if "s1" in dbg and seq == 0:
            dbg_dump("s1", BIG[:])
        emit_final_update(seq)

    p.wait("gp", sSTS, sSTS.val)
    p.wait("gp", sIG, sIG.val)
    p.wait("gp", sIG2, sIG2.val)

    for s in p.sems.values():
        s.handle = ctx.enter_context(nc.semaphore(s.name))
    p.run(nc)
    ctx.close()
    return nc, dbg



def tile_proj(W, nob=None):
    outdim, indim = W.shape
    nob = outdim // 128 if nob is None else nob
    nic = indim // 128
    wt = W.T.reshape(nic, 128, outdim)
    out = np.empty((nob, 128, nic * 128), W.dtype)
    for ob in range(nob):
        sl = wt[:, :, ob * 128:(ob + 1) * 128]
        out[ob] = sl.transpose(1, 0, 2).reshape(128, nic * 128)
    return np.ascontiguousarray(out)


def rope_tables_l0():
    HD = 64
    inv = 1.0 / 10000.0 ** (np.arange(0, HD, 2, dtype=np.float32) / HD)
    t = np.arange(T, dtype=np.float32)
    emb = np.concatenate([np.outer(t, inv)] * 2, axis=-1)
    cos = np.cos(emb); sin = np.sin(emb)
    cos_t = np.zeros((128, T), np.float32)
    sin_t = np.zeros((128, T), np.float32)
    for r in range(128):
        d = r % 64
        cos_t[r] = cos[:, d]
        sin_t[r] = sin[:, d] * (-1.0 if d % 2 == 0 else 1.0)
    return cos_t, sin_t


def rope_tables_l1():
    HD = 64
    i = np.arange(1, HD // 2 + 1, dtype=np.float32)
    thetas = np.repeat(1000.0 ** (-2.0 * i / HD), 2)
    pos = np.arange(1, T + 1, dtype=np.float32)
    args = pos[:, None] * thetas[None, :]
    sin = np.sin(args); cos = np.cos(args)
    cos_t = np.zeros((128, T), np.float32)
    sin_t = np.zeros((128, T), np.float32)
    for r in range(128):
        d = r % 64
        cos_t[r] = cos[:, d]
        sin_t[r] = sin[:, d] * (-1.0 if d % 2 == 0 else 1.0)
    return cos_t, sin_t


def pack_inputs(core, inp):
    d = {}
    x = inp["L0_attn_x"]
    xs = x[2 * core:2 * core + 2]
    d["x_in"] = np.ascontiguousarray(xs.transpose(0, 2, 1)).reshape(NSEQ, NC8, 128, T)

    d["wq0"] = tile_proj(inp["L0_attn_Wq"])
    d["wk0"] = tile_proj(inp["L0_attn_Wk"])
    d["wv0"] = tile_proj(inp["L0_attn_Wv"])
    d["wo0"] = tile_proj(inp["L0_attn_Wo"])
    d["bq0"] = np.ascontiguousarray(inp["L0_attn_bq"].reshape(8, 128).T)
    d["bk0"] = np.ascontiguousarray(inp["L0_attn_bk"].reshape(8, 128).T)
    d["bv0"] = np.ascontiguousarray(
        np.broadcast_to(inp["L0_attn_bv"][None, :], (128, D)))
    vecs = np.zeros((128, 64), np.float32)
    vecs[:, 0:8] = inp["L0_attn_bq"].reshape(8, 128).T
    vecs[:, 8:16] = inp["L0_attn_bk"].reshape(8, 128).T
    vecs[:, 16:24] = inp["L0_attn_bo"].reshape(8, 128).T
    vecs[:, 24:32] = inp["L1_attn_ln_gamma"].reshape(8, 128).T
    vecs[:, 32:40] = inp["L1_attn_ln_beta"].reshape(8, 128).T
    vecs[:, 40:48] = inp["L1_ffn_ln_w"].reshape(8, 128).T
    vecs[:, 48:56] = inp["L1_ffn_ln_b"].reshape(8, 128).T
    d["vecs_in"] = vecs
    d["bo0"] = vecs[:, 16:24].copy()
    import ml_dtypes
    c0, s0 = rope_tables_l0()
    d["cos0"], d["sin0"] = c0.astype(ml_dtypes.bfloat16), s0.astype(ml_dtypes.bfloat16)
    c1, s1 = rope_tables_l1()
    d["cos1"], d["sin1"] = c1.astype(ml_dtypes.bfloat16), s1.astype(ml_dtypes.bfloat16)

    d["wg0"] = tile_proj(inp["L0_ffn_w_gate"])
    d["wu0"] = tile_proj(inp["L0_ffn_w_up"])
    d["wd0"] = tile_proj(inp["L0_ffn_w_down"])

    w_in = inp["L1_attn_w_in"]
    d["wq1"] = tile_proj(w_in[0:D])
    d["wk1"] = tile_proj(w_in[D:2 * D])
    d["wv1"] = tile_proj(w_in[2 * D:3 * D])
    d["wo1"] = tile_proj(inp["L1_attn_w_out"])

    d["wi1"] = tile_proj(inp["L1_ffn_W_in"])
    d["keysa"] = np.ascontiguousarray(inp["L1_ffn_keys_a"].transpose(0, 2, 1))
    d["keysb"] = np.ascontiguousarray(inp["L1_ffn_keys_b"].transpose(0, 2, 1))
    d["ush"] = np.ascontiguousarray(inp["L1_ffn_u_shared"].reshape(128, 1))
    d["expv"] = inp["L1_ffn_expert_v"]
    d["wout1"] = tile_proj(inp["L1_ffn_W_out"])
    d["ones_c"] = np.ones((128, 32), np.float32)
    d["iota16"] = np.broadcast_to(np.arange(16, dtype=np.float32)[None, :],
                                  (128, 16)).copy()
    d["ident_in"] = np.eye(128, dtype=np.float32)
    return d


_CACHED = {}


def kernel(**inputs):
    if "nc" not in _CACHED:
        nc, _ = build_program()
        nc.finalize()
        _CACHED["nc"] = nc
    nc = _CACHED["nc"]
    in_maps = [pack_inputs(c, inputs) for c in range(8)]
    res = run_bass_kernel_spmd(nc, in_maps, core_ids=list(range(8)))
    out = np.empty((16, T, D), np.float32)
    for c in range(8):
        o = res.results[c]["out"]
        o = o.reshape(NSEQ, D, T).transpose(0, 2, 1)
        out[2 * c:2 * c + 2] = o
    return out
